# version 15 (final)
# baseline (speedup 1.0000x reference)
import contextlib
import os
import numpy as np

DBG_NODEFER = os.environ.get("K_NODEFER", "") == "1"
DBG_ALLHEAVY = os.environ.get("K_ALLHEAVY", "") == "1"
import concourse.bass as bass
import concourse.mybir as mybir
from concourse.bass_utils import run_bass_kernel_spmd

F32 = mybir.dt.float32
BF16 = mybir.dt.bfloat16
AF = mybir.ActivationFunctionType
ALU = mybir.AluOpType
AX = mybir.AxisListType

D = 1024
T = 4096
DC = D // 128
NCORES = 8
FFN = 2816
FC = FFN // 128
VELT_C = 4194305.0
ENGINES = ("sync", "vector", "scalar", "gpsimd", "tensor")
DMA_QUEUES = ("sync", "vector", "scalar", "gpsimd")


DMA_LANES = 4


class H:
    __slots__ = ("eng", "kind", "marked", "mark_count", "dma_count", "lane")

    def __init__(self, eng, kind):
        self.eng = eng
        self.kind = kind
        self.marked = False
        self.mark_count = None
        self.dma_count = None
        self.lane = None


class KB:
    def __init__(self, nc, stack):
        self.nc = nc
        self.esem = {e: stack.enter_context(nc.semaphore(f"es_{e}")) for e in ENGINES}
        self.dsem = {e: [stack.enter_context(nc.semaphore(f"ds_{e}{j}"))
                         for j in range(DMA_LANES)] for e in DMA_QUEUES}
        self.ops = {e: [] for e in ENGINES}
        self.ndma = {e: 0 for e in DMA_QUEUES}
        self.last_op = {e: None for e in ENGINES}
        self.last_dma = {e: None for e in DMA_QUEUES}

    def op(self, eng, fn, after=()):
        h = H(eng, "op")
        self.ops[eng].append((fn, h, tuple(after)))
        self.last_op[eng] = h
        return h

    def dma(self, eng, fn, after=()):
        h = H(eng, "dma")
        k = self.ndma[eng]
        self.ndma[eng] += 1
        h.lane = k % DMA_LANES
        h.dma_count = (k // DMA_LANES + 1) * 16
        self.ops[eng].append((fn, h, tuple(after)))
        self.last_op[eng] = h
        self.last_dma[eng] = h
        return h

    def frontier(self):
        out = []
        for e in ENGINES:
            if self.last_op[e] is not None:
                out.append(self.last_op[e])
        for q in DMA_QUEUES:
            if self.last_dma[q] is not None:
                out.append(self.last_dma[q])
        return out

    def finalize(self, block):
        for e in ENGINES:
            for (fn, h, deps) in self.ops[e]:
                for d in deps:
                    if d is None:
                        continue
                    if d.kind == "dma":
                        d.marked = True
                    elif not (e == "tensor" and d.eng == "tensor"):
                        d.marked = True
        cnt = {e: 0 for e in ENGINES}
        for e in ENGINES:
            for (fn, h, deps) in self.ops[e]:
                if h.kind == "op" and h.marked:
                    cnt[e] += 1
                    h.mark_count = cnt[e]

        for e in ENGINES:
            oplist = self.ops[e]
            if not oplist and e != "sync":
                continue
            esem, dsem = self.esem, self.dsem

            def body(eng, e=e, oplist=oplist):
                waited = {}
                for (fn, h, deps) in oplist:
                    for d in deps:
                        if d is None:
                            continue
                        if d.kind == "dma":
                            key = ("d", d.eng, d.lane)
                            tgt, sem = d.dma_count, dsem[d.eng][d.lane]
                        elif e == "tensor" and d.eng == "tensor":
                            continue
                        else:
                            key, tgt, sem = ("e", d.eng), d.mark_count, esem[d.eng]
                        if waited.get(key, 0) < tgt:
                            eng.wait_ge(sem, tgt)
                            waited[key] = tgt
                    if h.kind == "dma" and h.dma_count > 16:
                        key = ("d", e, h.lane)
                        tgt0 = h.dma_count - 16
                        if waited.get(key, 0) < tgt0:
                            eng.wait_ge(dsem[e][h.lane], tgt0)
                            waited[key] = tgt0
                    instr = fn(eng)
                    if h.kind == "dma":
                        instr.then_inc(dsem[e][h.lane], 16)
                    elif h.marked:
                        instr.then_inc(esem[e], 1)
                if e == "sync":
                    for q in DMA_QUEUES:
                        for j in range(DMA_LANES):
                            cnt = (self.ndma[q] - j + DMA_LANES - 1) // DMA_LANES
                            if cnt > 0:
                                eng.wait_ge(dsem[q][j], cnt * 16)
            getattr(block, e)(body)


class Buf:
    __slots__ = ("ap", "w", "r")

    def __init__(self, ap):
        self.ap = ap
        self.w = None
        self.r = []


class Ctx:
    def __init__(self, kb):
        self.kb = kb
        self.bdeps = []
        self.cbuf = None
        self.cmap = {}

    def const_col(self, val):
        if val not in self.cmap:
            i = len(self.cmap)
            col = Buf(self.cbuf.ap[:, i:i + 1])
            self.op("vector", lambda v, col=col, val=val: v.memset(col.ap, float(val)),
                    writes=[col])
            self.cmap[val] = col
        return self.cmap[val]

    def barrier(self):
        self.bdeps = self.kb.frontier()

    def _deps(self, reads, writes, after):
        deps = list(self.bdeps)
        for b in reads:
            if b.w is not None:
                deps.append(b.w)
        for b in writes:
            deps.extend(b.r)
            if b.w is not None:
                deps.append(b.w)
        deps.extend(after)
        return deps

    def _upd(self, h, reads, writes):
        for b in reads:
            b.r.append(h)
        for b in writes:
            b.w = h
            b.r = []
        return h

    def op(self, eng, fn, reads=(), writes=(), after=()):
        h = self.kb.op(eng, fn, self._deps(reads, writes, after))
        return self._upd(h, reads, writes)

    def dma(self, eng, out_ap, in_ap, reads=(), writes=(), after=(), **kw):
        def fn(e):
            return e.dma_start(out=out_ap, in_=in_ap, **kw)
        h = self.kb.dma(eng, fn, self._deps(reads, writes, after))
        return self._upd(h, reads, writes)


_SB_N = [0]


def sbuf(stack, nc, name, shape, dt=F32):
    _SB_N[0] += 1
    return Buf(stack.enter_context(nc.sbuf_tensor(f"{name}_{_SB_N[0]}", shape, dt))[:])


class PsumPool:

    def __init__(self, stack, nc):
        self.t = [stack.enter_context(nc.psum_tensor(f"ps{i}", [128, 2048], F32))
                  for i in range(2)]
        self.cur = {}
        for i in range(2):
            for q in range(4):
                self.cur[(i, q)] = Buf(self.t[i][:, q * 512:(q + 1) * 512])

    def take(self, i, qi, dt=F32, cols=512):
        old = self.cur[(i, qi)]
        base = self.t[i][:, qi * 512:(qi + 1) * 512]
        ap = base.bitcast(dt)[:, 0:cols] if dt != F32 else base[:, 0:cols]
        nb = Buf(ap)
        nb.w = old.w
        nb.r = list(old.r)
        self.cur[(i, qi)] = nb
        return nb

    def group(self, i):
        qs = [self.take(i, q) for q in range(4)]
        return self.t[i][:, :], qs

    def span(self, i, q0, nq, dt=F32):
        base = self.t[i][:, q0 * 512:(q0 + nq) * 512]
        ap = base.bitcast(dt) if dt != F32 else base
        nb = Buf(ap)
        deps_w = []
        deps_r = []
        for q in range(q0, q0 + nq):
            old = self.cur[(i, q)]
            if old.w is not None:
                deps_w.append(old.w)
            deps_r.extend(old.r)
            self.cur[(i, q)] = nb
        nb.w = deps_w[-1] if deps_w else None
        nb.r = deps_r + deps_w[:-1]
        return nb


def quant_scratch(stack, nc, W, tag):
    B = {nm: sbuf(stack, nc, f"q{nm}_{tag}", [128, W]) for nm in "abcdefg"}
    for nm in ("bm", "sc", "rs"):
        B[nm] = sbuf(stack, nc, f"q{nm}_{tag}", [128, W // 16])
    B["W"] = W
    return B


def emit_quant(cx, B, x, out, xoff=0, ooff=0, W=None):
    W = W or B["W"]
    Wb = W // 16
    x3 = x.ap[:, xoff:xoff + W].rearrange("p (b k) -> p b k", k=16)
    A, Bz, C, Dd, E, F, G = (B[n] for n in "abcdefg")

    def s3(b):
        return b.ap[:, 0:W].rearrange("p (b k) -> p b k", k=16)
    cx.op("vector", lambda v: v.reduce_max(B["bm"].ap[:, 0:Wb], x3, axis=AX.X,
                                           apply_absolute_value=True),
          reads=[x], writes=[B["bm"]])
    cx.op("vector", lambda v: v.tensor_scalar(B["sc"].ap[:, 0:Wb], B["bm"].ap[:, 0:Wb],
                                              1.0 / 6.0, 1e-12, ALU.mult, ALU.max),
          reads=[B["bm"]], writes=[B["sc"]])
    cx.op("vector", lambda v: v.reciprocal(B["rs"].ap[:, 0:Wb], B["sc"].ap[:, 0:Wb]),
          reads=[B["sc"]], writes=[B["rs"]])
    rs3 = B["rs"].ap[:, 0:Wb].unsqueeze(2).broadcast_to([128, Wb, 16])
    sc3 = B["sc"].ap[:, 0:Wb].unsqueeze(2).broadcast_to([128, Wb, 16])
    cx.op("vector", lambda v: v.tensor_tensor(s3(A), x3, rs3, ALU.mult),
          reads=[x, B["rs"]], writes=[A])
    cx.op("scalar", lambda s: s.activation(out=Bz.ap[:, 0:W], in_=A.ap[:, 0:W], func=AF.Abs),
          reads=[A], writes=[Bz])
    cx.op("scalar", lambda s: s.sign(out=A.ap[:, 0:W], in_=x.ap[:, xoff:xoff + W]),
          reads=[x], writes=[A])
    cx.op("vector", lambda v: v.tensor_tensor(s3(G), s3(A), sc3, ALU.mult),
          reads=[A, B["sc"]], writes=[G])
    c6 = cx.const_col(6.0)
    c5 = cx.const_col(5.0)
    cv = cx.const_col(VELT_C)
    cx.op("scalar", lambda s: s.activation(out=C.ap[:, 0:W], in_=Bz.ap[:, 0:W],
                                           func=AF.Relu, scale=-1.0, bias=c6.ap),
          reads=[Bz, c6], writes=[C])
    cx.op("scalar", lambda s: s.activation(out=Dd.ap[:, 0:W], in_=C.ap[:, 0:W],
                                           func=AF.Relu, scale=-1.0, bias=c5.ap),
          reads=[C, c5], writes=[Dd])
    cx.op("scalar", lambda s: s.activation(out=C.ap[:, 0:W], in_=Dd.ap[:, 0:W],
                                           func=AF.Relu, scale=VELT_C, bias=cv.ap),
          reads=[Dd, cv], writes=[C])
    cx.op("vector", lambda v: v.tensor_scalar(E.ap[:, 0:W], Bz.ap[:, 0:W],
                                              0.25, 0.5, ALU.is_gt, ALU.mult),
          reads=[Bz], writes=[E])
    cx.op("vector", lambda v: v.tensor_single_scalar(F.ap[:, 0:W], Bz.ap[:, 0:W],
                                                     0.75, ALU.is_gt),
          reads=[Bz], writes=[F])
    cx.op("vector", lambda v: v.scalar_tensor_tensor(Bz.ap[:, 0:W], Dd.ap[:, 0:W], 1.0,
                                                     C.ap[:, 0:W], ALU.add, ALU.subtract),
          reads=[Dd, C], writes=[Bz])
    cx.op("gpsimd", lambda g: g.tensor_add(Dd.ap[:, 0:W], C.ap[:, 0:W], Bz.ap[:, 0:W]),
          reads=[C, Bz], writes=[Dd])
    cx.op("gpsimd", lambda g: g.tensor_mul(C.ap[:, 0:W], Dd.ap[:, 0:W], F.ap[:, 0:W]),
          reads=[Dd, F], writes=[C])
    cx.op("vector", lambda v: v.tensor_max(Dd.ap[:, 0:W], C.ap[:, 0:W], E.ap[:, 0:W]),
          reads=[C, E], writes=[Dd])
    return cx.op("vector", lambda v: v.tensor_tensor(out.ap[:, ooff:ooff + W],
                                                     Dd.ap[:, 0:W], G.ap[:, 0:W], ALU.mult),
                 reads=[Dd, G], writes=[out])


def np_fp4(x):
    GRID = np.array([0.0, 0.5, 1.0, 1.5, 2.0, 3.0, 4.0, 6.0], np.float32)
    MID = np.array([0.25, 0.75, 1.25, 1.75, 2.5, 3.5, 5.0], np.float32)
    shape = x.shape
    xb = x.reshape(shape[:-1] + (shape[-1] // 16, 16))
    scale = np.maximum(np.abs(xb).max(-1, keepdims=True) / 6.0,
                       np.float32(1e-12)).astype(np.float32)
    xn = (xb / scale).astype(np.float32)
    idx = np.searchsorted(MID, np.abs(xn))
    q = GRID[idx] * np.sign(xn) * scale
    return q.reshape(shape).astype(np.float32)


def make_identity(cx, ident):
    cx.op("gpsimd", lambda g: g.memset(ident.ap, 0.0), writes=[ident])
    cx.op("gpsimd", lambda g: g.affine_select(out=ident.ap, in_=ident.ap,
                                              compare_op=ALU.not_equal, fill=1.0, base=0,
                                              pattern=[[-1, ident.ap.shape[1]]],
                                              channel_multiplier=1),
          reads=[ident], writes=[ident])


def load_cols(cx, dst, dst_col, src_ap, off, n=128, dq="sync"):
    return cx.dma(dq, dst.ap[0:n, dst_col], src_ap[off:off + n], writes=[dst])


def emit_l0_weightquant(stack, nc, cx, t, ident_bf):
    with contextlib.ExitStack() as ph:
        Bs = [quant_scratch(ph, nc, 1408, f"wq{i}") for i in range(2)]
        wrow = [sbuf(ph, nc, f"wqrow{i}", [128, FFN]) for i in range(2)]
        qouts = [sbuf(ph, nc, f"wqout{i}", [128, FFN], BF16) for i in range(2)]
        plane = sbuf(ph, nc, "wqplane", [128, 22528], BF16)
        jobs = [(t["w_gate"], FFN // 128, D, t["wqgt"], DC),
                (t["w_up"], FFN // 128, D, t["wqut"], DC),
                (t["w_down"], D // 128, FFN, t["wqdt"], FC)]
        tp = 0
        for (src, nr, W, dst, nfi) in jobs:
            for r in range(nr):
                rb = wrow[r % 2]
                qout = qouts[r % 2]
                cx.dma("sync", rb.ap[:, 0:W], src[r * 128:(r + 1) * 128, :], writes=[rb])
                if W <= 1408:
                    emit_quant(cx, Bs[r % 2], rb, qout, W=W)
                else:
                    emit_quant(cx, Bs[0], rb, qout, xoff=0, ooff=0, W=W // 2)
                    emit_quant(cx, Bs[1], rb, qout, xoff=W // 2, ooff=W // 2, W=W // 2)
                for fi in range(nfi):
                    pq = cx.pspool.take(1, tp % 2, BF16, 512)
                    tp += 1
                    cx.op("tensor",
                          lambda te, pq=pq, fi=fi, qout=qout: te.transpose(
                              pq.ap[:, 0:128], qout.ap[:, fi * 128:(fi + 1) * 128],
                              ident_bf.ap),
                          reads=[qout, ident_bf], writes=[pq])
                    cx.op("scalar",
                          lambda s, pq=pq, fi=fi, r=r, nr=nr: s.copy(
                              plane.ap[:, (fi * nr + r) * 128:(fi * nr + r + 1) * 128],
                              pq.ap[:, 0:128]),
                          reads=[pq], writes=[plane])
            for fi in range(nfi):
                cx.dma("sync", dst[fi, :, :],
                       plane.ap[:, fi * nr * 128:(fi + 1) * nr * 128], reads=[plane])


def emit_l0_p1(stack, nc, cx, t, ident_bf, ident_f):
    with contextlib.ExitStack() as ph:
        x_tm = [sbuf(ph, nc, f"p1x{i}", [128, 4096]) for i in range(2)]
        xn32 = [sbuf(ph, nc, f"p1n{i}", [128, 4096]) for i in range(2)]
        xnb = [sbuf(ph, nc, f"p1nb{i}", [128, 4096], BF16) for i in range(2)]
        xqb = [sbuf(ph, nc, f"p1qb{i}", [128, 4096], BF16) for i in range(2)]
        Bs1 = [quant_scratch(ph, nc, 1024, f"p1{i}") for i in range(2)]
        st = [sbuf(ph, nc, f"p1st{i}", [128, 4]) for i in range(2)]
        sq = [sbuf(ph, nc, f"p1sq{i}", [128, 4]) for i in range(2)]
        mvr = [sbuf(ph, nc, f"p1mv{i}", [128, 12]) for i in range(2)]
        junk = sbuf(ph, nc, "p1junk", [128, 1024], BF16)
        stg_b = [sbuf(ph, nc, f"p1sb{i}", [128, 512], BF16) for i in range(2)]
        stg_f = [sbuf(ph, nc, f"p1sf{i}", [128, 512]) for i in range(2)]
        tpb = 0
        tpf = 0
        for i in range(8):
            bi = i % 2
            x, xn, xnb_, xqb_ = x_tm[bi], xn32[bi], xnb[bi], xqb[bi]
            t0 = i * 512
            cx.dma("sync", x.ap.rearrange("p (c d) -> p c d", d=1024),
                   t["x_in"][t0:t0 + 512, :].rearrange("(c p) d -> p c d", p=128),
                   writes=[x])
            for c in range(4):
                cx.op("vector", lambda v, c=c, x=x, b=st[bi]: v.reduce_sum(
                    b.ap[:, c:c + 1], x.ap[:, c * 1024:(c + 1) * 1024], axis=AX.X),
                    reads=[x], writes=[st[bi]])
                cx.op("scalar", lambda s, c=c, x=x, b=sq[bi]: s.activation(
                    out=junk.ap, in_=x.ap[:, c * 1024:(c + 1) * 1024], func=AF.Square,
                    accum_out=b.ap[:, c:c + 1]),
                    reads=[x], writes=[sq[bi], junk])
            m = mvr[bi]
            cx.op("vector", lambda v, b=st[bi], m=m: v.tensor_scalar(
                m.ap[:, 0:4], b.ap[:, 0:4], 1.0 / D, None, ALU.mult),
                reads=[st[bi]], writes=[m])
            cx.op("vector", lambda v, b=sq[bi], m=m: v.tensor_scalar(
                m.ap[:, 4:8], b.ap[:, 0:4], 1.0 / D, None, ALU.mult),
                reads=[sq[bi]], writes=[m])
            cx.op("vector", lambda v, m=m: v.tensor_tensor(
                m.ap[:, 8:12], m.ap[:, 0:4], m.ap[:, 0:4], ALU.mult),
                reads=[m], writes=[m])
            cx.op("vector", lambda v, m=m: v.tensor_tensor(
                m.ap[:, 4:8], m.ap[:, 4:8], m.ap[:, 8:12], ALU.subtract),
                reads=[m], writes=[m])
            ce = cx.const_col(1e-5)
            cx.op("scalar", lambda s, m=m, ce=ce: s.activation(
                out=m.ap[:, 8:12], in_=m.ap[:, 4:8], func=AF.Ln, bias=ce.ap),
                reads=[m, ce], writes=[m])
            cx.op("scalar", lambda s, m=m: s.activation(
                out=m.ap[:, 8:12], in_=m.ap[:, 8:12], func=AF.Exp, scale=-0.5),
                reads=[m], writes=[m])
            if "dbg_st" in t:
                cx.dma("sync", t["dbg_st"][i, :, 0:4], st[bi].ap, reads=[st[bi]])
                cx.dma("sync", t["dbg_st"][i, :, 4:8], sq[bi].ap, reads=[sq[bi]])
                cx.dma("sync", t["dbg_st"][i, :, 8:20], m.ap, reads=[m])
            for c in range(4):
                cx.op("vector", lambda v, c=c, x=x, xn=xn, m=m: v.tensor_scalar(
                    xn.ap[:, c * 1024:(c + 1) * 1024], x.ap[:, c * 1024:(c + 1) * 1024],
                    m.ap[:, c:c + 1], m.ap[:, 8 + c:9 + c], ALU.subtract, ALU.mult),
                    reads=[x, m], writes=[xn])
            cx.op("vector", lambda v, xn=xn, xnb_=xnb_: v.tensor_copy(xnb_.ap, xn.ap),
                  reads=[xn], writes=[xnb_])
            for qq in range(4):
                emit_quant(cx, Bs1[qq % 2], xn, xqb_, xoff=qq * 1024,
                           ooff=qq * 1024, W=1024)
            for (srcb, dst, isf) in ((x, t["x_fm"], True), (xnb_, t["xn_fm"], False),
                                     (xqb_, t["xq_fm"], False)):
                for fi in range(8):
                    if isf:
                        pq = cx.pspool.take(0, tpf % 2, F32, 512)
                        stg = stg_f[tpf % 2]
                        tpf += 1
                        idn = ident_f
                    else:
                        pq = cx.pspool.take(1, tpb % 2, BF16, 512)
                        stg = stg_b[tpb % 2]
                        tpb += 1
                        idn = ident_bf
                    for c in range(4):
                        cx.op("tensor", lambda te, pq=pq, c=c, fi=fi, srcb=srcb, idn=idn:
                              te.transpose(pq.ap[:, c * 128:(c + 1) * 128],
                                           srcb.ap[:, c * 1024 + fi * 128:
                                                   c * 1024 + (fi + 1) * 128],
                                           idn.ap),
                              reads=[srcb, idn], writes=[pq])
                    cx.op("scalar", lambda s, pq=pq, stg=stg: s.copy(stg.ap, pq.ap),
                          reads=[pq], writes=[stg])
                    cx.dma("sync", dst[fi, :, t0:t0 + 512], stg.ap, reads=[stg])


def emit_l0_p2(stack, nc, cx, t):
    with contextlib.ExitStack() as ph:
        xna = sbuf(ph, nc, "p2xn", [128, DC * 4096], BF16)
        xnb = [Buf(xna.ap[:, c * 4096:(c + 1) * 4096]) for c in range(DC)]
        w_in = sbuf(ph, nc, "p2win", [128, 16 * 1024], BF16)
        wo_st = [sbuf(ph, nc, f"p2wo{i}", [128, 1024], BF16) for i in range(2)]
        dcol = sbuf(ph, nc, "p2d", [128, 8])
        ccol = sbuf(ph, nc, "p2c", [128, 8])
        acol = sbuf(ph, nc, "p2a", [128, 8])
        bi_c = sbuf(ph, nc, "p2bi", [128, 16])
        bo_c = sbuf(ph, nc, "p2bo", [128, 8])
        hst = sbuf(ph, nc, "p2h", [128, 8])
        hstv = [Buf(hst.ap[:, c:c + 1]) for c in range(8)]
        v_sb = sbuf(ph, nc, "p2v", [128, 2048])
        y_sb = sbuf(ph, nc, "p2y", [128, 2048])
        g_sb = [sbuf(ph, nc, f"p2g{i}", [128, 2048]) for i in range(2)]
        sm = [sbuf(ph, nc, f"p2sm{c}", [128, 2048], BF16) for c in range(DC)]
        s_stg = [sbuf(ph, nc, f"p2ss{i}", [128, 2048]) for i in range(2)]

        xn3 = xna.ap.rearrange("p (c t) -> p c t", t=4096)
        cx.dma("sync", xn3[:, 0:4, :],
               t["xn_fm"][0:4].rearrange("c p t -> p c t"), writes=xnb[0:4] + [xna])
        cx.dma("scalar", xn3[:, 4:8, :],
               t["xn_fm"][4:8].rearrange("c p t -> p c t"), writes=xnb[4:8])
        w_in_v = [Buf(w_in.ap[:, f * 1024:(f + 1) * 1024]) for f in range(16)]
        for f in range(16):
            cx.dma("gpsimd",
                   w_in.ap[:, f * 1024:(f + 1) * 1024].rearrange("p (k f2) -> p k f2", f2=128),
                   t["w_in"][:, f * 128:(f + 1) * 128].rearrange("(k p) f -> p k f", p=128),
                   writes=[w_in_v[f]])
        for c in range(8):
            load_cols(cx, acol, c, t["a_in"], c * 128)
            load_cols(cx, ccol, c, t["c_in"], c * 128)
            load_cols(cx, bo_c, c, t["b_out"], c * 128)
        for c in range(16):
            load_cols(cx, bi_c, c, t["b_in"], c * 128)
        cx.op("scalar", lambda s: s.activation(out=dcol.ap, in_=acol.ap, func=AF.Sigmoid),
              reads=[acol], writes=[dcol])
        cx.op("vector", lambda v: v.memset(hst.ap, 0.0), writes=hstv)

        for half in range(2):
            h0 = half * 2048
            for c in range(8):
                gb = g_sb[c % 2]
                grp = {}
                for (pi, f) in ((0, c), (1, 8 + c)):
                    psap, qs = cx.pspool.group(pi)
                    grp[pi] = (psap, qs)
                    for k in range(8):
                        for tt in range(4):
                            cx.op("tensor",
                                  lambda te, f=f, k=k, tt=tt, qs=qs, h0=h0:
                                  te.matmul(qs[tt].ap,
                                            w_in.ap[:, f * 1024 + k * 128:
                                                    f * 1024 + (k + 1) * 128],
                                            xnb[k].ap[:, h0 + tt * 512:h0 + (tt + 1) * 512],
                                            start=(k == 0), stop=(k == 7)),
                                  reads=[w_in_v[f], xnb[k]], writes=[qs[tt]])
                ps0ap, ps0q = grp[0]
                ps1ap, ps1q = grp[1]
                cx.op("vector", lambda v, c=c, a=ps0ap: v.tensor_scalar(
                    v_sb.ap, a, bi_c.ap[:, c:c + 1], None, ALU.add),
                    reads=ps0q + [bi_c], writes=[v_sb])
                cx.op("vector", lambda v, c=c: v.tensor_tensor_scan(
                    y_sb.ap, dcol.ap[:, c:c + 1].broadcast_to([128, 2048]), v_sb.ap,
                    hstv[c].ap, ALU.mult, ALU.add),
                    reads=[v_sb, dcol, hstv[c]], writes=[y_sb])
                cx.op("vector", lambda v, c=c: v.tensor_copy(
                    hstv[c].ap, y_sb.ap[:, 2047:2048]),
                    reads=[y_sb], writes=[hstv[c]])
                cx.op("scalar", lambda s, gb=gb, c=c, a=ps1ap: s.activation(
                    out=gb.ap, in_=a, func=AF.Sigmoid, bias=bi_c.ap[:, 8 + c:9 + c]),
                    reads=ps1q + [bi_c], writes=[gb])
                cx.op("vector", lambda v, gb=gb, c=c: v.scalar_tensor_tensor(
                    sm[c].ap, y_sb.ap, ccol.ap[:, c:c + 1], gb.ap, ALU.mult, ALU.mult),
                    reads=[y_sb, gb, ccol], writes=[sm[c]])
                if "dbg_p2" in t and half == 0 and c == 0:
                    cx.dma("sync", t["dbg_p2"][0, :, :], v_sb.ap, reads=[v_sb])
                    cx.dma("sync", t["dbg_p2"][1, :, :], y_sb.ap, reads=[y_sb])
                    cx.dma("sync", t["dbg_p2"][2, :, :], gb.ap, reads=[gb])
                    cx.dma("gpsimd", t["dbg_p2"][3, :, :], sm[c].ap, reads=[sm[c]])
            for f in range(8):
                wo = wo_st[f % 2]
                cx.dma("gpsimd",
                       wo.ap.rearrange("p (k f2) -> p k f2", f2=128),
                       t["w_out"][:, f * 128:(f + 1) * 128].rearrange("(k p) f -> p k f", p=128),
                       writes=[wo])
                psap, qs = cx.pspool.group(f % 2)
                for k in range(8):
                    for tt in range(4):
                        cx.op("tensor",
                              lambda te, wo=wo, k=k, tt=tt, qs=qs:
                              te.matmul(qs[tt].ap, wo.ap[:, k * 128:(k + 1) * 128],
                                        sm[k].ap[:, tt * 512:(tt + 1) * 512],
                                        start=(k == 0), stop=(k == 7)),
                              reads=[wo, sm[k]], writes=[qs[tt]])
                stg = s_stg[f % 2]
                cx.op("vector", lambda v, stg=stg, f=f, a=psap, qs=qs: v.tensor_scalar(
                    stg.ap, a, bo_c.ap[:, f:f + 1], None, ALU.add),
                    reads=qs + [bo_c], writes=[stg])
                cx.dma("sync", t["s_fm"][f, :, h0:h0 + 2048], stg.ap, reads=[stg])


def emit_l0_p3(stack, nc, cx, t, ident_bf, ident_f):
    with contextlib.ExitStack() as ph:
        xqa = sbuf(ph, nc, "p3xq", [128, DC * 2048], BF16)
        xqb = [Buf(xqa.ap[:, c * 2048:(c + 1) * 2048]) for c in range(DC)]
        wg_st = [sbuf(ph, nc, f"p3wg{i}", [128, 1024], BF16) for i in range(2)]
        wu_st = [sbuf(ph, nc, f"p3wu{i}", [128, 1024], BF16) for i in range(2)]
        wd_st = [sbuf(ph, nc, f"p3wd{i}", [128, FC * 128], BF16) for i in range(2)]
        sg_sb2 = [sbuf(ph, nc, "p3sg", [128, 2048])] * 2
        h32_2 = [sbuf(ph, nc, "p3h", [128, 2048])] * 2
        htm_2 = [sbuf(ph, nc, f"p3htm{i}", [128, 2048]) for i in range(2)]
        hqtm_2 = [sbuf(ph, nc, f"p3hqtm{i}", [128, 2048], BF16) for i in range(2)]
        hq_ld = [sbuf(ph, nc, f"p3hl{i}", [128, 2048], BF16) for i in range(2)]
        Bs3 = [quant_scratch(ph, nc, 1024, f"p3{i}") for i in range(2)]
        sil = [sbuf(ph, nc, f"p3sil{i}", [128, 2048]) for i in range(2)]
        s_ld = [sbuf(ph, nc, "p3sl", [128, 2048])] * 2
        x_ld = [sbuf(ph, nc, "p3xl", [128, 2048])] * 2
        x1st = [sbuf(ph, nc, "p3x1", [128, 2048])] * 2

        for half in range(2):
            h0 = half * 2048
            cx.dma("sync", xqa.ap.rearrange("p (c t) -> p c t", t=2048),
                   t["xq_fm"][:, :, h0:h0 + 2048].rearrange("c p t -> p c t"),
                   writes=xqb + [xqa])
            for f in range(FC):
                wg, wu = wg_st[f % 2], wu_st[f % 2]
                sg_sb, h32 = sg_sb2[f % 2], h32_2[f % 2]
                htm, hqtm = htm_2[f % 2], hqtm_2[f % 2]
                cx.dma("gpsimd",
                       wg.ap.rearrange("p (k f2) -> p k f2", f2=128),
                       t["wqgt"][:, :, f * 128:(f + 1) * 128].rearrange("k p f -> p k f"),
                       writes=[wg])
                cx.dma("gpsimd",
                       wu.ap.rearrange("p (k f2) -> p k f2", f2=128),
                       t["wqut"][:, :, f * 128:(f + 1) * 128].rearrange("k p f -> p k f"),
                       writes=[wu])
                grp = {}
                for (pi, wb) in ((0, wg), (1, wu)):
                    psap, qs = cx.pspool.group(pi)
                    grp[pi] = (psap, qs)
                    for k in range(8):
                        for tt in range(4):
                            cx.op("tensor",
                                  lambda te, wb=wb, k=k, tt=tt, qs=qs:
                                  te.matmul(qs[tt].ap, wb.ap[:, k * 128:(k + 1) * 128],
                                            xqb[k].ap[:, tt * 512:(tt + 1) * 512],
                                            start=(k == 0), stop=(k == 7)),
                                  reads=[wb, xqb[k]], writes=[qs[tt]])
                ps0ap, ps0q = grp[0]
                ps1ap, ps1q = grp[1]
                cx.op("scalar", lambda s, a=ps0ap, sg_sb=sg_sb: s.activation(
                    out=sg_sb.ap, in_=a, func=AF.Silu),
                    reads=ps0q, writes=[sg_sb])
                cx.op("vector", lambda v, a=ps1ap, sg_sb=sg_sb, h32=h32:
                      v.tensor_tensor(h32.ap, sg_sb.ap, a, ALU.mult),
                      reads=[sg_sb] + ps1q, writes=[h32])
                for q4 in range(4):
                    pq = cx.pspool.take(0, q4 % 2, F32, 512)
                    for j in range(4):
                        cx.op("tensor", lambda te, pq=pq, q4=q4, j=j, h32=h32:
                              te.transpose(
                                  pq.ap[:, j * 128:(j + 1) * 128],
                                  h32.ap[:, (q4 * 4 + j) * 128:(q4 * 4 + j + 1) * 128],
                                  ident_f.ap),
                              reads=[h32, ident_f], writes=[pq])
                    cx.op("scalar", lambda s, pq=pq, q4=q4, htm=htm: s.copy(
                        htm.ap[:, q4 * 512:(q4 + 1) * 512], pq.ap),
                        reads=[pq], writes=[htm])
                emit_quant(cx, Bs3[0], htm, hqtm, 0, 0)
                emit_quant(cx, Bs3[1], htm, hqtm, 1024, 1024)
                hst = hq_ld[f % 2]
                for q4 in range(4):
                    pq = cx.pspool.take(0, 2 + q4 % 2, BF16, 512)
                    for j in range(4):
                        cx.op("tensor", lambda te, pq=pq, q4=q4, j=j, hqtm=hqtm:
                              te.transpose(
                                  pq.ap[:, j * 128:(j + 1) * 128],
                                  hqtm.ap[:, (q4 * 4 + j) * 128:(q4 * 4 + j + 1) * 128],
                                  ident_bf.ap),
                              reads=[hqtm, ident_bf], writes=[pq])
                    cx.op("scalar", lambda s, pq=pq, q4=q4, hst=hst: s.copy(
                        hst.ap[:, q4 * 512:(q4 + 1) * 512], pq.ap),
                        reads=[pq], writes=[hst])
                cx.dma("sync", t["hq_s"][f, :, h0:h0 + 2048], hst.ap,
                       reads=[hst])
            cx.barrier()
            for fop in range(4):
                grp = {}
                for sub in range(2):
                    psap, qs = cx.pspool.group(sub)
                    grp[sub] = (psap, qs)
                wds = []
                for sub in range(2):
                    fo = fop * 2 + sub
                    wd = wd_st[sub]
                    cx.dma("gpsimd",
                           wd.ap.rearrange("p (k f2) -> p k f2", f2=128),
                           t["wqdt"][:, :, fo * 128:(fo + 1) * 128].rearrange("k p f -> p k f"),
                           writes=[wd])
                    wds.append(wd)
                for k in range(FC):
                    hl = hq_ld[k % 2]
                    cx.dma("sync", hl.ap, t["hq_s"][k, :, h0:h0 + 2048], writes=[hl])
                    for sub in range(2):
                        psap, qs = grp[sub]
                        for tt in range(4):
                            cx.op("tensor",
                                  lambda te, wd=wds[sub], k=k, tt=tt, qs=qs, hl=hl:
                                  te.matmul(qs[tt].ap, wd.ap[:, k * 128:(k + 1) * 128],
                                            hl.ap[:, tt * 512:(tt + 1) * 512],
                                            start=(k == 0), stop=(k == FC - 1)),
                                  reads=[wds[sub], hl], writes=[qs[tt]])
                for sub in range(2):
                    fo = fop * 2 + sub
                    psap, qs = grp[sub]
                    sl, xl, x1, si = s_ld[sub], x_ld[sub], x1st[sub], sil[sub]
                    cx.dma("sync", sl.ap, t["s_fm"][fo, :, h0:h0 + 2048], writes=[sl])
                    cx.dma("sync", xl.ap, t["x_fm"][fo, :, h0:h0 + 2048], writes=[xl])
                    cx.op("scalar", lambda s, a=psap, si=si: s.activation(
                        out=si.ap, in_=a, func=AF.Silu),
                        reads=qs, writes=[si])
                    cx.op("vector", lambda v, sl=sl, si=si, x1=x1: v.tensor_tensor(
                        x1.ap, si.ap, sl.ap, ALU.mult),
                        reads=[si, sl], writes=[x1])
                    cx.op("vector", lambda v, xl=xl, x1=x1: v.tensor_tensor(
                        x1.ap, x1.ap, xl.ap, ALU.add),
                        reads=[x1, xl], writes=[x1])
                    cx.dma("sync", t["x1_fm"][fo, :, h0:h0 + 2048], x1.ap, reads=[x1])


def build_l0(debug=False):
    nc = bass.Bass("TRN2", target_bir_lowering=False)
    t = {}
    t["x_in"] = nc.dram_tensor("x", [T, D], F32, kind="ExternalInput").ap()
    t["w_in"] = nc.dram_tensor("W_in", [D, 2 * D], F32, kind="ExternalInput").ap()
    t["b_in"] = nc.dram_tensor("b_in", [2 * D], F32, kind="ExternalInput").ap()
    t["a_in"] = nc.dram_tensor("A", [D], F32, kind="ExternalInput").ap()
    t["c_in"] = nc.dram_tensor("C", [D], F32, kind="ExternalInput").ap()
    t["w_out"] = nc.dram_tensor("W_out", [D, D], F32, kind="ExternalInput").ap()
    t["b_out"] = nc.dram_tensor("b_out", [D], F32, kind="ExternalInput").ap()
    t["w_gate"] = nc.dram_tensor("w_gate", [FFN, D], F32, kind="ExternalInput").ap()
    t["w_up"] = nc.dram_tensor("w_up", [FFN, D], F32, kind="ExternalInput").ap()
    t["w_down"] = nc.dram_tensor("w_down", [D, FFN], F32, kind="ExternalInput").ap()
    t["x1_fm"] = nc.dram_tensor("x1_fm", [DC, 128, T], F32, kind="ExternalOutput").ap()
    sk = "ExternalOutput" if debug else "Internal"
    t["xn_fm"] = nc.dram_tensor("xn_fm", [DC, 128, T], BF16, kind=sk).ap()
    t["xq_fm"] = nc.dram_tensor("xq_fm", [DC, 128, T], BF16, kind=sk).ap()
    t["x_fm"] = nc.dram_tensor("x_fm_s", [DC, 128, T], F32, kind=sk).ap()
    t["s_fm"] = nc.dram_tensor("s_fm_s", [DC, 128, T], F32, kind=sk).ap()
    t["hq_s"] = nc.dram_tensor("hq_s", [FC, 128, T], BF16, kind="Internal").ap()
    t["wqgt"] = nc.dram_tensor("wqgt", [DC, 128, FFN], BF16, kind=sk).ap()
    t["wqut"] = nc.dram_tensor("wqut", [DC, 128, FFN], BF16, kind=sk).ap()
    t["wqdt"] = nc.dram_tensor("wqdt", [FC, 128, D], BF16, kind=sk).ap()
    if debug:
        t["dbg_st"] = nc.dram_tensor("dbg_st", [8, 128, 20], F32, kind="ExternalOutput").ap()
        t["dbg_p2"] = nc.dram_tensor("dbg_p2", [4, 128, 2048], F32, kind="ExternalOutput").ap()

    with contextlib.ExitStack() as stack:
        kb = KB(nc, stack)
        cx = Ctx(kb)
        cx.pspool = PsumPool(stack, nc)
        cx.cbuf = sbuf(stack, nc, "consts", [128, 16])
        ident_bf = sbuf(stack, nc, "ident_bf", [128, 128], BF16)
        ident_f = sbuf(stack, nc, "ident_f", [128, 128], F32)
        make_identity(cx, ident_bf)
        make_identity(cx, ident_f)
        emit_l0_weightquant(stack, nc, cx, t, ident_bf)
        cx.barrier()
        emit_l0_p1(stack, nc, cx, t, ident_bf, ident_f)
        cx.barrier()
        emit_l0_p2(stack, nc, cx, t)
        cx.barrier()
        emit_l0_p3(stack, nc, cx, t, ident_bf, ident_f)
        with nc.Block() as block:
            kb.finalize(block)
    return nc


def _bf(x):
    import ml_dtypes
    return np.asarray(x, dtype=ml_dtypes.bfloat16)


def make_tables():
    tb = {}
    tb["ident_f"] = np.eye(128, dtype=np.float32)
    tb["ident_bf"] = _bf(np.eye(128, dtype=np.float32))
    qpos = np.arange(T, dtype=np.float64)
    qhi = 16.0 * np.rint(qpos / 16.0)
    qlo = qpos - qhi
    tb["qsplit"] = _bf(np.stack([qhi, qhi, qlo, qlo]))
    slopes = (2.0 ** (-8.0 / 16.0)) ** np.arange(1, 17)
    ms = np.zeros((4, 16 * 128), np.float32)
    for h in range(16):
        S = np.float32(8.0 * slopes[h])
        shi = np.float32(_bf(S))
        slo = np.float32(S) - shi
        ms[0, h * 128:(h + 1) * 128] = -shi
        ms[1, h * 128:(h + 1) * 128] = -slo
        ms[2, h * 128:(h + 1) * 128] = -shi
        ms[3, h * 128:(h + 1) * 128] = -slo
    tb["mslope"] = _bf(ms)
    kb = np.zeros((128, 16 * 32), np.float32)
    p = np.arange(128)
    for h in range(16):
        for c in range(32):
            kb[:, h * 32 + c] = slopes[h] * (c * 128 + p)
    tb["kbias"] = kb.astype(np.float32)
    kb2 = np.zeros((128, 12 * 32), np.float32)
    for h in range(4, 16):
        for idx in range(32):
            kb2[:, (h - 4) * 32 + idx] = slopes[h] * ((idx - 28) * 128 + p - 255)
    tb["kbias2"] = kb2.astype(np.float32)
    tb["ones_col_bf"] = _bf(np.ones((128, 1)))
    tb["ones_row_f"] = np.ones((1, 128), np.float32)
    DIM_HEAD, WINDOW = 64, 512
    half_idx = np.arange(0, DIM_HEAD, 2, dtype=np.float32)
    inv_freq = (1.0 / 10000.0 ** (half_idx / DIM_HEAD)).astype(np.float32)
    tpos = np.arange(T, dtype=np.float32)
    freqs = tpos[:, None] * inv_freq[None, :]
    freqs = np.concatenate([freqs, freqs], axis=-1)
    base = ((half_idx + 0.4 * DIM_HEAD) / (1.4 * DIM_HEAD)).astype(np.float32)
    power = ((tpos - T // 2) / float(WINDOW // 2)).astype(np.float32)
    scale = base[None, :] ** power[:, None]
    scale = np.concatenate([scale, scale], axis=-1)
    cos, sin = np.cos(freqs), np.sin(freqs)
    sgn = np.concatenate([-np.ones(32), np.ones(32)])[None, :]
    csq = (cos * scale).T
    snq = (sin * scale * sgn).T
    csk = (cos / scale).T
    snk = (sin / scale * sgn).T
    for nm, v in (("csq", csq), ("snq", snq), ("csk", csk), ("snk", snk)):
        tb[nm] = np.concatenate([v, v], axis=0).astype(np.float32)
    jb = np.arange(640)[None, :]
    pp = p[:, None]
    m1 = np.where((jb >= pp) & (jb <= pp + 512), 0.0, -1e7).astype(np.float32)
    masks = [m1]
    for qs in range(4):
        masks.append(np.where((jb >= pp) & (jb <= pp + 512) &
                              (jb >= 512 - qs * 128), 0.0, -1e7).astype(np.float32))
    tb["lmask"] = np.stack(masks)
    for c in range(4):
        ind = np.zeros((128, 8), np.float32)
        ind[0:64, 2 * c] = 1.0
        ind[64:128, 2 * c + 1] = 1.0
        tb[f"ind{c}"] = _bf(ind)
        b8 = np.zeros((8, 128), np.float32)
        b8[2 * c, 0:64] = 1.0
        b8[2 * c + 1, 64:128] = 1.0
        tb[f"b8_{c}"] = b8.astype(np.float32)
    return tb


_TABLES = make_tables()


_TAB_N = [0]


def load_table(stack, nc, cx, name, dq="sync"):
    arr = _TABLES[name]
    _TAB_N[0] += 1
    un = f"{name}_{_TAB_N[0]}"
    dt = BF16 if arr.dtype.str.endswith("bfloat16") or arr.dtype.itemsize == 2 else F32
    h = nc.inline_tensor(np.ascontiguousarray(arr), f"tab_{un}").ap()
    b = sbuf(stack, nc, f"tb_{un}", list(arr.shape[-2:]) if arr.ndim == 2 else
             [arr.shape[1], arr.shape[0] * arr.shape[2]], dt)
    if arr.ndim == 2:
        cx.dma(dq, b.ap, h[:, :], writes=[b])
    else:
        for i in range(arr.shape[0]):
            cx.dma(dq, b.ap[:, i * arr.shape[2]:(i + 1) * arr.shape[2]], h[i, :, :],
                   writes=[b])
    return b


def emit_l1_p4(stack, nc, cx, t, onesr):
    with contextlib.ExitStack() as ph:
        x1 = [sbuf(ph, nc, f"p4x{i}", [128, 4096]) for i in range(2)]
        x1b = [sbuf(ph, nc, f"p4b{i}", [128, 4096], BF16) for i in range(2)]
        sqb = sbuf(ph, nc, "p4sq", [128, 4096], BF16)
        rows = [sbuf(ph, nc, f"p4r{i}", [1, 2048]) for i in range(2)]
        xn1 = [sbuf(ph, nc, f"p4n{i}", [128, 4096]) for i in range(2)]
        xn1b = [sbuf(ph, nc, f"p4nb{i}", [128, 4096], BF16) for i in range(2)]
        onescol = load_table(ph, nc, cx, "ones_col_bf")
        ce = cx.const_col(1e-5)
        for i in range(8):
            bi = i % 2
            x, xb, xn, xnb_, rw = x1[bi], x1b[bi], xn1[bi], xn1b[bi], rows[bi]
            t0 = i * 512
            cx.dma("sync", x.ap.rearrange("p (c t) -> p c t", t=512),
                   t["x1_fm"][:, :, t0:t0 + 512].rearrange("c p t -> p c t"),
                   writes=[x])
            cx.op("vector", lambda v, x=x, xb=xb: v.tensor_copy(xb.ap, x.ap),
                  reads=[x], writes=[xb])
            cx.op("scalar", lambda s, x=x: s.activation(out=sqb.ap, in_=x.ap,
                                                        func=AF.Square),
                  reads=[x], writes=[sqb])
            psap, qs = cx.pspool.group(0)
            for c in range(8):
                cx.op("tensor", lambda te, c=c, xb=xb, qs=qs: te.matmul(
                    qs[0].ap[0:1, 0:512], onescol.ap, xb.ap[:, c * 512:(c + 1) * 512],
                    start=(c == 0), stop=(c == 7)),
                    reads=[xb, onescol], writes=[qs[0]])
            for c in range(8):
                cx.op("tensor", lambda te, c=c, qs=qs: te.matmul(
                    qs[1].ap[0:1, 0:512], onescol.ap, sqb.ap[:, c * 512:(c + 1) * 512],
                    start=(c == 0), stop=(c == 7)),
                    reads=[sqb, onescol], writes=[qs[1]])
            cx.op("vector", lambda v, rw=rw, qs=qs: v.tensor_scalar(
                rw.ap[0:1, 0:512], qs[0].ap[0:1, 0:512], 1.0 / D, None, ALU.mult),
                reads=[qs[0]], writes=[rw])
            cx.op("vector", lambda v, rw=rw, qs=qs: v.tensor_scalar(
                rw.ap[0:1, 512:1024], qs[1].ap[0:1, 0:512], 1.0 / D, None, ALU.mult),
                reads=[qs[1]], writes=[rw])
            cx.op("vector", lambda v, rw=rw: v.tensor_tensor(
                rw.ap[0:1, 1024:1536], rw.ap[0:1, 0:512], rw.ap[0:1, 0:512], ALU.mult),
                reads=[rw], writes=[rw])
            cx.op("vector", lambda v, rw=rw: v.tensor_tensor(
                rw.ap[0:1, 512:1024], rw.ap[0:1, 512:1024], rw.ap[0:1, 1024:1536],
                ALU.subtract), reads=[rw], writes=[rw])
            cx.op("scalar", lambda s, rw=rw: s.activation(
                out=rw.ap[0:1, 1024:1536], in_=rw.ap[0:1, 512:1024], func=AF.Ln,
                bias=ce.ap[0:1, 0:1]), reads=[rw, ce], writes=[rw])
            cx.op("scalar", lambda s, rw=rw: s.activation(
                out=rw.ap[0:1, 1024:1536], in_=rw.ap[0:1, 1024:1536], func=AF.Exp,
                scale=-0.5), reads=[rw], writes=[rw])
            cx.op("vector", lambda v, rw=rw: v.tensor_tensor(
                rw.ap[0:1, 1536:2048], rw.ap[0:1, 0:512], rw.ap[0:1, 1024:1536],
                ALU.mult), reads=[rw], writes=[rw])
            pr = cx.pspool.take(1, (i % 2) * 2)
            pm = cx.pspool.take(1, (i % 2) * 2 + 1)
            cx.op("tensor", lambda te, rw=rw, pr=pr: te.matmul(
                pr.ap, onesr.ap, rw.ap[0:1, 1024:1536], start=True, stop=True),
                reads=[rw, onesr], writes=[pr])
            cx.op("tensor", lambda te, rw=rw, pm=pm: te.matmul(
                pm.ap, onesr.ap, rw.ap[0:1, 1536:2048], start=True, stop=True),
                reads=[rw, onesr], writes=[pm])
            for c in range(8):
                cx.op("vector", lambda v, c=c, x=x, xn=xn, pr=pr: v.tensor_tensor(
                    xn.ap[:, c * 512:(c + 1) * 512], x.ap[:, c * 512:(c + 1) * 512],
                    pr.ap, ALU.mult), reads=[x, pr], writes=[xn])
                cx.op("vector", lambda v, c=c, xn=xn, pm=pm: v.tensor_tensor(
                    xn.ap[:, c * 512:(c + 1) * 512], xn.ap[:, c * 512:(c + 1) * 512],
                    pm.ap, ALU.subtract), reads=[xn, pm], writes=[xn])
            cx.op("vector", lambda v, xn=xn, xnb_=xnb_: v.tensor_copy(xnb_.ap, xn.ap),
                  reads=[xn], writes=[xnb_])
            cx.dma("sync", t["xn1_f32"][:, :, t0:t0 + 512].rearrange("c p t -> p c t"),
                   xn.ap.rearrange("p (c t) -> p c t", t=512), reads=[xn])
            cx.dma("sync", t["xn1_bf"][:, :, t0:t0 + 512].rearrange("c p t -> p c t"),
                   xnb_.ap.rearrange("p (c t) -> p c t", t=512), reads=[xnb_])


def emit_l1_p5(stack, nc, cx, t, ident_bf, onesr):
    with contextlib.ExitStack() as ph:
        v_sb = [sbuf(ph, nc, f"p5v{tc}", [128, 1040], BF16) for tc in range(32)]
        with contextlib.ExitStack() as pa:
            xna = sbuf(pa, nc, "p5xn", [128, DC * 4096], BF16)
            xnb = [Buf(xna.ap[:, c * 4096:(c + 1) * 4096]) for c in range(DC)]
            wst = [sbuf(pa, nc, f"p5w{i}", [128, 1024], BF16) for i in range(2)]
            bcol = sbuf(pa, nc, "p5bc", [128, 24])
            stg = [sbuf(pa, nc, f"p5st{i}", [128, 2048], BF16) for i in range(2)]
            vstg = sbuf(pa, nc, "p5vs", [128, 128], BF16)
            cx.dma("sync", xna.ap.rearrange("p (c t) -> p c t", t=4096),
                   t["xn1_bf"].rearrange("c p t -> p c t"), writes=xnb + [xna])
            for ci in range(8):
                load_cols(cx, bcol, ci, t["bq"], ci * 128)
                load_cols(cx, bcol, 8 + ci, t["bk"], ci * 128)
                load_cols(cx, bcol, 16 + ci, t["bv"], ci * 128)
            jobs = [(t["Wq"], 0, "q"), (t["Wk"], 8, "k"), (t["Wv"], 16, "v")]
            for (wsrc, boff, kind) in jobs:
                for f in range(8):
                    ws = wst[f % 2]
                    cx.dma("gpsimd",
                           ws.ap.rearrange("p (k f2) -> p k f2", f2=128),
                           wsrc[:, f * 128:(f + 1) * 128].rearrange(
                               "(k p) f -> p k f", p=128),
                           writes=[ws])
                    grp = []
                    for half in range(2):
                        psap, qs = cx.pspool.group(half)
                        grp.append((psap, qs))
                        for k in range(8):
                            for tt in range(4):
                                cx.op("tensor",
                                      lambda te, ws=ws, k=k, tt=tt, qs=qs, half=half:
                                      te.matmul(qs[tt].ap,
                                                ws.ap[:, k * 128:(k + 1) * 128],
                                                xnb[k].ap[:, half * 2048 + tt * 512:
                                                          half * 2048 + (tt + 1) * 512],
                                                start=(k == 0), stop=(k == 7)),
                                      reads=[ws, xnb[k]], writes=[qs[tt]])
                    for half in range(2):
                        psap, qs = grp[half]
                        sg = stg[half]
                        cx.op("vector", lambda v, sg=sg, a=psap, f=f, boff=boff, qs=qs:
                              v.tensor_scalar(sg.ap, a, bcol.ap[:, boff + f:boff + f + 1],
                                              None, ALU.add),
                              reads=qs + [bcol], writes=[sg])
                        if kind == "q":
                            cx.dma("sync", t["q_fm"][f, :, half * 2048:(half + 1) * 2048],
                                   sg.ap, reads=[sg])
                        elif kind == "k":
                            cx.dma("sync", t["k_fm"][f, :, half * 2048:(half + 1) * 2048],
                                   sg.ap, reads=[sg])
                        else:
                            for j in range(16):
                                tc = half * 16 + j
                                pv = cx.pspool.take(half, j % 4, BF16, 512)
                                cx.op("tensor", lambda te, pv=pv, sg=sg, j=j:
                                      te.transpose(pv.ap[:, 0:128],
                                                   sg.ap[:, j * 128:(j + 1) * 128],
                                                   ident_bf.ap),
                                      reads=[sg, ident_bf], writes=[pv])
                                cx.op("scalar", lambda s, pv=pv, vstg=vstg: s.copy(
                                    vstg.ap, pv.ap[:, 0:128]),
                                    reads=[pv], writes=[vstg])
                                cx.op("vector",
                                      lambda v, tc=tc, f=f, vstg=vstg: v.tensor_copy(
                                          v_sb[tc].ap.rearrange(
                                              "p (h d) -> p h d", d=65
                                          )[:, 2 * f:2 * f + 2, 0:64],
                                          vstg.ap.rearrange("p (h d) -> p h d", d=64)),
                                      reads=[vstg], writes=[v_sb[tc]])
            for tc in range(32):
                cx.op("vector", lambda v, tc=tc: v.memset(
                    v_sb[tc].ap.rearrange("p (h d) -> p h d", d=65)[:, :, 64:65], 1.0),
                    reads=[v_sb[tc]], writes=[v_sb[tc]])

        cx.barrier()
        with contextlib.ExitStack() as pb:
            kalla = sbuf(pb, nc, "p5ka", [128, DC * 4096], BF16)
            k_sb = [Buf(kalla.ap[:, c * 4096:(c + 1) * 4096]) for c in range(DC)]
            q_t2 = [[sbuf(pb, nc, f"p5q{j}{c}", [128, 512], BF16)
                     for c in range(DC)] for j in range(2)]
            kbias = load_table(pb, nc, cx, "kbias")
            kbias2 = load_table(pb, nc, cx, "kbias2")
            qsplit = load_table(pb, nc, cx, "qsplit")
            mslope = load_table(pb, nc, cx, "mslope")
            pt_sb = [sbuf(pb, nc, f"p5pt{i}", [128, 512], BF16) for i in range(4)]
            av_sb = [sbuf(pb, nc, f"p5av{i}", [65, 512]) for i in range(2)]
            linv = [sbuf(pb, nc, f"p5li{i}", [1, 512]) for i in range(2)]
            o_fm = [sbuf(pb, nc, f"p5o{c}", [128, 512], BF16) for c in range(DC)]
            wo_all = sbuf(pb, nc, "p5wo", [128, 8 * 1024], BF16)
            bo_c = sbuf(pb, nc, "p5bo", [128, 8])
            s1st = [sbuf(pb, nc, f"p5s1{i}", [128, 512]) for i in range(2)]
            for ci in range(8):
                load_cols(cx, bo_c, ci, t["bo"], ci * 128)
            for f in range(8):
                cx.dma("gpsimd",
                       wo_all.ap[:, f * 1024:(f + 1) * 1024].rearrange(
                           "p (k f2) -> p k f2", f2=128),
                       t["Wo"][:, f * 128:(f + 1) * 128].rearrange(
                           "(k p) f -> p k f", p=128),
                       writes=[wo_all])
            ka3 = kalla.ap.rearrange("p (c t) -> p c t", t=4096)
            cx.dma("gpsimd", ka3[:, 0:4, :],
                   t["k_fm"][0:4].rearrange("c p t -> p c t"),
                   writes=k_sb[0:4] + [kalla])
            cx.dma("sync", ka3[:, 4:8, :],
                   t["k_fm"][4:8].rearrange("c p t -> p c t"),
                   writes=k_sb[4:8])
            def q_load(qtx):
                for c in range(DC):
                    cx.dma("sync", q_t2[qtx % 2][c].ap,
                           t["q_fm"][c, :, qtx * 512:(qtx + 1) * 512],
                           writes=[q_t2[qtx % 2][c]])
            q_load(0)
            for qt in range(8):
                q_t = q_t2[qt % 2]
                nk = (qt + 1) * 4
                for hp in range(8):
                    h0, h1 = 2 * hp, 2 * hp + 1
                    heavy = hp < 2 or DBG_ALLHEAVY
                    avq = [cx.pspool.take(0, (hp % 2) * 2 + i) for i in range(2)]

                    def emit_av(kcg2, avq=avq, nk=nk, h0=h0, h1=h1):
                        for i, h in enumerate((h0, h1)):
                            ptb = pt_sb[(kcg2 % 2) * 2 + i]
                            cx.op("tensor",
                                  lambda te, ptb=ptb, h=h, kcg2=kcg2, av=avq[i], nk=nk:
                                  te.matmul(av.ap[0:65, :],
                                            v_sb[kcg2].ap[:, h * 65:h * 65 + 65],
                                            ptb.ap, start=(kcg2 == 0),
                                            stop=(kcg2 == nk - 1)),
                                  reads=[v_sb[kcg2], ptb], writes=[avq[i]])
                    pend = None
                    for kcg in range(nk):
                        sq = [cx.pspool.take(1, (kcg % 2) * 2 + i) for i in range(2)]
                        for i, of in enumerate((0, 64)):
                            cx.op("tensor",
                                  lambda te, pq=sq[i], hp=hp, of=of, kcg=kcg,
                                  hv=heavy, qtb=q_t[hp]:
                                  te.matmul(pq.ap,
                                            k_sb[hp].ap[of:of + 64,
                                                        kcg * 128:(kcg + 1) * 128],
                                            qtb.ap[of:of + 64, :],
                                            start=True, stop=not hv),
                                  reads=[k_sb[hp], q_t[hp]], writes=[sq[i]])
                        if heavy:
                            for i, h in enumerate((h0, h1)):
                                cx.op("tensor",
                                      lambda te, pq=sq[i], h=h, qt=qt:
                                      te.matmul(pq.ap,
                                                mslope.ap[0:4, h * 128:(h + 1) * 128],
                                                qsplit.ap[0:4, qt * 512:(qt + 1) * 512],
                                                start=False, stop=True),
                                      reads=[mslope, qsplit], writes=[sq[i]])
                        ptbs = [pt_sb[(kcg % 2) * 2 + i] for i in range(2)]
                        for i, h in enumerate((h0, h1)):
                            if heavy:
                                bt, col = kbias, h * 32 + kcg
                            else:
                                bt, col = kbias2, (h - 4) * 32 + (kcg - 4 * qt + 28)
                            cx.op("scalar",
                                  lambda s, pq=sq[i], ptb=ptbs[i], bt=bt, col=col:
                                  s.activation(out=ptb.ap, in_=pq.ap, func=AF.Exp,
                                               scale=0.125,
                                               bias=bt.ap[:, col:col + 1]),
                                  reads=[sq[i], bt], writes=[ptbs[i]])
                            if kcg >= 4 * qt:
                                cx.op("gpsimd",
                                      lambda g, ptb=ptbs[i], qt=qt, kcg=kcg:
                                      g.affine_select(
                                          out=ptb.ap, in_=ptb.ap,
                                          compare_op=ALU.is_ge, fill=0.0,
                                          base=qt * 512 - (kcg * 128),
                                          pattern=[[1, 512]],
                                          channel_multiplier=-1),
                                      reads=[ptbs[i]], writes=[ptbs[i]])
                        if DBG_NODEFER:
                            emit_av(kcg)
                        else:
                            if pend is not None:
                                emit_av(pend)
                            pend = kcg
                    if not DBG_NODEFER:
                        emit_av(pend)
                    for i, of in enumerate((0, 64)):
                        cx.op("scalar", lambda s, av=avq[i], ab=av_sb[i]: s.copy(
                            ab.ap, av.ap[0:65, :]),
                            reads=[avq[i]], writes=[av_sb[i]])
                        cx.op("vector", lambda v, ab=av_sb[i], li=linv[i]: v.reciprocal(
                            li.ap, ab.ap[64:65, :]),
                            reads=[av_sb[i]], writes=[linv[i]])
                        pl = cx.pspool.take(1, i)
                        cx.op("tensor", lambda te, pl=pl, li=linv[i]: te.matmul(
                            pl.ap[0:64, :], onesr.ap[0:1, 0:64], li.ap,
                            start=True, stop=True),
                            reads=[linv[i], onesr], writes=[pl])
                        cx.op("vector", lambda v, hp=hp, of=of, pl=pl, ab=av_sb[i]:
                              v.tensor_tensor(o_fm[hp].ap[of:of + 64, :],
                                              ab.ap[0:64, :], pl.ap[0:64, :], ALU.mult),
                              reads=[av_sb[i], pl], writes=[o_fm[hp]])
                    if hp == 3 and qt < 7:
                        q_load(qt + 1)
                for f in range(8):
                    pq = cx.pspool.take(1, f % 4)
                    for k in range(8):
                        cx.op("tensor", lambda te, f=f, k=k, pq=pq: te.matmul(
                            pq.ap,
                            wo_all.ap[:, f * 1024 + k * 128:f * 1024 + (k + 1) * 128],
                            o_fm[k].ap, start=(k == 0), stop=(k == 7)),
                            reads=[wo_all, o_fm[k]], writes=[pq])
                    sg = s1st[f % 2]
                    cx.op("vector", lambda v, sg=sg, pq=pq, f=f: v.tensor_scalar(
                        sg.ap, pq.ap, bo_c.ap[:, f:f + 1], None, ALU.add),
                        reads=[pq, bo_c], writes=[sg])
                    cx.dma("sync", t["s1_fm"][f, :, qt * 512:(qt + 1) * 512], sg.ap,
                           reads=[sg])


def emit_l1_p6(stack, nc, cx, t, lay, u_in_f32, u_in_bf, u_out_f32, u_out_bf,
               ident_bf, ident_f, onesr, tabs):
    HID = 2730
    with contextlib.ExitStack() as ph:
        qr = [sbuf(ph, nc, f"p6q{c}", [128, 4096], BF16) for c in range(4)]
        kr = [sbuf(ph, nc, f"p6k{c}", [128, 4608], BF16) for c in range(4)]
        vtm = [sbuf(ph, nc, f"p6v{tc}", [128, 512], BF16) for tc in range(36)]
        ce = cx.const_col(1e-5)
        c24 = cx.const_col(1e-24)
        with contextlib.ExitStack() as pa:
            usba = sbuf(pa, nc, "p6u", [128, 8 * 4097], BF16)
            u_sb = [Buf(usba.ap[:, c * 4097:(c + 1) * 4097]) for c in range(8)]
            wst = [sbuf(pa, nc, f"p6w{i}", [128, 1024], BF16) for i in range(2)]
            qkv_st = sbuf(pa, nc, "p6qkv", [128, 2048], BF16)
            vstg = sbuf(pa, nc, "p6vst", [128, 128], BF16)
            for c in range(8):
                cx.op("vector", lambda v, c=c: v.memset(u_sb[c].ap[:, 0:1], 0.0),
                      writes=[u_sb[c]])
            cx.dma("sync",
                   usba.ap.rearrange("p (c t) -> p c t", t=4097)[:, :, 1:4097],
                   u_in_bf.rearrange("c p t -> p c t"),
                   reads=list(u_sb), writes=u_sb + [usba])
            for f in range(12):
                ws = wst[f % 2]
                cx.dma("gpsimd",
                       ws.ap.rearrange("p (k f2) -> p k f2", f2=128),
                       t["wqkv"][lay][:, f * 128:(f + 1) * 128].rearrange(
                           "(k p) f -> p k f", p=128),
                       writes=[ws])
                for half in range(2):
                    psap, qs = cx.pspool.group(half % 2)
                    for k in range(8):
                        off = 1 if k < 4 else 0
                        for tt in range(4):
                            cx.op("tensor",
                                  lambda te, ws=ws, k=k, tt=tt, qs=qs, half=half, off=off:
                                  te.matmul(qs[tt].ap,
                                            ws.ap[:, k * 128:(k + 1) * 128],
                                            u_sb[k].ap[:, off + half * 2048 + tt * 512:
                                                       off + half * 2048 + (tt + 1) * 512],
                                            start=(k == 0), stop=(k == 7)),
                                  reads=[ws, u_sb[k]], writes=[qs[tt]])
                    cx.op("vector", lambda v, a=psap, qs=qs: v.tensor_copy(
                        qkv_st.ap, a), reads=qs, writes=[qkv_st])
                    if f < 4:
                        cx.op("vector", lambda v, f=f, half=half: v.tensor_copy(
                            qr[f].ap[:, half * 2048:(half + 1) * 2048], qkv_st.ap),
                            reads=[qkv_st], writes=[qr[f]])
                    elif f < 8:
                        cx.op("vector", lambda v, f=f, half=half: v.tensor_copy(
                            kr[f - 4].ap[:, 512 + half * 2048:512 + (half + 1) * 2048],
                            qkv_st.ap), reads=[qkv_st], writes=[kr[f - 4]])
                    else:
                        for j in range(16):
                            tc = 4 + half * 16 + j
                            pv = cx.pspool.take(1 - half % 2, j % 4, BF16, 512)
                            cx.op("tensor", lambda te, pv=pv, j=j: te.transpose(
                                pv.ap[:, 0:128], qkv_st.ap[:, j * 128:(j + 1) * 128],
                                ident_bf.ap),
                                reads=[qkv_st, ident_bf], writes=[pv])
                            cx.op("scalar", lambda s, pv=pv: s.copy(
                                vstg.ap, pv.ap[:, 0:128]), reads=[pv], writes=[vstg])
                            cx.op("vector", lambda v, tc=tc, f=f: v.tensor_copy(
                                vtm[tc].ap[:, (f - 8) * 128:(f - 7) * 128], vstg.ap),
                                reads=[vstg], writes=[vtm[tc]])
            for tc in range(4):
                cx.op("vector", lambda v, tc=tc: v.memset(vtm[tc].ap, 0.0),
                      writes=[vtm[tc]])
            for c in range(4):
                cx.op("vector", lambda v, c=c: v.memset(kr[c].ap[:, 0:512], 0.0),
                      reads=[kr[c]], writes=[kr[c]])
        cx.barrier()
        with contextlib.ExitStack() as pa:
            csq = load_table(pa, nc, cx, "csq")
            snq = load_table(pa, nc, cx, "snq")
            csk = load_table(pa, nc, cx, "csk")
            snk = load_table(pa, nc, cx, "snk")
            nrm2 = [sbuf(pa, nc, f"p6n{i}", [8, 512]) for i in range(2)]
            sqb2_ = [sbuf(pa, nc, f"p6sq{i}", [128, 512], BF16) for i in range(2)]
            qn_sb = sbuf(pa, nc, "p6qn", [128, 512])
            qsw = sbuf(pa, nc, "p6qs", [128, 512])
            t1 = sbuf(pa, nc, "p6t1", [128, 512])
            t2 = sbuf(pa, nc, "p6t2", [128, 512])
            scol = sbuf(pa, nc, "p6sc", [128, 2])
            load_cols(cx, scol, 0, t["q_scale"][lay], 0, n=64)
            load_cols(cx, scol, 1, t["k_scale"][lay], 0, n=64)
            cx.op("vector", lambda v: v.tensor_copy(scol.ap[64:128, 0:2],
                                                    scol.ap[0:64, 0:2]),
                  reads=[scol], writes=[scol])
            for (dst, prefix, scix, cst, snt) in ((qr, 0, 0, csq, snq),
                                                  (kr, 512, 1, csk, snk)):
                for st in range(8):
                    pn = cx.pspool.take(0, st % 2)
                    nrm = nrm2[st % 2]
                    for c in range(4):
                        sqb = sqb2_[c % 2]
                        cx.op("scalar", lambda s, c=c, st=st, dst=dst, prefix=prefix,
                              sqb=sqb:
                              s.activation(out=sqb.ap,
                                           in_=dst[c].ap[:, prefix + st * 512:
                                                         prefix + (st + 1) * 512],
                                           func=AF.Square),
                              reads=[dst[c]], writes=[sqb])
                        cx.op("tensor", lambda te, c=c, pn=pn, sqb=sqb: te.matmul(
                            pn.ap[0:8, :], tabs[f"ind{c}"].ap, sqb.ap,
                            start=(c == 0), stop=(c == 3)),
                            reads=[sqb, tabs[f"ind{c}"]], writes=[pn])
                    cx.op("vector", lambda v, pn=pn, nrm=nrm: v.tensor_scalar(
                        nrm.ap, pn.ap[0:8, :], c24.ap[0:8, 0:1], None, ALU.max),
                        reads=[pn, c24], writes=[nrm])
                    cx.op("scalar", lambda s, nrm=nrm: s.activation(
                        out=nrm.ap, in_=nrm.ap, func=AF.Ln),
                        reads=[nrm], writes=[nrm])
                    cx.op("scalar", lambda s, nrm=nrm: s.activation(
                        out=nrm.ap, in_=nrm.ap, func=AF.Exp, scale=-0.5),
                        reads=[nrm], writes=[nrm])
                    for c in range(4):
                        pb = cx.pspool.take(0, 2 + c % 2)
                        cx.op("tensor", lambda te, c=c, pb=pb, nrm=nrm: te.matmul(
                            pb.ap, tabs[f"b8_{c}"].ap, nrm.ap, start=True, stop=True),
                            reads=[nrm, tabs[f"b8_{c}"]], writes=[pb])
                        cx.op("vector", lambda v, c=c, st=st, dst=dst, prefix=prefix,
                              pb=pb, scix=scix: v.scalar_tensor_tensor(
                                  qn_sb.ap, dst[c].ap[:, prefix + st * 512:
                                                      prefix + (st + 1) * 512],
                                  scol.ap[:, scix:scix + 1], pb.ap, ALU.mult, ALU.mult),
                              reads=[dst[c], scol, pb], writes=[qn_sb])
                        for b4 in range(4):
                            src = [32, 0, 96, 64][b4]
                            cx.op("vector", lambda v, b4=b4, src=src: v.tensor_copy(
                                qsw.ap[b4 * 32:(b4 + 1) * 32, :],
                                qn_sb.ap[src:src + 32, :]),
                                reads=[qn_sb], writes=[qsw])
                        cx.op("vector", lambda v, st=st, cst=cst: v.tensor_tensor(
                            t1.ap, qn_sb.ap, cst.ap[:, st * 512:(st + 1) * 512],
                            ALU.mult), reads=[qn_sb, cst], writes=[t1])
                        cx.op("vector", lambda v, st=st, snt=snt: v.tensor_tensor(
                            t2.ap, qsw.ap, snt.ap[:, st * 512:(st + 1) * 512],
                            ALU.mult), reads=[qsw, snt], writes=[t2])
                        cx.op("vector", lambda v, c=c, st=st, dst=dst, prefix=prefix:
                              v.tensor_tensor(
                                  dst[c].ap[:, prefix + st * 512:prefix + (st + 1) * 512],
                                  t1.ap, t2.ap, ALU.add),
                              reads=[t1, t2], writes=[dst[c]])
        cx.barrier()
        with contextlib.ExitStack() as pb_:
            msk = sbuf(pb_, nc, "p6m", [128, 5 * 640])
            mh = nc.inline_tensor(np.ascontiguousarray(_TABLES["lmask"]),
                                  f"lmask{lay}").ap()
            for i in range(5):
                cx.dma("sync", msk.ap[:, i * 640:(i + 1) * 640], mh[i, :, :],
                       writes=[msk])
            masked = [sbuf(pb_, nc, f"p6ms{i}", [128, 640]) for i in range(4)]
            mcol = [sbuf(pb_, nc, f"p6mc{i}", [128, 2]) for i in range(4)]
            pbf = [sbuf(pb_, nc, f"p6pb{i}", [128, 640], BF16) for i in range(4)]
            pt_sb = [sbuf(pb_, nc, f"p6pt{i}", [128, 640], BF16) for i in range(2)]
            o_fm = [sbuf(pb_, nc, f"p6o{c}", [128, 512], BF16) for c in range(4)]
            wo_st = [sbuf(pb_, nc, f"p6wo{i}", [128, 512], BF16) for i in range(2)]
            u_ld = [sbuf(pb_, nc, f"p6ul{i}", [128, 512]) for i in range(2)]
            u1st = [sbuf(pb_, nc, f"p6u1{i}", [128, 512]) for i in range(2)]
            u1bf = sbuf(pb_, nc, "p6u1b", [128, 512], BF16)
            for w in range(8):

                def stage2(pb, bi, ch, of, qs, w=w):
                    pt = pt_sb[bi % 2]
                    for kc in range(5):
                        pv = cx.pspool.take(0, 2 + (kc % 2), BF16, 512)
                        cx.op("tensor", lambda te, pv=pv, pb=pb, kc=kc: te.transpose(
                            pv.ap[:, 0:128], pb.ap[:, kc * 128:(kc + 1) * 128],
                            ident_bf.ap),
                            reads=[pb, ident_bf], writes=[pv])
                        cx.op("scalar", lambda s, pv=pv, pt=pt, kc=kc: s.copy(
                            pt.ap[:, kc * 128:(kc + 1) * 128], pv.ap[:, 0:128]),
                            reads=[pv], writes=[pt])
                    pav = cx.pspool.take(0, bi % 2)
                    h2_ = 2 * ch + (1 if of else 0)
                    for kc in range(5):
                        tc = (w - 1) * 4 + qs + kc + 4
                        cx.op("tensor", lambda te, pav=pav, tc=tc, h2_=h2_, pt=pt,
                              kc=kc:
                              te.matmul(pav.ap[0:64, 0:128],
                                        vtm[tc].ap[:, h2_ * 64:(h2_ + 1) * 64],
                                        pt.ap[:, kc * 128:(kc + 1) * 128],
                                        start=(kc == 0), stop=(kc == 4)),
                              reads=[vtm[tc], pt], writes=[pav])
                    cx.op("scalar", lambda s, pav=pav, ch=ch, of=of, qs=qs: s.copy(
                        o_fm[ch].ap[of:of + 64, qs * 128:(qs + 1) * 128],
                        pav.ap[0:64, 0:128]),
                        reads=[pav], writes=[o_fm[ch]])
                pend = []
                bi = 0
                for h in range(8):
                    ch, of = h // 2, 64 * (h % 2)
                    for qs in range(4):
                        pq = cx.pspool.span(1, 2 * (bi % 2), 2)
                        ms, mc, pb = masked[bi % 4], mcol[bi % 4], pbf[bi % 4]
                        cx.op("tensor", lambda te, pq=pq, ch=ch, of=of, w=w, qs=qs:
                              te.matmul(pq.ap[:, 0:512],
                                        qr[ch].ap[of:of + 64,
                                                  w * 512 + qs * 128:
                                                  w * 512 + (qs + 1) * 128],
                                        kr[ch].ap[of:of + 64,
                                                  w * 512 + qs * 128:
                                                  w * 512 + qs * 128 + 512],
                                        start=True, stop=True),
                              reads=[qr[ch], kr[ch]], writes=[pq])
                        cx.op("tensor", lambda te, pq=pq, ch=ch, of=of, w=w, qs=qs:
                              te.matmul(pq.ap[:, 512:640],
                                        qr[ch].ap[of:of + 64,
                                                  w * 512 + qs * 128:
                                                  w * 512 + (qs + 1) * 128],
                                        kr[ch].ap[of:of + 64,
                                                  w * 512 + qs * 128 + 512:
                                                  w * 512 + qs * 128 + 640],
                                        start=True, stop=True),
                              reads=[qr[ch], kr[ch]], writes=[pq])
                        mi = 0 if w > 0 else 1 + qs
                        cx.op("vector", lambda v, pq=pq, ms=ms, mi=mi:
                              v.scalar_tensor_tensor(
                                  ms.ap, pq.ap[:, 0:640], 1.0,
                                  msk.ap[:, mi * 640:(mi + 1) * 640],
                                  ALU.mult, ALU.add),
                              reads=[pq, msk], writes=[ms])
                        cx.op("vector", lambda v, ms=ms, mc=mc: v.reduce_max(
                            mc.ap[:, 0:1], ms.ap, axis=AX.X),
                            reads=[ms], writes=[mc])
                        cx.op("vector", lambda v, mc=mc: v.tensor_scalar(
                            mc.ap[:, 1:2], mc.ap[:, 0:1], -8.0, None, ALU.mult),
                            reads=[mc], writes=[mc])
                        cx.op("scalar", lambda s, ms=ms, mc=mc, pb=pb: s.activation(
                            out=pb.ap, in_=ms.ap, func=AF.Exp, scale=8.0,
                            bias=mc.ap[:, 1:2], accum_out=mc.ap[:, 0:1]),
                            reads=[ms, mc], writes=[pb, mc])
                        cx.op("vector", lambda v, mc=mc: v.reciprocal(
                            mc.ap[:, 0:1], mc.ap[:, 0:1]),
                            reads=[mc], writes=[mc])
                        cx.op("vector", lambda v, pb=pb, mc=mc: v.tensor_scalar(
                            pb.ap, pb.ap, mc.ap[:, 0:1], None, ALU.mult),
                            reads=[pb, mc], writes=[pb])
                        pend.append((pb, bi, ch, of, qs))
                        if len(pend) > 3:
                            stage2(*pend.pop(0))
                        bi += 1
                for pe_ in pend:
                    stage2(*pe_)
                for f in range(8):
                    wo = wo_st[f % 2]
                    cx.dma("gpsimd",
                           wo.ap.rearrange("p (k f2) -> p k f2", f2=128),
                           t["wo"][lay][:, f * 128:(f + 1) * 128].rearrange(
                               "(k p) f -> p k f", p=128),
                           writes=[wo])
                    pq = cx.pspool.take(1, f % 2)
                    for k in range(4):
                        cx.op("tensor", lambda te, wo=wo, k=k, pq=pq: te.matmul(
                            pq.ap, wo.ap[:, k * 128:(k + 1) * 128], o_fm[k].ap,
                            start=(k == 0), stop=(k == 3)),
                            reads=[wo, o_fm[k]], writes=[pq])
                    ul, u1 = u_ld[f % 2], u1st[f % 2]
                    cx.dma("sync", ul.ap, u_in_f32[f, :, w * 512:(w + 1) * 512],
                           writes=[ul])
                    cx.op("vector", lambda v, pq=pq, ul=ul, u1=u1: v.tensor_tensor(
                        u1.ap, pq.ap, ul.ap, ALU.add),
                        reads=[pq, ul], writes=[u1])
                    cx.op("vector", lambda v, u1=u1: v.tensor_copy(u1bf.ap, u1.ap),
                          reads=[u1], writes=[u1bf])
                    cx.dma("sync", t["u1_f32"][f, :, w * 512:(w + 1) * 512], u1.ap,
                           reads=[u1])
                    cx.dma("sync", t["u1_bf"][f, :, w * 512:(w + 1) * 512], u1bf.ap,
                           reads=[u1bf])
    cx.barrier()
    with contextlib.ExitStack() as pc:
        u1a = sbuf(pc, nc, "p6c", [128, 8 * 4097], BF16)
        u1_sb = [Buf(u1a.ap[:, c * 4097:(c + 1) * 4097]) for c in range(8)]
        hn = [sbuf(pc, nc, f"p6h{c}", [128, 4096], BF16) for c in range(8)]
        sqb2l = [sbuf(pc, nc, f"p6sq2{i}", [128, 512], BF16)
                 for i in range(2)]
        rows2l = [sbuf(pc, nc, "p6r", [1, 2048])] * 2
        lgc = sbuf(pc, nc, "p6lg", [128, 16])
        onescol = load_table(pc, nc, cx, "ones_col_bf", dq="gpsimd")
        tA = sbuf(pc, nc, "p6tA", [128, 512])
        for ci in range(8):
            load_cols(cx, lgc, ci, t["ln_g"][lay], ci * 128)
            load_cols(cx, lgc, 8 + ci, t["ln_b"][lay], ci * 128)
        for c in range(8):
            cx.op("vector", lambda v, c=c: v.memset(u1_sb[c].ap[:, 0:1], 0.0),
                  writes=[u1_sb[c]])
        cx.dma("sync",
               u1a.ap.rearrange("p (c t) -> p c t", t=4097)[:, :, 1:4097],
               t["u1_bf"].rearrange("c p t -> p c t"),
               reads=list(u1_sb), writes=u1_sb + [u1a])

        def ush(c, st):
            off = 1 if c < 4 else 0
            return u1_sb[c].ap[:, off + st * 512:off + (st + 1) * 512]
        for st in range(8):
            rows = rows2l[st % 2]
            psap, qs = cx.pspool.group(0)
            for c in range(8):
                cx.op("tensor", lambda te, c=c, st=st, qs=qs: te.matmul(
                    qs[0].ap[0:1, :], onescol.ap, ush(c, st),
                    start=(c == 0), stop=(c == 7)),
                    reads=[u1_sb[c], onescol], writes=[qs[0]])
            for c in range(8):
                sqb2 = sqb2l[c % 2]
                cx.op("scalar", lambda s, c=c, st=st, sqb2=sqb2: s.activation(
                    out=sqb2.ap, in_=ush(c, st), func=AF.Square),
                    reads=[u1_sb[c]], writes=[sqb2])
                cx.op("tensor", lambda te, c=c, qs=qs, sqb2=sqb2: te.matmul(
                    qs[1].ap[0:1, :], onescol.ap, sqb2.ap,
                    start=(c == 0), stop=(c == 7)),
                    reads=[sqb2, onescol], writes=[qs[1]])
            cx.op("vector", lambda v, qs=qs: v.tensor_scalar(
                rows.ap[0:1, 0:512], qs[0].ap[0:1, :], 1.0 / D, None, ALU.mult),
                reads=[qs[0]], writes=[rows])
            cx.op("vector", lambda v, qs=qs: v.tensor_scalar(
                rows.ap[0:1, 512:1024], qs[1].ap[0:1, :], 1.0 / D, None, ALU.mult),
                reads=[qs[1]], writes=[rows])
            cx.op("vector", lambda v: v.tensor_tensor(
                rows.ap[0:1, 1024:1536], rows.ap[0:1, 0:512], rows.ap[0:1, 0:512],
                ALU.mult), reads=[rows], writes=[rows])
            cx.op("vector", lambda v: v.tensor_tensor(
                rows.ap[0:1, 512:1024], rows.ap[0:1, 512:1024],
                rows.ap[0:1, 1024:1536], ALU.subtract), reads=[rows], writes=[rows])
            cx.op("scalar", lambda s: s.activation(
                out=rows.ap[0:1, 1024:1536], in_=rows.ap[0:1, 512:1024], func=AF.Ln,
                bias=ce.ap[0:1, 0:1]), reads=[rows, ce], writes=[rows])
            cx.op("scalar", lambda s: s.activation(
                out=rows.ap[0:1, 1024:1536], in_=rows.ap[0:1, 1024:1536],
                func=AF.Exp, scale=-0.5), reads=[rows], writes=[rows])
            cx.op("vector", lambda v: v.tensor_tensor(
                rows.ap[0:1, 1536:2048], rows.ap[0:1, 0:512],
                rows.ap[0:1, 1024:1536], ALU.mult), reads=[rows], writes=[rows])
            pr = cx.pspool.take(1, (st % 2) * 2)
            pm = cx.pspool.take(1, (st % 2) * 2 + 1)
            cx.op("tensor", lambda te, pr=pr: te.matmul(
                pr.ap, onesr.ap, rows.ap[0:1, 1024:1536], start=True, stop=True),
                reads=[rows, onesr], writes=[pr])
            cx.op("tensor", lambda te, pm=pm: te.matmul(
                pm.ap, onesr.ap, rows.ap[0:1, 1536:2048], start=True, stop=True),
                reads=[rows, onesr], writes=[pm])
            for c in range(8):
                cx.op("vector", lambda v, c=c, st=st, pr=pr: v.tensor_tensor(
                    tA.ap, ush(c, st), pr.ap, ALU.mult),
                    reads=[u1_sb[c], pr], writes=[tA])
                cx.op("vector", lambda v, pm=pm: v.tensor_tensor(
                    tA.ap, tA.ap, pm.ap, ALU.subtract),
                    reads=[tA, pm], writes=[tA])
                cx.op("vector", lambda v, c=c, st=st: v.tensor_scalar(
                    hn[c].ap[:, st * 512:(st + 1) * 512], tA.ap,
                    lgc.ap[:, c:c + 1], lgc.ap[:, 8 + c:9 + c], ALU.mult, ALU.add),
                    reads=[tA, lgc], writes=[hn[c]])
        wa_st = [sbuf(pc, nc, f"p6wa{i}", [128, 1024], BF16) for i in range(2)]
        wg_st = [sbuf(pc, nc, f"p6wg{i}", [128, 1024], BF16) for i in range(2)]
        gel = sbuf(pc, nc, "p6gel", [128, 2048])
        h2b = sbuf(pc, nc, "p6h2", [128, 2048], BF16)
        b1c = sbuf(pc, nc, "p6b1", [128, 44])
        for j in range(21):
            load_cols(cx, b1c, j, t["b1"][lay], j * 128)
            load_cols(cx, b1c, 22 + j, t["b1"][lay], HID + j * 128)
        cx.dma("sync", b1c.ap[0:42, 21], t["b1"][lay][21 * 128:HID], writes=[b1c])
        cx.dma("sync", b1c.ap[0:42, 43], t["b1"][lay][HID + 21 * 128:2 * HID],
               writes=[b1c])
        for half in range(2):
            h0 = half * 2048
            for j in range(22):
                wj = 128 if j < 21 else 42
                wa, wg = wa_st[j % 2], wg_st[j % 2]
                cx.dma("gpsimd",
                       wa.ap[:, 0:8 * wj].rearrange("p (k f2) -> p k f2", f2=wj),
                       t["w1"][lay][:, j * 128:j * 128 + wj].rearrange(
                           "(k p) f -> p k f", p=128),
                       writes=[wa])
                cx.dma("gpsimd",
                       wg.ap[:, 0:8 * wj].rearrange("p (k f2) -> p k f2", f2=wj),
                       t["w1"][lay][:, HID + j * 128:HID + j * 128 + wj].rearrange(
                           "(k p) f -> p k f", p=128),
                       writes=[wg])
                grp = {}
                for (pi, wb) in ((0, wa), (1, wg)):
                    psap, qs = cx.pspool.group(pi)
                    grp[pi] = (psap, qs)
                    for k in range(8):
                        for tt in range(4):
                            cx.op("tensor",
                                  lambda te, wb=wb, k=k, tt=tt, qs=qs, wj=wj, h0=h0:
                                  te.matmul(qs[tt].ap[0:wj, :],
                                            wb.ap[:, k * wj:(k + 1) * wj],
                                            hn[k].ap[:, h0 + tt * 512:
                                                     h0 + (tt + 1) * 512],
                                            start=(k == 0), stop=(k == 7)),
                                  reads=[wb, hn[k]], writes=[qs[tt]])
                ps0ap, ps0q = grp[0]
                ps1ap, ps1q = grp[1]
                cx.op("scalar", lambda s, a=ps1ap, j=j, wj=wj: s.activation(
                    out=gel.ap[0:wj, :], in_=cx.pspool.t[1][0:wj, :], func=AF.Gelu,
                    bias=b1c.ap[0:wj, 22 + j:23 + j]),
                    reads=ps1q + [b1c], writes=[gel])
                cx.op("vector", lambda v, j=j, wj=wj: v.tensor_scalar(
                    h2b.ap[0:wj, :], cx.pspool.t[0][0:wj, :],
                    b1c.ap[0:wj, j:j + 1], None, ALU.add),
                    reads=ps0q + [b1c], writes=[h2b])
                cx.op("vector", lambda v, wj=wj: v.tensor_tensor(
                    h2b.ap[0:wj, :].bitcast(F32)[:, 0:2048] if False else
                    h2b.ap[0:wj, :], h2b.ap[0:wj, :], gel.ap[0:wj, :], ALU.mult),
                    reads=[h2b, gel], writes=[h2b])
                cx.dma("sync", t["h2_s"][j, 0:wj, h0:h0 + 2048], h2b.ap[0:wj, :],
                       reads=[h2b])
        cx.barrier()
        w2a = [sbuf(pc, nc, f"p6w2{i}", [128, 22 * 128], BF16) for i in range(2)]
        h2l = [sbuf(pc, nc, f"p6h2l{i}", [128, 2048], BF16) for i in range(2)]
        b2c = sbuf(pc, nc, "p6b2", [128, 8])
        uol = [sbuf(pc, nc, "p6uo", [128, 2048])] * 2
        u2st = [sbuf(pc, nc, "p6u2", [128, 2048])] * 2
        u2bf = sbuf(pc, nc, "p6u2b", [128, 2048], BF16)
        for ci in range(8):
            load_cols(cx, b2c, ci, t["b2"][lay], ci * 128)
        for half in range(2):
            h0 = half * 2048
            for fop in range(4):
                grp = {}
                wds = []
                for sub in range(2):
                    fo = fop * 2 + sub
                    psap, qs = cx.pspool.group(sub)
                    grp[sub] = (psap, qs)
                    wd = w2a[sub]
                    cx.dma("gpsimd",
                           wd.ap[:, 0:21 * 128].rearrange(
                               "p (k f2) -> p k f2", f2=128),
                           t["w2"][lay][0:21 * 128, fo * 128:(fo + 1) * 128]
                           .rearrange("(k p) f -> p k f", p=128),
                           writes=[wd])
                    cx.dma("gpsimd",
                           wd.ap[0:42, 21 * 128:22 * 128],
                           t["w2"][lay][21 * 128:HID, fo * 128:(fo + 1) * 128],
                           writes=[wd])
                    wds.append(wd)
                for j in range(22):
                    wj = 128 if j < 21 else 42
                    hl = h2l[j % 2]
                    cx.dma("sync", hl.ap[0:wj, :], t["h2_s"][j, 0:wj, h0:h0 + 2048],
                           writes=[hl])
                    for sub in range(2):
                        psap, qs = grp[sub]
                        for tt in range(4):
                            cx.op("tensor",
                                  lambda te, wd=wds[sub], j=j, tt=tt, qs=qs, wj=wj,
                                  hl=hl:
                                  te.matmul(qs[tt].ap,
                                            wd.ap[0:wj, j * 128:j * 128 + 128],
                                            hl.ap[0:wj, tt * 512:(tt + 1) * 512],
                                            start=(j == 0), stop=(j == 21)),
                                  reads=[wds[sub], hl], writes=[qs[tt]])
                for sub in range(2):
                    fo = fop * 2 + sub
                    psap, qs = grp[sub]
                    ul, u2 = uol[sub], u2st[sub]
                    cx.dma("sync", ul.ap, t["u1_f32"][fo, :, h0:h0 + 2048],
                           writes=[ul])
                    cx.op("vector", lambda v, u2=u2, a=psap, fo=fo, qs=qs:
                          v.tensor_scalar(u2.ap, a, b2c.ap[:, fo:fo + 1], None,
                                          ALU.add),
                          reads=qs + [b2c], writes=[u2])
                    cx.op("vector", lambda v, u2=u2, ul=ul: v.tensor_tensor(
                        u2.ap, u2.ap, ul.ap, ALU.add),
                        reads=[u2, ul], writes=[u2])
                    cx.op("vector", lambda v, u2=u2: v.tensor_copy(u2bf.ap, u2.ap),
                          reads=[u2], writes=[u2bf])
                    cx.dma("sync", u_out_f32[fo, :, h0:h0 + 2048], u2.ap,
                           reads=[u2])
                    cx.dma("sync", u_out_bf[fo, :, h0:h0 + 2048], u2bf.ap,
                           reads=[u2bf])


def emit_l1_p8(stack, nc, cx, t, ident_f):
    with contextlib.ExitStack() as ph:
        u4 = [sbuf(ph, nc, f"p8u{i}", [128, 4096]) for i in range(2)]
        s1 = [sbuf(ph, nc, f"p8s{i}", [128, 4096]) for i in range(2)]
        x1 = [sbuf(ph, nc, f"p8x{i}", [128, 4096]) for i in range(2)]
        sil = sbuf(ph, nc, "p8si", [128, 4096])
        x2 = [sbuf(ph, nc, f"p8x2{i}", [128, 4096]) for i in range(2)]
        ost = [sbuf(ph, nc, f"p8o{i}", [128, 1024]) for i in range(2)]
        for i in range(8):
            bi = i % 2
            t0 = i * 512
            uu, ss, xx, xo = u4[bi], s1[bi], x1[bi], x2[bi]
            cx.dma("sync", uu.ap.rearrange("p (c t) -> p c t", t=512),
                   t["u4_f32"][:, :, t0:t0 + 512].rearrange("c p t -> p c t"),
                   writes=[uu])
            cx.dma("sync", ss.ap.rearrange("p (c t) -> p c t", t=512),
                   t["s1_fm"][:, :, t0:t0 + 512].rearrange("c p t -> p c t"),
                   writes=[ss])
            cx.dma("sync", xx.ap.rearrange("p (c t) -> p c t", t=512),
                   t["x1_fm"][:, :, t0:t0 + 512].rearrange("c p t -> p c t"),
                   writes=[xx])
            cx.op("scalar", lambda s, uu=uu: s.activation(
                out=sil.ap, in_=uu.ap, func=AF.Silu), reads=[uu], writes=[sil])
            cx.op("vector", lambda v, ss=ss, xo=xo: v.tensor_tensor(
                xo.ap, sil.ap, ss.ap, ALU.mult), reads=[sil, ss], writes=[xo])
            cx.op("vector", lambda v, xx=xx, xo=xo: v.tensor_tensor(
                xo.ap, xo.ap, xx.ap, ALU.add), reads=[xo, xx], writes=[xo])
            for j in range(4):
                ob = ost[j % 2]
                for cc in range(2):
                    pq = cx.pspool.take(0, 2 * (j % 2) + cc)
                    for c4 in range(4):
                        c = cc * 4 + c4
                        cx.op("tensor", lambda te, pq=pq, c=c, c4=c4, j=j, xo=xo:
                              te.transpose(pq.ap[:, c4 * 128:(c4 + 1) * 128],
                                           xo.ap[:, c * 512 + j * 128:
                                                 c * 512 + (j + 1) * 128],
                                           ident_f.ap),
                              reads=[xo, ident_f], writes=[pq])
                    cx.op("scalar", lambda s, pq=pq, cc=cc, ob=ob: s.copy(
                        ob.ap[:, cc * 512:(cc + 1) * 512], pq.ap),
                        reads=[pq], writes=[ob])
                cx.dma("sync", t["out"][t0 + j * 128:t0 + (j + 1) * 128, :], ob.ap,
                       reads=[ob])


def build_l1(debug=False):
    nc = bass.Bass("TRN2", target_bir_lowering=False)
    t = {}
    t["x1_fm"] = nc.dram_tensor("x1_fm", [DC, 128, T], F32, kind="ExternalInput").ap()
    for nm, shp in [("Wq", [D, D]), ("bq", [D]), ("Wk", [D, D]), ("bk", [D]),
                    ("Wv", [D, D]), ("bv", [D]), ("Wo", [D, D]), ("bo", [D])]:
        t[nm] = nc.dram_tensor(nm, shp, F32, kind="ExternalInput").ap()
    HID = 2730
    for nm, shp in [("wqkv", [2, D, 1536]), ("q_scale", [2, 64]), ("k_scale", [2, 64]),
                    ("wo", [2, 512, D]), ("ln_g", [2, D]), ("ln_b", [2, D]),
                    ("w1", [2, D, 2 * HID]), ("b1", [2, 2 * HID]),
                    ("w2", [2, HID, D]), ("b2", [2, D])]:
        ap = nc.dram_tensor(nm, shp, F32, kind="ExternalInput").ap()
        t[nm] = [ap[0], ap[1]]
    t["out"] = nc.dram_tensor("out", [T, D], F32, kind="ExternalOutput").ap()
    sk = "ExternalOutput" if debug else "Internal"
    t["xn1_f32"] = nc.dram_tensor("xn1_f32", [DC, 128, T], F32, kind=sk).ap()
    t["xn1_bf"] = nc.dram_tensor("xn1_bf", [DC, 128, T], BF16, kind="Internal").ap()
    t["q_fm"] = nc.dram_tensor("q_fm", [DC, 128, T], BF16, kind="Internal").ap()
    t["k_fm"] = nc.dram_tensor("k_fm", [DC, 128, T], BF16, kind="Internal").ap()
    t["s1_fm"] = nc.dram_tensor("s1_fm", [DC, 128, T], F32, kind=sk).ap()
    t["u1_f32"] = nc.dram_tensor("u1_f32", [DC, 128, T], F32, kind=sk).ap()
    t["u1_bf"] = nc.dram_tensor("u1_bf", [DC, 128, T], BF16, kind="Internal").ap()
    t["u2_f32"] = nc.dram_tensor("u2_f32", [DC, 128, T], F32, kind=sk).ap()
    t["u2_bf"] = nc.dram_tensor("u2_bf", [DC, 128, T], BF16, kind="Internal").ap()
    t["u4_f32"] = nc.dram_tensor("u4_f32", [DC, 128, T], F32, kind=sk).ap()
    t["u4_bf"] = nc.dram_tensor("u4_bf", [DC, 128, T], BF16, kind="Internal").ap()
    t["h2_s"] = nc.dram_tensor("h2_s", [22, 128, T], BF16, kind="Internal").ap()

    with contextlib.ExitStack() as stack:
        kb = KB(nc, stack)
        cx = Ctx(kb)
        cx.pspool = PsumPool(stack, nc)
        cx.cbuf = sbuf(stack, nc, "consts", [128, 16])
        tabs = {}
        for nm in ("ident_bf", "ident_f", "ones_row_f",
                   "ind0", "ind1", "ind2", "ind3", "b8_0", "b8_1", "b8_2", "b8_3"):
            tabs[nm] = load_table(stack, nc, cx, nm)
        ident_bf, ident_f, onesr = tabs["ident_bf"], tabs["ident_f"], tabs["ones_row_f"]
        emit_l1_p4(stack, nc, cx, t, onesr)
        cx.barrier()
        emit_l1_p5(stack, nc, cx, t, ident_bf, onesr)
        cx.barrier()
        emit_l1_p6(stack, nc, cx, t, 0, t["xn1_f32"], t["xn1_bf"],
                   t["u2_f32"], t["u2_bf"], ident_bf, ident_f, onesr, tabs)
        cx.barrier()
        emit_l1_p6(stack, nc, cx, t, 1, t["u2_f32"], t["u2_bf"],
                   t["u4_f32"], t["u4_bf"], ident_bf, ident_f, onesr, tabs)
        cx.barrier()
        emit_l1_p8(stack, nc, cx, t, ident_f)
        with nc.Block() as block:
            kb.finalize(block)
    return nc


def build_full():
    nc = bass.Bass("TRN2", target_bir_lowering=False)
    t = {}
    t["x_in"] = nc.dram_tensor("x", [T, D], F32, kind="ExternalInput").ap()
    t["w_in"] = nc.dram_tensor("W_in", [D, 2 * D], F32, kind="ExternalInput").ap()
    t["b_in"] = nc.dram_tensor("b_in", [2 * D], F32, kind="ExternalInput").ap()
    t["a_in"] = nc.dram_tensor("A", [D], F32, kind="ExternalInput").ap()
    t["c_in"] = nc.dram_tensor("C", [D], F32, kind="ExternalInput").ap()
    t["w_out"] = nc.dram_tensor("W_out", [D, D], F32, kind="ExternalInput").ap()
    t["b_out"] = nc.dram_tensor("b_out", [D], F32, kind="ExternalInput").ap()
    t["w_gate"] = nc.dram_tensor("w_gate", [FFN, D], F32, kind="ExternalInput").ap()
    t["w_up"] = nc.dram_tensor("w_up", [FFN, D], F32, kind="ExternalInput").ap()
    t["w_down"] = nc.dram_tensor("w_down", [D, FFN], F32, kind="ExternalInput").ap()
    for nm, shp in [("Wq", [D, D]), ("bq", [D]), ("Wk", [D, D]), ("bk", [D]),
                    ("Wv", [D, D]), ("bv", [D]), ("Wo", [D, D]), ("bo", [D])]:
        t[nm] = nc.dram_tensor(nm, shp, F32, kind="ExternalInput").ap()
    HID = 2730
    for nm, shp in [("wqkv", [2, D, 1536]), ("q_scale", [2, 64]), ("k_scale", [2, 64]),
                    ("wo", [2, 512, D]), ("ln_g", [2, D]), ("ln_b", [2, D]),
                    ("w1", [2, D, 2 * HID]), ("b1", [2, 2 * HID]),
                    ("w2", [2, HID, D]), ("b2", [2, D])]:
        ap = nc.dram_tensor(nm, shp, F32, kind="ExternalInput").ap()
        t[nm] = [ap[0], ap[1]]
    t["out"] = nc.dram_tensor("out", [T, D], F32, kind="ExternalOutput").ap()
    dbg5 = os.environ.get("K_DBG5", "") == "1"
    for nm, shp, dt in [("x1_fm", [DC, 128, T], F32), ("xn_fm", [DC, 128, T], BF16),
                        ("xq_fm", [DC, 128, T], BF16), ("x_fm", [DC, 128, T], F32),
                        ("s_fm", [DC, 128, T], F32), ("hq_s", [FC, 128, T], BF16),
                        ("wqgt", [DC, 128, FFN], BF16), ("wqut", [DC, 128, FFN], BF16),
                        ("wqdt", [FC, 128, D], BF16),
                        ("xn1_f32", [DC, 128, T], F32), ("xn1_bf", [DC, 128, T], BF16),
                        ("q_fm", [DC, 128, T], BF16), ("k_fm", [DC, 128, T], BF16),
                        ("s1_fm", [DC, 128, T], F32),
                        ("u1_f32", [DC, 128, T], F32), ("u1_bf", [DC, 128, T], BF16),
                        ("u2_f32", [DC, 128, T], F32), ("u2_bf", [DC, 128, T], BF16),
                        ("u4_f32", [DC, 128, T], F32), ("u4_bf", [DC, 128, T], BF16),
                        ("h2_s", [22, 128, T], BF16)]:
        kind = "Internal"
        if dbg5 and nm in ("s1_fm", "u4_f32", "x1_fm", "q_fm", "k_fm",
                           "xn_fm", "xq_fm", "x_fm", "s_fm", "hq_s",
                           "wqgt", "wqut", "wqdt"):
            kind = "ExternalOutput"
        t[nm] = nc.dram_tensor(nm + "_s" if nm in ("x_fm", "s_fm") else nm,
                               shp, dt, kind=kind).ap()
    with contextlib.ExitStack() as stack:
        kb = KB(nc, stack)
        cx = Ctx(kb)
        cx.pspool = PsumPool(stack, nc)
        cx.cbuf = sbuf(stack, nc, "consts", [128, 16])
        tabs = {}
        for nm in ("ident_bf", "ident_f", "ones_row_f",
                   "ind0", "ind1", "ind2", "ind3", "b8_0", "b8_1", "b8_2", "b8_3"):
            tabs[nm] = load_table(stack, nc, cx, nm)
        ident_bf, ident_f, onesr = tabs["ident_bf"], tabs["ident_f"], tabs["ones_row_f"]
        emit_l0_weightquant(stack, nc, cx, t, ident_bf)
        cx.barrier()
        emit_l0_p1(stack, nc, cx, t, ident_bf, ident_f)
        cx.barrier()
        emit_l0_p2(stack, nc, cx, t)
        cx.barrier()
        emit_l0_p3(stack, nc, cx, t, ident_bf, ident_f)
        cx.barrier()
        emit_l1_p4(stack, nc, cx, t, onesr)
        cx.barrier()
        emit_l1_p5(stack, nc, cx, t, ident_bf, onesr)
        cx.barrier()
        emit_l1_p6(stack, nc, cx, t, 0, t["xn1_f32"], t["xn1_bf"],
                   t["u2_f32"], t["u2_bf"], ident_bf, ident_f, onesr, tabs)
        cx.barrier()
        emit_l1_p6(stack, nc, cx, t, 1, t["u2_f32"], t["u2_bf"],
                   t["u4_f32"], t["u4_bf"], ident_bf, ident_f, onesr, tabs)
        cx.barrier()
        emit_l1_p8(stack, nc, cx, t, ident_f)
        with nc.Block() as block:
            kb.finalize(block)
    return nc


_NC_CACHE = {}


def kernel(**inputs):
    if "full" not in _NC_CACHE:
        _NC_CACHE["full"] = build_full()
    ncf = _NC_CACHE["full"]
    x = np.ascontiguousarray(inputs["L0_attn_x"])
    ins = []
    for c in range(NCORES):
        ins.append(dict(
            x=np.ascontiguousarray(x[c]),
            W_in=inputs["L0_attn_W_in"], b_in=inputs["L0_attn_b_in"],
            A=inputs["L0_attn_A"], C=inputs["L0_attn_C"],
            W_out=inputs["L0_attn_W_out"], b_out=inputs["L0_attn_b_out"],
            w_gate=inputs["L0_ffn_w_gate"], w_up=inputs["L0_ffn_w_up"],
            w_down=inputs["L0_ffn_w_down"],
            Wq=inputs["L1_attn_Wq"], bq=inputs["L1_attn_bq"],
            Wk=inputs["L1_attn_Wk"], bk=inputs["L1_attn_bk"],
            Wv=inputs["L1_attn_Wv"], bv=inputs["L1_attn_bv"],
            Wo=inputs["L1_attn_Wo"], bo=inputs["L1_attn_bo"],
            wqkv=inputs["L1_ffn_wqkv"], q_scale=inputs["L1_ffn_q_scale"],
            k_scale=inputs["L1_ffn_k_scale"], wo=inputs["L1_ffn_wo"],
            ln_g=inputs["L1_ffn_ln_g"], ln_b=inputs["L1_ffn_ln_b"],
            w1=inputs["L1_ffn_w1"], b1=inputs["L1_ffn_b1"],
            w2=inputs["L1_ffn_w2"], b2=inputs["L1_ffn_b2"]))
    r = run_bass_kernel_spmd(ncf, ins, core_ids=list(range(NCORES)))
    return np.stack([r.results[c]["out"] for c in range(NCORES)], axis=0)


# version 17
# speedup vs baseline: 1.1662x; 1.1662x over previous
import contextlib
import os
import numpy as np

DBG_NODEFER = os.environ.get("K_NODEFER", "") == "1"
DBG_ALLHEAVY = os.environ.get("K_ALLHEAVY", "") == "1"
import concourse.bass as bass
import concourse.mybir as mybir
from concourse.bass_utils import run_bass_kernel_spmd

F32 = mybir.dt.float32
BF16 = mybir.dt.bfloat16
AF = mybir.ActivationFunctionType
ALU = mybir.AluOpType
AX = mybir.AxisListType

D = 1024
T = 4096
DC = D // 128
NCORES = 8
FFN = 2816
FC = FFN // 128
VELT_C = 4194305.0
ENGINES = ("sync", "vector", "scalar", "gpsimd", "tensor")
DMA_QUEUES = ("sync", "vector", "scalar", "gpsimd")


DMA_LANES = 4


class H:
    __slots__ = ("eng", "kind", "marked", "mark_count", "dma_count", "lane")

    def __init__(self, eng, kind):
        self.eng = eng
        self.kind = kind
        self.marked = False
        self.mark_count = None
        self.dma_count = None
        self.lane = None


class KB:
    def __init__(self, nc, stack):
        self.nc = nc
        self.esem = {e: stack.enter_context(nc.semaphore(f"es_{e}")) for e in ENGINES}
        self.dsem = {e: [stack.enter_context(nc.semaphore(f"ds_{e}{j}"))
                         for j in range(DMA_LANES)] for e in DMA_QUEUES}
        self.ops = {e: [] for e in ENGINES}
        self.ndma = {e: 0 for e in DMA_QUEUES}
        self.last_op = {e: None for e in ENGINES}
        self.last_dma = {e: None for e in DMA_QUEUES}

    def op(self, eng, fn, after=()):
        h = H(eng, "op")
        self.ops[eng].append((fn, h, tuple(after)))
        self.last_op[eng] = h
        return h

    def dma(self, eng, fn, after=()):
        h = H(eng, "dma")
        k = self.ndma[eng]
        self.ndma[eng] += 1
        h.lane = k % DMA_LANES
        h.dma_count = (k // DMA_LANES + 1) * 16
        self.ops[eng].append((fn, h, tuple(after)))
        self.last_op[eng] = h
        self.last_dma[eng] = h
        return h

    def frontier(self):
        out = []
        for e in ENGINES:
            if self.last_op[e] is not None:
                out.append(self.last_op[e])
        for q in DMA_QUEUES:
            if self.last_dma[q] is not None:
                out.append(self.last_dma[q])
        return out

    def finalize(self, block):
        for e in ENGINES:
            for (fn, h, deps) in self.ops[e]:
                for d in deps:
                    if d is None:
                        continue
                    if d.kind == "dma":
                        d.marked = True
                    elif not (e == "tensor" and d.eng == "tensor"):
                        d.marked = True
        cnt = {e: 0 for e in ENGINES}
        for e in ENGINES:
            for (fn, h, deps) in self.ops[e]:
                if h.kind == "op" and h.marked:
                    cnt[e] += 1
                    h.mark_count = cnt[e]

        for e in ENGINES:
            oplist = self.ops[e]
            if not oplist and e != "sync":
                continue
            esem, dsem = self.esem, self.dsem

            def body(eng, e=e, oplist=oplist):
                waited = {}
                for (fn, h, deps) in oplist:
                    for d in deps:
                        if d is None:
                            continue
                        if d.kind == "dma":
                            key = ("d", d.eng, d.lane)
                            tgt, sem = d.dma_count, dsem[d.eng][d.lane]
                        elif e == "tensor" and d.eng == "tensor":
                            continue
                        else:
                            key, tgt, sem = ("e", d.eng), d.mark_count, esem[d.eng]
                        if waited.get(key, 0) < tgt:
                            eng.wait_ge(sem, tgt)
                            waited[key] = tgt
                    if h.kind == "dma" and h.dma_count > 16:
                        key = ("d", e, h.lane)
                        tgt0 = h.dma_count - 16
                        if waited.get(key, 0) < tgt0:
                            eng.wait_ge(dsem[e][h.lane], tgt0)
                            waited[key] = tgt0
                    instr = fn(eng)
                    if h.kind == "dma":
                        instr.then_inc(dsem[e][h.lane], 16)
                    elif h.marked:
                        instr.then_inc(esem[e], 1)
                if e == "sync":
                    for q in DMA_QUEUES:
                        for j in range(DMA_LANES):
                            cnt = (self.ndma[q] - j + DMA_LANES - 1) // DMA_LANES
                            if cnt > 0:
                                eng.wait_ge(dsem[q][j], cnt * 16)
            getattr(block, e)(body)


class Buf:
    __slots__ = ("ap", "w", "r")

    def __init__(self, ap):
        self.ap = ap
        self.w = None
        self.r = []


class Ctx:
    def __init__(self, kb):
        self.kb = kb
        self.bdeps = []
        self.cbuf = None
        self.cmap = {}

    def const_col(self, val):
        if val not in self.cmap:
            i = len(self.cmap)
            col = Buf(self.cbuf.ap[:, i:i + 1])
            self.op("vector", lambda v, col=col, val=val: v.memset(col.ap, float(val)),
                    writes=[col])
            self.cmap[val] = col
        return self.cmap[val]

    def barrier(self):
        self.bdeps = self.kb.frontier()

    def _deps(self, reads, writes, after):
        deps = list(self.bdeps)
        for b in reads:
            if b.w is not None:
                deps.append(b.w)
        for b in writes:
            deps.extend(b.r)
            if b.w is not None:
                deps.append(b.w)
        deps.extend(after)
        return deps

    def _upd(self, h, reads, writes):
        for b in reads:
            b.r.append(h)
        for b in writes:
            b.w = h
            b.r = []
        return h

    def op(self, eng, fn, reads=(), writes=(), after=()):
        h = self.kb.op(eng, fn, self._deps(reads, writes, after))
        return self._upd(h, reads, writes)

    def dma(self, eng, out_ap, in_ap, reads=(), writes=(), after=(), **kw):
        def fn(e):
            return e.dma_start(out=out_ap, in_=in_ap, **kw)
        h = self.kb.dma(eng, fn, self._deps(reads, writes, after))
        return self._upd(h, reads, writes)


_SB_N = [0]


def sbuf(stack, nc, name, shape, dt=F32):
    _SB_N[0] += 1
    return Buf(stack.enter_context(nc.sbuf_tensor(f"{name}_{_SB_N[0]}", shape, dt))[:])


class PsumPool:

    def __init__(self, stack, nc):
        self.t = [stack.enter_context(nc.psum_tensor(f"ps{i}", [128, 2048], F32))
                  for i in range(2)]
        self.cur = {}
        for i in range(2):
            for q in range(4):
                self.cur[(i, q)] = Buf(self.t[i][:, q * 512:(q + 1) * 512])

    def take(self, i, qi, dt=F32, cols=512):
        old = self.cur[(i, qi)]
        base = self.t[i][:, qi * 512:(qi + 1) * 512]
        ap = base.bitcast(dt)[:, 0:cols] if dt != F32 else base[:, 0:cols]
        nb = Buf(ap)
        nb.w = old.w
        nb.r = list(old.r)
        self.cur[(i, qi)] = nb
        return nb

    def group(self, i):
        qs = [self.take(i, q) for q in range(4)]
        return self.t[i][:, :], qs

    def span(self, i, q0, nq, dt=F32):
        base = self.t[i][:, q0 * 512:(q0 + nq) * 512]
        ap = base.bitcast(dt) if dt != F32 else base
        nb = Buf(ap)
        deps_w = []
        deps_r = []
        for q in range(q0, q0 + nq):
            old = self.cur[(i, q)]
            if old.w is not None:
                deps_w.append(old.w)
            deps_r.extend(old.r)
            self.cur[(i, q)] = nb
        nb.w = deps_w[-1] if deps_w else None
        nb.r = deps_r + deps_w[:-1]
        return nb


def quant_scratch(stack, nc, W, tag):
    B = {nm: sbuf(stack, nc, f"q{nm}_{tag}", [128, W]) for nm in "abcdefg"}
    for nm in ("bm", "sc", "rs"):
        B[nm] = sbuf(stack, nc, f"q{nm}_{tag}", [128, W // 16])
    B["W"] = W
    return B


def emit_quant(cx, B, x, out, xoff=0, ooff=0, W=None):
    W = W or B["W"]
    Wb = W // 16
    x3 = x.ap[:, xoff:xoff + W].rearrange("p (b k) -> p b k", k=16)
    A, Bz, C, Dd, E, F, G = (B[n] for n in "abcdefg")

    def s3(b):
        return b.ap[:, 0:W].rearrange("p (b k) -> p b k", k=16)
    cx.op("vector", lambda v: v.reduce_max(B["bm"].ap[:, 0:Wb], x3, axis=AX.X,
                                           apply_absolute_value=True),
          reads=[x], writes=[B["bm"]])
    cx.op("vector", lambda v: v.tensor_scalar(B["sc"].ap[:, 0:Wb], B["bm"].ap[:, 0:Wb],
                                              1.0 / 6.0, 1e-12, ALU.mult, ALU.max),
          reads=[B["bm"]], writes=[B["sc"]])
    cx.op("vector", lambda v: v.reciprocal(B["rs"].ap[:, 0:Wb], B["sc"].ap[:, 0:Wb]),
          reads=[B["sc"]], writes=[B["rs"]])
    rs3 = B["rs"].ap[:, 0:Wb].unsqueeze(2).broadcast_to([128, Wb, 16])
    sc3 = B["sc"].ap[:, 0:Wb].unsqueeze(2).broadcast_to([128, Wb, 16])
    cx.op("vector", lambda v: v.tensor_tensor(s3(A), x3, rs3, ALU.mult),
          reads=[x, B["rs"]], writes=[A])
    cx.op("scalar", lambda s: s.activation(out=Bz.ap[:, 0:W], in_=A.ap[:, 0:W], func=AF.Abs),
          reads=[A], writes=[Bz])
    cx.op("scalar", lambda s: s.sign(out=A.ap[:, 0:W], in_=x.ap[:, xoff:xoff + W]),
          reads=[x], writes=[A])
    cx.op("vector", lambda v: v.tensor_tensor(s3(G), s3(A), sc3, ALU.mult),
          reads=[A, B["sc"]], writes=[G])
    c6 = cx.const_col(6.0)
    c5 = cx.const_col(5.0)
    cv = cx.const_col(VELT_C)
    cx.op("scalar", lambda s: s.activation(out=C.ap[:, 0:W], in_=Bz.ap[:, 0:W],
                                           func=AF.Relu, scale=-1.0, bias=c6.ap),
          reads=[Bz, c6], writes=[C])
    cx.op("scalar", lambda s: s.activation(out=Dd.ap[:, 0:W], in_=C.ap[:, 0:W],
                                           func=AF.Relu, scale=-1.0, bias=c5.ap),
          reads=[C, c5], writes=[Dd])
    cx.op("scalar", lambda s: s.activation(out=C.ap[:, 0:W], in_=Dd.ap[:, 0:W],
                                           func=AF.Relu, scale=VELT_C, bias=cv.ap),
          reads=[Dd, cv], writes=[C])
    cx.op("vector", lambda v: v.tensor_scalar(E.ap[:, 0:W], Bz.ap[:, 0:W],
                                              0.25, 0.5, ALU.is_gt, ALU.mult),
          reads=[Bz], writes=[E])
    cx.op("vector", lambda v: v.tensor_single_scalar(F.ap[:, 0:W], Bz.ap[:, 0:W],
                                                     0.75, ALU.is_gt),
          reads=[Bz], writes=[F])
    cx.op("vector", lambda v: v.scalar_tensor_tensor(Bz.ap[:, 0:W], Dd.ap[:, 0:W], 1.0,
                                                     C.ap[:, 0:W], ALU.add, ALU.subtract),
          reads=[Dd, C], writes=[Bz])
    cx.op("gpsimd", lambda g: g.tensor_add(Dd.ap[:, 0:W], C.ap[:, 0:W], Bz.ap[:, 0:W]),
          reads=[C, Bz], writes=[Dd])
    cx.op("gpsimd", lambda g: g.tensor_mul(C.ap[:, 0:W], Dd.ap[:, 0:W], F.ap[:, 0:W]),
          reads=[Dd, F], writes=[C])
    cx.op("vector", lambda v: v.tensor_max(Dd.ap[:, 0:W], C.ap[:, 0:W], E.ap[:, 0:W]),
          reads=[C, E], writes=[Dd])
    return cx.op("vector", lambda v: v.tensor_tensor(out.ap[:, ooff:ooff + W],
                                                     Dd.ap[:, 0:W], G.ap[:, 0:W], ALU.mult),
                 reads=[Dd, G], writes=[out])


def np_fp4(x):
    GRID = np.array([0.0, 0.5, 1.0, 1.5, 2.0, 3.0, 4.0, 6.0], np.float32)
    MID = np.array([0.25, 0.75, 1.25, 1.75, 2.5, 3.5, 5.0], np.float32)
    shape = x.shape
    xb = x.reshape(shape[:-1] + (shape[-1] // 16, 16))
    scale = np.maximum(np.abs(xb).max(-1, keepdims=True) / 6.0,
                       np.float32(1e-12)).astype(np.float32)
    xn = (xb / scale).astype(np.float32)
    idx = np.searchsorted(MID, np.abs(xn))
    q = GRID[idx] * np.sign(xn) * scale
    return q.reshape(shape).astype(np.float32)


def make_identity(cx, ident):
    cx.op("gpsimd", lambda g: g.memset(ident.ap, 0.0), writes=[ident])
    cx.op("gpsimd", lambda g: g.affine_select(out=ident.ap, in_=ident.ap,
                                              compare_op=ALU.not_equal, fill=1.0, base=0,
                                              pattern=[[-1, ident.ap.shape[1]]],
                                              channel_multiplier=1),
          reads=[ident], writes=[ident])


def load_cols(cx, dst, dst_col, src_ap, off, n=128, dq="sync"):
    return cx.dma(dq, dst.ap[0:n, dst_col], src_ap[off:off + n], writes=[dst])


def emit_l0_weightquant(stack, nc, cx, t, ident_bf):
    with contextlib.ExitStack() as ph:
        Bs = [quant_scratch(ph, nc, 1408, f"wq{i}") for i in range(2)]
        wrow = [sbuf(ph, nc, f"wqrow{i}", [128, FFN]) for i in range(2)]
        qouts = [sbuf(ph, nc, f"wqout{i}", [128, FFN], BF16) for i in range(2)]
        plane = sbuf(ph, nc, "wqplane", [128, 22528], BF16)
        jobs = [(t["w_gate"], FFN // 128, D, t["wqgt"], DC),
                (t["w_up"], FFN // 128, D, t["wqut"], DC),
                (t["w_down"], D // 128, FFN, t["wqdt"], FC)]
        tp = 0
        for (src, nr, W, dst, nfi) in jobs:
            for r in range(nr):
                rb = wrow[r % 2]
                qout = qouts[r % 2]
                cx.dma("sync", rb.ap[:, 0:W], src[r * 128:(r + 1) * 128, :], writes=[rb])
                if W <= 1408:
                    emit_quant(cx, Bs[r % 2], rb, qout, W=W)
                else:
                    emit_quant(cx, Bs[0], rb, qout, xoff=0, ooff=0, W=W // 2)
                    emit_quant(cx, Bs[1], rb, qout, xoff=W // 2, ooff=W // 2, W=W // 2)
                for fi in range(nfi):
                    pq = cx.pspool.take(1, tp % 2, BF16, 512)
                    tp += 1
                    cx.op("tensor",
                          lambda te, pq=pq, fi=fi, qout=qout: te.transpose(
                              pq.ap[:, 0:128], qout.ap[:, fi * 128:(fi + 1) * 128],
                              ident_bf.ap),
                          reads=[qout, ident_bf], writes=[pq])
                    cx.op("scalar",
                          lambda s, pq=pq, fi=fi, r=r, nr=nr: s.copy(
                              plane.ap[:, (fi * nr + r) * 128:(fi * nr + r + 1) * 128],
                              pq.ap[:, 0:128]),
                          reads=[pq], writes=[plane])
            for fi in range(nfi):
                cx.dma("sync", dst[fi, :, :],
                       plane.ap[:, fi * nr * 128:(fi + 1) * nr * 128], reads=[plane])


def emit_l0_p1(stack, nc, cx, t, ident_bf, ident_f):
    with contextlib.ExitStack() as ph:
        x_tm = [sbuf(ph, nc, f"p1x{i}", [128, 4096]) for i in range(2)]
        xn32 = [sbuf(ph, nc, f"p1n{i}", [128, 4096]) for i in range(2)]
        xnb = [sbuf(ph, nc, f"p1nb{i}", [128, 4096], BF16) for i in range(2)]
        xqb = [sbuf(ph, nc, f"p1qb{i}", [128, 4096], BF16) for i in range(2)]
        Bs1 = [quant_scratch(ph, nc, 1024, f"p1{i}") for i in range(2)]
        st = [sbuf(ph, nc, f"p1st{i}", [128, 4]) for i in range(2)]
        sq = [sbuf(ph, nc, f"p1sq{i}", [128, 4]) for i in range(2)]
        mvr = [sbuf(ph, nc, f"p1mv{i}", [128, 12]) for i in range(2)]
        junk = sbuf(ph, nc, "p1junk", [128, 1024], BF16)
        stg_b = [sbuf(ph, nc, f"p1sb{i}", [128, 512], BF16) for i in range(2)]
        stg_f = [sbuf(ph, nc, f"p1sf{i}", [128, 512]) for i in range(2)]
        tpb = 0
        tpf = 0
        for i in range(8):
            bi = i % 2
            x, xn, xnb_, xqb_ = x_tm[bi], xn32[bi], xnb[bi], xqb[bi]
            t0 = i * 512
            cx.dma("sync", x.ap.rearrange("p (c d) -> p c d", d=1024),
                   t["x_in"][t0:t0 + 512, :].rearrange("(c p) d -> p c d", p=128),
                   writes=[x])
            for c in range(4):
                cx.op("vector", lambda v, c=c, x=x, b=st[bi]: v.reduce_sum(
                    b.ap[:, c:c + 1], x.ap[:, c * 1024:(c + 1) * 1024], axis=AX.X),
                    reads=[x], writes=[st[bi]])
                cx.op("scalar", lambda s, c=c, x=x, b=sq[bi]: s.activation(
                    out=junk.ap, in_=x.ap[:, c * 1024:(c + 1) * 1024], func=AF.Square,
                    accum_out=b.ap[:, c:c + 1]),
                    reads=[x], writes=[sq[bi], junk])
            m = mvr[bi]
            cx.op("vector", lambda v, b=st[bi], m=m: v.tensor_scalar(
                m.ap[:, 0:4], b.ap[:, 0:4], 1.0 / D, None, ALU.mult),
                reads=[st[bi]], writes=[m])
            cx.op("vector", lambda v, b=sq[bi], m=m: v.tensor_scalar(
                m.ap[:, 4:8], b.ap[:, 0:4], 1.0 / D, None, ALU.mult),
                reads=[sq[bi]], writes=[m])
            cx.op("vector", lambda v, m=m: v.tensor_tensor(
                m.ap[:, 8:12], m.ap[:, 0:4], m.ap[:, 0:4], ALU.mult),
                reads=[m], writes=[m])
            cx.op("vector", lambda v, m=m: v.tensor_tensor(
                m.ap[:, 4:8], m.ap[:, 4:8], m.ap[:, 8:12], ALU.subtract),
                reads=[m], writes=[m])
            ce = cx.const_col(1e-5)
            cx.op("scalar", lambda s, m=m, ce=ce: s.activation(
                out=m.ap[:, 8:12], in_=m.ap[:, 4:8], func=AF.Ln, bias=ce.ap),
                reads=[m, ce], writes=[m])
            cx.op("scalar", lambda s, m=m: s.activation(
                out=m.ap[:, 8:12], in_=m.ap[:, 8:12], func=AF.Exp, scale=-0.5),
                reads=[m], writes=[m])
            if "dbg_st" in t:
                cx.dma("sync", t["dbg_st"][i, :, 0:4], st[bi].ap, reads=[st[bi]])
                cx.dma("sync", t["dbg_st"][i, :, 4:8], sq[bi].ap, reads=[sq[bi]])
                cx.dma("sync", t["dbg_st"][i, :, 8:20], m.ap, reads=[m])
            for c in range(4):
                cx.op("vector", lambda v, c=c, x=x, xn=xn, m=m: v.tensor_scalar(
                    xn.ap[:, c * 1024:(c + 1) * 1024], x.ap[:, c * 1024:(c + 1) * 1024],
                    m.ap[:, c:c + 1], m.ap[:, 8 + c:9 + c], ALU.subtract, ALU.mult),
                    reads=[x, m], writes=[xn])
            cx.op("vector", lambda v, xn=xn, xnb_=xnb_: v.tensor_copy(xnb_.ap, xn.ap),
                  reads=[xn], writes=[xnb_])
            for qq in range(4):
                emit_quant(cx, Bs1[qq % 2], xn, xqb_, xoff=qq * 1024,
                           ooff=qq * 1024, W=1024)
            for (srcb, dst, isf, dq) in ((x, t["x_fm"], True, "scalar"),
                                         (xnb_, t["xn_fm"], False, "sync"),
                                         (xqb_, t["xq_fm"], False, "gpsimd")):
                for fi in range(8):
                    if isf:
                        pq = cx.pspool.take(0, tpf % 4, F32, 512)
                        stg = stg_f[tpf % 2]
                        tpf += 1
                        idn = ident_f
                    else:
                        pq = cx.pspool.take(1, tpb % 4, BF16, 512)
                        stg = stg_b[tpb % 2]
                        tpb += 1
                        idn = ident_bf
                    for c in range(4):
                        cx.op("tensor", lambda te, pq=pq, c=c, fi=fi, srcb=srcb, idn=idn:
                              te.transpose(pq.ap[:, c * 128:(c + 1) * 128],
                                           srcb.ap[:, c * 1024 + fi * 128:
                                                   c * 1024 + (fi + 1) * 128],
                                           idn.ap),
                              reads=[srcb, idn], writes=[pq])
                    cx.op("scalar", lambda s, pq=pq, stg=stg: s.copy(stg.ap, pq.ap),
                          reads=[pq], writes=[stg])
                    cx.dma(dq, dst[fi, :, t0:t0 + 512], stg.ap, reads=[stg])


def emit_l0_p2(stack, nc, cx, t):
    with contextlib.ExitStack() as ph:
        xna = sbuf(ph, nc, "p2xn", [128, DC * 4096], BF16)
        xnb = [Buf(xna.ap[:, c * 4096:(c + 1) * 4096]) for c in range(DC)]
        w_in = sbuf(ph, nc, "p2win", [128, 16 * 1024], BF16)
        wo_st = [sbuf(ph, nc, f"p2wo{i}", [128, 1024], BF16) for i in range(2)]
        dcol = sbuf(ph, nc, "p2d", [128, 8])
        ccol = sbuf(ph, nc, "p2c", [128, 8])
        acol = sbuf(ph, nc, "p2a", [128, 8])
        bi_c = sbuf(ph, nc, "p2bi", [128, 16])
        bo_c = sbuf(ph, nc, "p2bo", [128, 8])
        hst = sbuf(ph, nc, "p2h", [128, 8])
        hstv = [Buf(hst.ap[:, c:c + 1]) for c in range(8)]
        v_sb = sbuf(ph, nc, "p2v", [128, 2048])
        y_sb = sbuf(ph, nc, "p2y", [128, 2048])
        g_sb = [sbuf(ph, nc, f"p2g{i}", [128, 2048]) for i in range(2)]
        sm = [sbuf(ph, nc, f"p2sm{c}", [128, 2048], BF16) for c in range(DC)]
        s_stg = [sbuf(ph, nc, f"p2ss{i}", [128, 2048]) for i in range(2)]

        xn3 = xna.ap.rearrange("p (c t) -> p c t", t=4096)
        cx.dma("sync", xn3[:, 0:4, :],
               t["xn_fm"][0:4].rearrange("c p t -> p c t"), writes=xnb[0:4] + [xna])
        cx.dma("scalar", xn3[:, 4:8, :],
               t["xn_fm"][4:8].rearrange("c p t -> p c t"), writes=xnb[4:8])
        w_in_v = [Buf(w_in.ap[:, f * 1024:(f + 1) * 1024]) for f in range(16)]
        for f in range(16):
            cx.dma("gpsimd",
                   w_in.ap[:, f * 1024:(f + 1) * 1024].rearrange("p (k f2) -> p k f2", f2=128),
                   t["w_in"][:, f * 128:(f + 1) * 128].rearrange("(k p) f -> p k f", p=128),
                   writes=[w_in_v[f]])
        for c in range(8):
            load_cols(cx, acol, c, t["a_in"], c * 128)
            load_cols(cx, ccol, c, t["c_in"], c * 128)
            load_cols(cx, bo_c, c, t["b_out"], c * 128)
        for c in range(16):
            load_cols(cx, bi_c, c, t["b_in"], c * 128)
        cx.op("scalar", lambda s: s.activation(out=dcol.ap, in_=acol.ap, func=AF.Sigmoid),
              reads=[acol], writes=[dcol])
        cx.op("vector", lambda v: v.memset(hst.ap, 0.0), writes=hstv)

        for half in range(2):
            h0 = half * 2048
            for c in range(8):
                gb = g_sb[c % 2]
                grp = {}
                for (pi, f) in ((0, c), (1, 8 + c)):
                    psap, qs = cx.pspool.group(pi)
                    grp[pi] = (psap, qs)
                    for k in range(8):
                        for tt in range(4):
                            cx.op("tensor",
                                  lambda te, f=f, k=k, tt=tt, qs=qs, h0=h0:
                                  te.matmul(qs[tt].ap,
                                            w_in.ap[:, f * 1024 + k * 128:
                                                    f * 1024 + (k + 1) * 128],
                                            xnb[k].ap[:, h0 + tt * 512:h0 + (tt + 1) * 512],
                                            start=(k == 0), stop=(k == 7)),
                                  reads=[w_in_v[f], xnb[k]], writes=[qs[tt]])
                ps0ap, ps0q = grp[0]
                ps1ap, ps1q = grp[1]
                cx.op("vector", lambda v, c=c, a=ps0ap: v.tensor_scalar(
                    v_sb.ap, a, bi_c.ap[:, c:c + 1], None, ALU.add),
                    reads=ps0q + [bi_c], writes=[v_sb])
                cx.op("vector", lambda v, c=c: v.tensor_tensor_scan(
                    y_sb.ap, dcol.ap[:, c:c + 1].broadcast_to([128, 2048]), v_sb.ap,
                    hstv[c].ap, ALU.mult, ALU.add),
                    reads=[v_sb, dcol, hstv[c]], writes=[y_sb])
                cx.op("vector", lambda v, c=c: v.tensor_copy(
                    hstv[c].ap, y_sb.ap[:, 2047:2048]),
                    reads=[y_sb], writes=[hstv[c]])
                cx.op("scalar", lambda s, gb=gb, c=c, a=ps1ap: s.activation(
                    out=gb.ap, in_=a, func=AF.Sigmoid, bias=bi_c.ap[:, 8 + c:9 + c]),
                    reads=ps1q + [bi_c], writes=[gb])
                cx.op("vector", lambda v, gb=gb, c=c: v.scalar_tensor_tensor(
                    sm[c].ap, y_sb.ap, ccol.ap[:, c:c + 1], gb.ap, ALU.mult, ALU.mult),
                    reads=[y_sb, gb, ccol], writes=[sm[c]])
                if "dbg_p2" in t and half == 0 and c == 0:
                    cx.dma("sync", t["dbg_p2"][0, :, :], v_sb.ap, reads=[v_sb])
                    cx.dma("sync", t["dbg_p2"][1, :, :], y_sb.ap, reads=[y_sb])
                    cx.dma("sync", t["dbg_p2"][2, :, :], gb.ap, reads=[gb])
                    cx.dma("gpsimd", t["dbg_p2"][3, :, :], sm[c].ap, reads=[sm[c]])
            for f in range(8):
                wo = wo_st[f % 2]
                cx.dma("gpsimd",
                       wo.ap.rearrange("p (k f2) -> p k f2", f2=128),
                       t["w_out"][:, f * 128:(f + 1) * 128].rearrange("(k p) f -> p k f", p=128),
                       writes=[wo])
                psap, qs = cx.pspool.group(f % 2)
                for k in range(8):
                    for tt in range(4):
                        cx.op("tensor",
                              lambda te, wo=wo, k=k, tt=tt, qs=qs:
                              te.matmul(qs[tt].ap, wo.ap[:, k * 128:(k + 1) * 128],
                                        sm[k].ap[:, tt * 512:(tt + 1) * 512],
                                        start=(k == 0), stop=(k == 7)),
                              reads=[wo, sm[k]], writes=[qs[tt]])
                stg = s_stg[f % 2]
                cx.op("vector", lambda v, stg=stg, f=f, a=psap, qs=qs: v.tensor_scalar(
                    stg.ap, a, bo_c.ap[:, f:f + 1], None, ALU.add),
                    reads=qs + [bo_c], writes=[stg])
                cx.dma("sync", t["s_fm"][f, :, h0:h0 + 2048], stg.ap, reads=[stg])


def emit_l0_p3(stack, nc, cx, t, ident_bf, ident_f):
    with contextlib.ExitStack() as ph:
        xqa = sbuf(ph, nc, "p3xq", [128, DC * 2048], BF16)
        xqb = [Buf(xqa.ap[:, c * 2048:(c + 1) * 2048]) for c in range(DC)]
        wg_st = [sbuf(ph, nc, f"p3wg{i}", [128, 1024], BF16) for i in range(2)]
        wu_st = [sbuf(ph, nc, f"p3wu{i}", [128, 1024], BF16) for i in range(2)]
        wd_st = [sbuf(ph, nc, f"p3wd{i}", [128, FC * 128], BF16) for i in range(2)]
        sg_sb2 = [sbuf(ph, nc, "p3sg", [128, 2048])] * 2
        h32_2 = [sbuf(ph, nc, "p3h", [128, 2048])] * 2
        htm_2 = [sbuf(ph, nc, f"p3htm{i}", [128, 2048]) for i in range(2)]
        hqtm_2 = [sbuf(ph, nc, f"p3hqtm{i}", [128, 2048], BF16) for i in range(2)]
        hq_ld = [sbuf(ph, nc, f"p3hl{i}", [128, 2048], BF16) for i in range(2)]
        Bs3 = [quant_scratch(ph, nc, 1024, f"p3{i}") for i in range(2)]
        sil = [sbuf(ph, nc, f"p3sil{i}", [128, 2048]) for i in range(2)]
        s_ld = [sbuf(ph, nc, "p3sl", [128, 2048])] * 2
        x_ld = [sbuf(ph, nc, "p3xl", [128, 2048])] * 2
        x1st = [sbuf(ph, nc, "p3x1", [128, 2048])] * 2

        for half in range(2):
            h0 = half * 2048
            cx.dma("sync", xqa.ap.rearrange("p (c t) -> p c t", t=2048),
                   t["xq_fm"][:, :, h0:h0 + 2048].rearrange("c p t -> p c t"),
                   writes=xqb + [xqa])
            for f in range(FC):
                wg, wu = wg_st[f % 2], wu_st[f % 2]
                sg_sb, h32 = sg_sb2[f % 2], h32_2[f % 2]
                htm, hqtm = htm_2[f % 2], hqtm_2[f % 2]
                cx.dma("gpsimd",
                       wg.ap.rearrange("p (k f2) -> p k f2", f2=128),
                       t["wqgt"][:, :, f * 128:(f + 1) * 128].rearrange("k p f -> p k f"),
                       writes=[wg])
                cx.dma("gpsimd",
                       wu.ap.rearrange("p (k f2) -> p k f2", f2=128),
                       t["wqut"][:, :, f * 128:(f + 1) * 128].rearrange("k p f -> p k f"),
                       writes=[wu])
                grp = {}
                for (pi, wb) in ((0, wg), (1, wu)):
                    psap, qs = cx.pspool.group(pi)
                    grp[pi] = (psap, qs)
                    for k in range(8):
                        for tt in range(4):
                            cx.op("tensor",
                                  lambda te, wb=wb, k=k, tt=tt, qs=qs:
                                  te.matmul(qs[tt].ap, wb.ap[:, k * 128:(k + 1) * 128],
                                            xqb[k].ap[:, tt * 512:(tt + 1) * 512],
                                            start=(k == 0), stop=(k == 7)),
                                  reads=[wb, xqb[k]], writes=[qs[tt]])
                ps0ap, ps0q = grp[0]
                ps1ap, ps1q = grp[1]
                cx.op("scalar", lambda s, a=ps0ap, sg_sb=sg_sb: s.activation(
                    out=sg_sb.ap, in_=a, func=AF.Silu),
                    reads=ps0q, writes=[sg_sb])
                cx.op("vector", lambda v, a=ps1ap, sg_sb=sg_sb, h32=h32:
                      v.tensor_tensor(h32.ap, sg_sb.ap, a, ALU.mult),
                      reads=[sg_sb] + ps1q, writes=[h32])
                for q4 in range(4):
                    pq = cx.pspool.take(0, q4 % 2, F32, 512)
                    for j in range(4):
                        cx.op("tensor", lambda te, pq=pq, q4=q4, j=j, h32=h32:
                              te.transpose(
                                  pq.ap[:, j * 128:(j + 1) * 128],
                                  h32.ap[:, (q4 * 4 + j) * 128:(q4 * 4 + j + 1) * 128],
                                  ident_f.ap),
                              reads=[h32, ident_f], writes=[pq])
                    cx.op("scalar", lambda s, pq=pq, q4=q4, htm=htm: s.copy(
                        htm.ap[:, q4 * 512:(q4 + 1) * 512], pq.ap),
                        reads=[pq], writes=[htm])
                emit_quant(cx, Bs3[0], htm, hqtm, 0, 0)
                emit_quant(cx, Bs3[1], htm, hqtm, 1024, 1024)
                hst = hq_ld[f % 2]
                for q4 in range(4):
                    pq = cx.pspool.take(0, 2 + q4 % 2, BF16, 512)
                    for j in range(4):
                        cx.op("tensor", lambda te, pq=pq, q4=q4, j=j, hqtm=hqtm:
                              te.transpose(
                                  pq.ap[:, j * 128:(j + 1) * 128],
                                  hqtm.ap[:, (q4 * 4 + j) * 128:(q4 * 4 + j + 1) * 128],
                                  ident_bf.ap),
                              reads=[hqtm, ident_bf], writes=[pq])
                    cx.op("scalar", lambda s, pq=pq, q4=q4, hst=hst: s.copy(
                        hst.ap[:, q4 * 512:(q4 + 1) * 512], pq.ap),
                        reads=[pq], writes=[hst])
                cx.dma("sync", t["hq_s"][f, :, h0:h0 + 2048], hst.ap,
                       reads=[hst])
            cx.barrier()
            for fop in range(4):
                grp = {}
                for sub in range(2):
                    psap, qs = cx.pspool.group(sub)
                    grp[sub] = (psap, qs)
                wds = []
                for sub in range(2):
                    fo = fop * 2 + sub
                    wd = wd_st[sub]
                    cx.dma("gpsimd",
                           wd.ap.rearrange("p (k f2) -> p k f2", f2=128),
                           t["wqdt"][:, :, fo * 128:(fo + 1) * 128].rearrange("k p f -> p k f"),
                           writes=[wd])
                    wds.append(wd)
                for k in range(FC):
                    hl = hq_ld[k % 2]
                    cx.dma("sync", hl.ap, t["hq_s"][k, :, h0:h0 + 2048], writes=[hl])
                    for sub in range(2):
                        psap, qs = grp[sub]
                        for tt in range(4):
                            cx.op("tensor",
                                  lambda te, wd=wds[sub], k=k, tt=tt, qs=qs, hl=hl:
                                  te.matmul(qs[tt].ap, wd.ap[:, k * 128:(k + 1) * 128],
                                            hl.ap[:, tt * 512:(tt + 1) * 512],
                                            start=(k == 0), stop=(k == FC - 1)),
                                  reads=[wds[sub], hl], writes=[qs[tt]])
                for sub in range(2):
                    fo = fop * 2 + sub
                    psap, qs = grp[sub]
                    sl, xl, x1, si = s_ld[sub], x_ld[sub], x1st[sub], sil[sub]
                    cx.dma("sync", sl.ap, t["s_fm"][fo, :, h0:h0 + 2048], writes=[sl])
                    cx.dma("sync", xl.ap, t["x_fm"][fo, :, h0:h0 + 2048], writes=[xl])
                    cx.op("scalar", lambda s, a=psap, si=si: s.activation(
                        out=si.ap, in_=a, func=AF.Silu),
                        reads=qs, writes=[si])
                    cx.op("vector", lambda v, sl=sl, si=si, x1=x1: v.tensor_tensor(
                        x1.ap, si.ap, sl.ap, ALU.mult),
                        reads=[si, sl], writes=[x1])
                    cx.op("vector", lambda v, xl=xl, x1=x1: v.tensor_tensor(
                        x1.ap, x1.ap, xl.ap, ALU.add),
                        reads=[x1, xl], writes=[x1])
                    cx.dma("sync", t["x1_fm"][fo, :, h0:h0 + 2048], x1.ap, reads=[x1])


def build_l0(debug=False):
    nc = bass.Bass("TRN2", target_bir_lowering=False)
    t = {}
    t["x_in"] = nc.dram_tensor("x", [T, D], F32, kind="ExternalInput").ap()
    t["w_in"] = nc.dram_tensor("W_in", [D, 2 * D], F32, kind="ExternalInput").ap()
    t["b_in"] = nc.dram_tensor("b_in", [2 * D], F32, kind="ExternalInput").ap()
    t["a_in"] = nc.dram_tensor("A", [D], F32, kind="ExternalInput").ap()
    t["c_in"] = nc.dram_tensor("C", [D], F32, kind="ExternalInput").ap()
    t["w_out"] = nc.dram_tensor("W_out", [D, D], F32, kind="ExternalInput").ap()
    t["b_out"] = nc.dram_tensor("b_out", [D], F32, kind="ExternalInput").ap()
    t["w_gate"] = nc.dram_tensor("w_gate", [FFN, D], F32, kind="ExternalInput").ap()
    t["w_up"] = nc.dram_tensor("w_up", [FFN, D], F32, kind="ExternalInput").ap()
    t["w_down"] = nc.dram_tensor("w_down", [D, FFN], F32, kind="ExternalInput").ap()
    t["x1_fm"] = nc.dram_tensor("x1_fm", [DC, 128, T], F32, kind="ExternalOutput").ap()
    sk = "ExternalOutput" if debug else "Internal"
    t["xn_fm"] = nc.dram_tensor("xn_fm", [DC, 128, T], BF16, kind=sk).ap()
    t["xq_fm"] = nc.dram_tensor("xq_fm", [DC, 128, T], BF16, kind=sk).ap()
    t["x_fm"] = nc.dram_tensor("x_fm_s", [DC, 128, T], F32, kind=sk).ap()
    t["s_fm"] = nc.dram_tensor("s_fm_s", [DC, 128, T], F32, kind=sk).ap()
    t["hq_s"] = nc.dram_tensor("hq_s", [FC, 128, T], BF16, kind="Internal").ap()
    t["wqgt"] = nc.dram_tensor("wqgt", [DC, 128, FFN], BF16, kind=sk).ap()
    t["wqut"] = nc.dram_tensor("wqut", [DC, 128, FFN], BF16, kind=sk).ap()
    t["wqdt"] = nc.dram_tensor("wqdt", [FC, 128, D], BF16, kind=sk).ap()
    if debug:
        t["dbg_st"] = nc.dram_tensor("dbg_st", [8, 128, 20], F32, kind="ExternalOutput").ap()
        t["dbg_p2"] = nc.dram_tensor("dbg_p2", [4, 128, 2048], F32, kind="ExternalOutput").ap()

    with contextlib.ExitStack() as stack:
        kb = KB(nc, stack)
        cx = Ctx(kb)
        cx.pspool = PsumPool(stack, nc)
        cx.cbuf = sbuf(stack, nc, "consts", [128, 16])
        ident_bf = sbuf(stack, nc, "ident_bf", [128, 128], BF16)
        ident_f = sbuf(stack, nc, "ident_f", [128, 128], F32)
        make_identity(cx, ident_bf)
        make_identity(cx, ident_f)
        emit_l0_weightquant(stack, nc, cx, t, ident_bf)
        cx.barrier()
        emit_l0_p1(stack, nc, cx, t, ident_bf, ident_f)
        cx.barrier()
        emit_l0_p2(stack, nc, cx, t)
        cx.barrier()
        emit_l0_p3(stack, nc, cx, t, ident_bf, ident_f)
        with nc.Block() as block:
            kb.finalize(block)
    return nc


def _bf(x):
    import ml_dtypes
    return np.asarray(x, dtype=ml_dtypes.bfloat16)


def make_tables():
    tb = {}
    tb["ident_f"] = np.eye(128, dtype=np.float32)
    tb["ident_bf"] = _bf(np.eye(128, dtype=np.float32))
    qpos = np.arange(T, dtype=np.float64)
    qhi = 16.0 * np.rint(qpos / 16.0)
    qlo = qpos - qhi
    tb["qsplit"] = _bf(np.stack([qhi, qhi, qlo, qlo]))
    slopes = (2.0 ** (-8.0 / 16.0)) ** np.arange(1, 17)
    ms = np.zeros((4, 16 * 128), np.float32)
    for h in range(16):
        S = np.float32(8.0 * slopes[h])
        shi = np.float32(_bf(S))
        slo = np.float32(S) - shi
        ms[0, h * 128:(h + 1) * 128] = -shi
        ms[1, h * 128:(h + 1) * 128] = -slo
        ms[2, h * 128:(h + 1) * 128] = -shi
        ms[3, h * 128:(h + 1) * 128] = -slo
    tb["mslope"] = _bf(ms)
    kb = np.zeros((128, 16 * 32), np.float32)
    p = np.arange(128)
    for h in range(16):
        for c in range(32):
            kb[:, h * 32 + c] = slopes[h] * (c * 128 + p)
    tb["kbias"] = kb.astype(np.float32)
    kb2 = np.zeros((128, 12 * 32), np.float32)
    for h in range(4, 16):
        for idx in range(32):
            kb2[:, (h - 4) * 32 + idx] = slopes[h] * ((idx - 28) * 128 + p - 255)
    tb["kbias2"] = kb2.astype(np.float32)
    tb["ones_col_bf"] = _bf(np.ones((128, 1)))
    tb["ones_row_f"] = np.ones((1, 128), np.float32)
    DIM_HEAD, WINDOW = 64, 512
    half_idx = np.arange(0, DIM_HEAD, 2, dtype=np.float32)
    inv_freq = (1.0 / 10000.0 ** (half_idx / DIM_HEAD)).astype(np.float32)
    tpos = np.arange(T, dtype=np.float32)
    freqs = tpos[:, None] * inv_freq[None, :]
    freqs = np.concatenate([freqs, freqs], axis=-1)
    base = ((half_idx + 0.4 * DIM_HEAD) / (1.4 * DIM_HEAD)).astype(np.float32)
    power = ((tpos - T // 2) / float(WINDOW // 2)).astype(np.float32)
    scale = base[None, :] ** power[:, None]
    scale = np.concatenate([scale, scale], axis=-1)
    cos, sin = np.cos(freqs), np.sin(freqs)
    sgn = np.concatenate([-np.ones(32), np.ones(32)])[None, :]
    csq = (cos * scale).T
    snq = (sin * scale * sgn).T
    csk = (cos / scale).T
    snk = (sin / scale * sgn).T
    for nm, v in (("csq", csq), ("snq", snq), ("csk", csk), ("snk", snk)):
        tb[nm] = np.concatenate([v, v], axis=0).astype(np.float32)
    jb = np.arange(640)[None, :]
    pp = p[:, None]
    m1 = np.where((jb >= pp) & (jb <= pp + 512), 0.0, -1e7).astype(np.float32)
    masks = [m1]
    for qs in range(4):
        masks.append(np.where((jb >= pp) & (jb <= pp + 512) &
                              (jb >= 512 - qs * 128), 0.0, -1e7).astype(np.float32))
    tb["lmask"] = np.stack(masks)
    for c in range(4):
        ind = np.zeros((128, 8), np.float32)
        ind[0:64, 2 * c] = 1.0
        ind[64:128, 2 * c + 1] = 1.0
        tb[f"ind{c}"] = _bf(ind)
        b8 = np.zeros((8, 128), np.float32)
        b8[2 * c, 0:64] = 1.0
        b8[2 * c + 1, 64:128] = 1.0
        tb[f"b8_{c}"] = b8.astype(np.float32)
    return tb


_TABLES = make_tables()


_TAB_N = [0]


def load_table(stack, nc, cx, name, dq="sync"):
    arr = _TABLES[name]
    _TAB_N[0] += 1
    un = f"{name}_{_TAB_N[0]}"
    dt = BF16 if arr.dtype.str.endswith("bfloat16") or arr.dtype.itemsize == 2 else F32
    h = nc.inline_tensor(np.ascontiguousarray(arr), f"tab_{un}").ap()
    b = sbuf(stack, nc, f"tb_{un}", list(arr.shape[-2:]) if arr.ndim == 2 else
             [arr.shape[1], arr.shape[0] * arr.shape[2]], dt)
    if arr.ndim == 2:
        cx.dma(dq, b.ap, h[:, :], writes=[b])
    else:
        for i in range(arr.shape[0]):
            cx.dma(dq, b.ap[:, i * arr.shape[2]:(i + 1) * arr.shape[2]], h[i, :, :],
                   writes=[b])
    return b


def emit_l1_p4(stack, nc, cx, t, onesr):
    with contextlib.ExitStack() as ph:
        x1 = [sbuf(ph, nc, f"p4x{i}", [128, 4096]) for i in range(2)]
        x1b = [sbuf(ph, nc, f"p4b{i}", [128, 4096], BF16) for i in range(2)]
        sqb = sbuf(ph, nc, "p4sq", [128, 4096], BF16)
        rows = [sbuf(ph, nc, f"p4r{i}", [1, 2048]) for i in range(2)]
        xn1 = [sbuf(ph, nc, f"p4n{i}", [128, 4096]) for i in range(2)]
        xn1b = [sbuf(ph, nc, f"p4nb{i}", [128, 4096], BF16) for i in range(2)]
        onescol = load_table(ph, nc, cx, "ones_col_bf")
        ce = cx.const_col(1e-5)
        for i in range(8):
            bi = i % 2
            x, xb, xn, xnb_, rw = x1[bi], x1b[bi], xn1[bi], xn1b[bi], rows[bi]
            t0 = i * 512
            cx.dma("sync", x.ap.rearrange("p (c t) -> p c t", t=512),
                   t["x1_fm"][:, :, t0:t0 + 512].rearrange("c p t -> p c t"),
                   writes=[x])
            cx.op("vector", lambda v, x=x, xb=xb: v.tensor_copy(xb.ap, x.ap),
                  reads=[x], writes=[xb])
            cx.op("scalar", lambda s, x=x: s.activation(out=sqb.ap, in_=x.ap,
                                                        func=AF.Square),
                  reads=[x], writes=[sqb])
            psap, qs = cx.pspool.group(0)
            for c in range(8):
                cx.op("tensor", lambda te, c=c, xb=xb, qs=qs: te.matmul(
                    qs[0].ap[0:1, 0:512], onescol.ap, xb.ap[:, c * 512:(c + 1) * 512],
                    start=(c == 0), stop=(c == 7)),
                    reads=[xb, onescol], writes=[qs[0]])
            for c in range(8):
                cx.op("tensor", lambda te, c=c, qs=qs: te.matmul(
                    qs[1].ap[0:1, 0:512], onescol.ap, sqb.ap[:, c * 512:(c + 1) * 512],
                    start=(c == 0), stop=(c == 7)),
                    reads=[sqb, onescol], writes=[qs[1]])
            cx.op("vector", lambda v, rw=rw, qs=qs: v.tensor_scalar(
                rw.ap[0:1, 0:512], qs[0].ap[0:1, 0:512], 1.0 / D, None, ALU.mult),
                reads=[qs[0]], writes=[rw])
            cx.op("vector", lambda v, rw=rw, qs=qs: v.tensor_scalar(
                rw.ap[0:1, 512:1024], qs[1].ap[0:1, 0:512], 1.0 / D, None, ALU.mult),
                reads=[qs[1]], writes=[rw])
            cx.op("vector", lambda v, rw=rw: v.tensor_tensor(
                rw.ap[0:1, 1024:1536], rw.ap[0:1, 0:512], rw.ap[0:1, 0:512], ALU.mult),
                reads=[rw], writes=[rw])
            cx.op("vector", lambda v, rw=rw: v.tensor_tensor(
                rw.ap[0:1, 512:1024], rw.ap[0:1, 512:1024], rw.ap[0:1, 1024:1536],
                ALU.subtract), reads=[rw], writes=[rw])
            cx.op("scalar", lambda s, rw=rw: s.activation(
                out=rw.ap[0:1, 1024:1536], in_=rw.ap[0:1, 512:1024], func=AF.Ln,
                bias=ce.ap[0:1, 0:1]), reads=[rw, ce], writes=[rw])
            cx.op("scalar", lambda s, rw=rw: s.activation(
                out=rw.ap[0:1, 1024:1536], in_=rw.ap[0:1, 1024:1536], func=AF.Exp,
                scale=-0.5), reads=[rw], writes=[rw])
            cx.op("vector", lambda v, rw=rw: v.tensor_tensor(
                rw.ap[0:1, 1536:2048], rw.ap[0:1, 0:512], rw.ap[0:1, 1024:1536],
                ALU.mult), reads=[rw], writes=[rw])
            pr = cx.pspool.take(1, (i % 2) * 2)
            pm = cx.pspool.take(1, (i % 2) * 2 + 1)
            cx.op("tensor", lambda te, rw=rw, pr=pr: te.matmul(
                pr.ap, onesr.ap, rw.ap[0:1, 1024:1536], start=True, stop=True),
                reads=[rw, onesr], writes=[pr])
            cx.op("tensor", lambda te, rw=rw, pm=pm: te.matmul(
                pm.ap, onesr.ap, rw.ap[0:1, 1536:2048], start=True, stop=True),
                reads=[rw, onesr], writes=[pm])
            for c in range(8):
                cx.op("vector", lambda v, c=c, x=x, xn=xn, pr=pr: v.tensor_tensor(
                    xn.ap[:, c * 512:(c + 1) * 512], x.ap[:, c * 512:(c + 1) * 512],
                    pr.ap, ALU.mult), reads=[x, pr], writes=[xn])
                cx.op("vector", lambda v, c=c, xn=xn, pm=pm: v.tensor_tensor(
                    xn.ap[:, c * 512:(c + 1) * 512], xn.ap[:, c * 512:(c + 1) * 512],
                    pm.ap, ALU.subtract), reads=[xn, pm], writes=[xn])
            cx.op("vector", lambda v, xn=xn, xnb_=xnb_: v.tensor_copy(xnb_.ap, xn.ap),
                  reads=[xn], writes=[xnb_])
            cx.dma("sync", t["xn1_f32"][:, :, t0:t0 + 512].rearrange("c p t -> p c t"),
                   xn.ap.rearrange("p (c t) -> p c t", t=512), reads=[xn])
            cx.dma("sync", t["xn1_bf"][:, :, t0:t0 + 512].rearrange("c p t -> p c t"),
                   xnb_.ap.rearrange("p (c t) -> p c t", t=512), reads=[xnb_])


def emit_l1_p5(stack, nc, cx, t, ident_bf, onesr):
    with contextlib.ExitStack() as ph:
        v_sb = [sbuf(ph, nc, f"p5v{tc}", [128, 1040], BF16) for tc in range(32)]
        with contextlib.ExitStack() as pa:
            xna = sbuf(pa, nc, "p5xn", [128, DC * 4096], BF16)
            xnb = [Buf(xna.ap[:, c * 4096:(c + 1) * 4096]) for c in range(DC)]
            wst = [sbuf(pa, nc, f"p5w{i}", [128, 1024], BF16) for i in range(2)]
            bcol = sbuf(pa, nc, "p5bc", [128, 24])
            stg = [sbuf(pa, nc, f"p5st{i}", [128, 2048], BF16) for i in range(2)]
            vstg = sbuf(pa, nc, "p5vs", [128, 128], BF16)
            cx.dma("sync", xna.ap.rearrange("p (c t) -> p c t", t=4096),
                   t["xn1_bf"].rearrange("c p t -> p c t"), writes=xnb + [xna])
            for ci in range(8):
                load_cols(cx, bcol, ci, t["bq"], ci * 128)
                load_cols(cx, bcol, 8 + ci, t["bk"], ci * 128)
                load_cols(cx, bcol, 16 + ci, t["bv"], ci * 128)
            jobs = [(t["Wq"], 0, "q"), (t["Wk"], 8, "k"), (t["Wv"], 16, "v")]
            for (wsrc, boff, kind) in jobs:
                for f in range(8):
                    ws = wst[f % 2]
                    cx.dma("gpsimd",
                           ws.ap.rearrange("p (k f2) -> p k f2", f2=128),
                           wsrc[:, f * 128:(f + 1) * 128].rearrange(
                               "(k p) f -> p k f", p=128),
                           writes=[ws])
                    grp = []
                    for half in range(2):
                        psap, qs = cx.pspool.group(half)
                        grp.append((psap, qs))
                        for k in range(8):
                            for tt in range(4):
                                cx.op("tensor",
                                      lambda te, ws=ws, k=k, tt=tt, qs=qs, half=half:
                                      te.matmul(qs[tt].ap,
                                                ws.ap[:, k * 128:(k + 1) * 128],
                                                xnb[k].ap[:, half * 2048 + tt * 512:
                                                          half * 2048 + (tt + 1) * 512],
                                                start=(k == 0), stop=(k == 7)),
                                      reads=[ws, xnb[k]], writes=[qs[tt]])
                    for half in range(2):
                        psap, qs = grp[half]
                        sg = stg[half]
                        cx.op("vector", lambda v, sg=sg, a=psap, f=f, boff=boff, qs=qs:
                              v.tensor_scalar(sg.ap, a, bcol.ap[:, boff + f:boff + f + 1],
                                              None, ALU.add),
                              reads=qs + [bcol], writes=[sg])
                        if kind == "q":
                            cx.dma("sync", t["q_fm"][f, :, half * 2048:(half + 1) * 2048],
                                   sg.ap, reads=[sg])
                        elif kind == "k":
                            cx.dma("sync", t["k_fm"][f, :, half * 2048:(half + 1) * 2048],
                                   sg.ap, reads=[sg])
                        else:
                            for j in range(16):
                                tc = half * 16 + j
                                pv = cx.pspool.take(half, j % 4, BF16, 512)
                                cx.op("tensor", lambda te, pv=pv, sg=sg, j=j:
                                      te.transpose(pv.ap[:, 0:128],
                                                   sg.ap[:, j * 128:(j + 1) * 128],
                                                   ident_bf.ap),
                                      reads=[sg, ident_bf], writes=[pv])
                                cx.op("scalar", lambda s, pv=pv, vstg=vstg: s.copy(
                                    vstg.ap, pv.ap[:, 0:128]),
                                    reads=[pv], writes=[vstg])
                                cx.op("vector",
                                      lambda v, tc=tc, f=f, vstg=vstg: v.tensor_copy(
                                          v_sb[tc].ap.rearrange(
                                              "p (h d) -> p h d", d=65
                                          )[:, 2 * f:2 * f + 2, 0:64],
                                          vstg.ap.rearrange("p (h d) -> p h d", d=64)),
                                      reads=[vstg], writes=[v_sb[tc]])
            for tc in range(32):
                cx.op("vector", lambda v, tc=tc: v.memset(
                    v_sb[tc].ap.rearrange("p (h d) -> p h d", d=65)[:, :, 64:65], 1.0),
                    reads=[v_sb[tc]], writes=[v_sb[tc]])

        cx.barrier()
        with contextlib.ExitStack() as pb:
            kalla = sbuf(pb, nc, "p5ka", [128, DC * 4096], BF16)
            k_sb = [Buf(kalla.ap[:, c * 4096:(c + 1) * 4096]) for c in range(DC)]
            q_t2 = [[sbuf(pb, nc, f"p5q{j}{c}", [128, 512], BF16)
                     for c in range(DC)] for j in range(2)]
            kbias = load_table(pb, nc, cx, "kbias")
            kbias2 = load_table(pb, nc, cx, "kbias2")
            qsplit = load_table(pb, nc, cx, "qsplit")
            mslope = load_table(pb, nc, cx, "mslope")
            pt_sb = [sbuf(pb, nc, f"p5pt{i}", [128, 512], BF16) for i in range(4)]
            av_sb = [sbuf(pb, nc, f"p5av{i}", [65, 512]) for i in range(2)]
            linv = [sbuf(pb, nc, f"p5li{i}", [1, 512]) for i in range(2)]
            o_fm = [sbuf(pb, nc, f"p5o{c}", [128, 512], BF16) for c in range(DC)]
            wo_all = sbuf(pb, nc, "p5wo", [128, 8 * 1024], BF16)
            bo_c = sbuf(pb, nc, "p5bo", [128, 8])
            s1st = [sbuf(pb, nc, f"p5s1{i}", [128, 512]) for i in range(2)]
            for ci in range(8):
                load_cols(cx, bo_c, ci, t["bo"], ci * 128)
            for f in range(8):
                cx.dma("gpsimd",
                       wo_all.ap[:, f * 1024:(f + 1) * 1024].rearrange(
                           "p (k f2) -> p k f2", f2=128),
                       t["Wo"][:, f * 128:(f + 1) * 128].rearrange(
                           "(k p) f -> p k f", p=128),
                       writes=[wo_all])
            ka3 = kalla.ap.rearrange("p (c t) -> p c t", t=4096)
            cx.dma("gpsimd", ka3[:, 0:4, :],
                   t["k_fm"][0:4].rearrange("c p t -> p c t"),
                   writes=k_sb[0:4] + [kalla])
            cx.dma("sync", ka3[:, 4:8, :],
                   t["k_fm"][4:8].rearrange("c p t -> p c t"),
                   writes=k_sb[4:8])
            def q_load(qtx):
                for c in range(DC):
                    cx.dma("sync", q_t2[qtx % 2][c].ap,
                           t["q_fm"][c, :, qtx * 512:(qtx + 1) * 512],
                           writes=[q_t2[qtx % 2][c]])
            q_load(0)
            for qt in range(8):
                q_t = q_t2[qt % 2]
                nk = (qt + 1) * 4
                for hp in range(8):
                    h0, h1 = 2 * hp, 2 * hp + 1
                    heavy = hp < 2 or DBG_ALLHEAVY
                    avq = [cx.pspool.take(0, (hp % 2) * 2 + i) for i in range(2)]

                    def emit_av(kcg2, avq=avq, nk=nk, h0=h0, h1=h1):
                        for i, h in enumerate((h0, h1)):
                            ptb = pt_sb[(kcg2 % 2) * 2 + i]
                            cx.op("tensor",
                                  lambda te, ptb=ptb, h=h, kcg2=kcg2, av=avq[i], nk=nk:
                                  te.matmul(av.ap[0:65, :],
                                            v_sb[kcg2].ap[:, h * 65:h * 65 + 65],
                                            ptb.ap, start=(kcg2 == 0),
                                            stop=(kcg2 == nk - 1)),
                                  reads=[v_sb[kcg2], ptb], writes=[avq[i]])
                    pend = None
                    for kcg in range(nk):
                        sq = [cx.pspool.take(1, (kcg % 2) * 2 + i) for i in range(2)]
                        for i, of in enumerate((0, 64)):
                            cx.op("tensor",
                                  lambda te, pq=sq[i], hp=hp, of=of, kcg=kcg,
                                  hv=heavy, qtb=q_t[hp]:
                                  te.matmul(pq.ap,
                                            k_sb[hp].ap[of:of + 64,
                                                        kcg * 128:(kcg + 1) * 128],
                                            qtb.ap[of:of + 64, :],
                                            start=True, stop=not hv),
                                  reads=[k_sb[hp], q_t[hp]], writes=[sq[i]])
                        if heavy:
                            for i, h in enumerate((h0, h1)):
                                cx.op("tensor",
                                      lambda te, pq=sq[i], h=h, qt=qt:
                                      te.matmul(pq.ap,
                                                mslope.ap[0:4, h * 128:(h + 1) * 128],
                                                qsplit.ap[0:4, qt * 512:(qt + 1) * 512],
                                                start=False, stop=True),
                                      reads=[mslope, qsplit], writes=[sq[i]])
                        ptbs = [pt_sb[(kcg % 2) * 2 + i] for i in range(2)]
                        for i, h in enumerate((h0, h1)):
                            if heavy:
                                bt, col = kbias, h * 32 + kcg
                            else:
                                bt, col = kbias2, (h - 4) * 32 + (kcg - 4 * qt + 28)
                            cx.op("scalar",
                                  lambda s, pq=sq[i], ptb=ptbs[i], bt=bt, col=col:
                                  s.activation(out=ptb.ap, in_=pq.ap, func=AF.Exp,
                                               scale=0.125,
                                               bias=bt.ap[:, col:col + 1]),
                                  reads=[sq[i], bt], writes=[ptbs[i]])
                            if kcg >= 4 * qt:
                                cx.op("gpsimd",
                                      lambda g, ptb=ptbs[i], qt=qt, kcg=kcg:
                                      g.affine_select(
                                          out=ptb.ap, in_=ptb.ap,
                                          compare_op=ALU.is_ge, fill=0.0,
                                          base=qt * 512 - (kcg * 128),
                                          pattern=[[1, 512]],
                                          channel_multiplier=-1),
                                      reads=[ptbs[i]], writes=[ptbs[i]])
                        if DBG_NODEFER:
                            emit_av(kcg)
                        else:
                            if pend is not None:
                                emit_av(pend)
                            pend = kcg
                    if not DBG_NODEFER:
                        emit_av(pend)
                    for i, of in enumerate((0, 64)):
                        cx.op("scalar", lambda s, av=avq[i], ab=av_sb[i]: s.copy(
                            ab.ap, av.ap[0:65, :]),
                            reads=[avq[i]], writes=[av_sb[i]])
                        cx.op("vector", lambda v, ab=av_sb[i], li=linv[i]: v.reciprocal(
                            li.ap, ab.ap[64:65, :]),
                            reads=[av_sb[i]], writes=[linv[i]])
                        pl = cx.pspool.take(1, i)
                        cx.op("tensor", lambda te, pl=pl, li=linv[i]: te.matmul(
                            pl.ap[0:64, :], onesr.ap[0:1, 0:64], li.ap,
                            start=True, stop=True),
                            reads=[linv[i], onesr], writes=[pl])
                        cx.op("vector", lambda v, hp=hp, of=of, pl=pl, ab=av_sb[i]:
                              v.tensor_tensor(o_fm[hp].ap[of:of + 64, :],
                                              ab.ap[0:64, :], pl.ap[0:64, :], ALU.mult),
                              reads=[av_sb[i], pl], writes=[o_fm[hp]])
                    if hp == 3 and qt < 7:
                        q_load(qt + 1)
                for f in range(8):
                    pq = cx.pspool.take(1, f % 4)
                    for k in range(8):
                        cx.op("tensor", lambda te, f=f, k=k, pq=pq: te.matmul(
                            pq.ap,
                            wo_all.ap[:, f * 1024 + k * 128:f * 1024 + (k + 1) * 128],
                            o_fm[k].ap, start=(k == 0), stop=(k == 7)),
                            reads=[wo_all, o_fm[k]], writes=[pq])
                    sg = s1st[f % 2]
                    cx.op("vector", lambda v, sg=sg, pq=pq, f=f: v.tensor_scalar(
                        sg.ap, pq.ap, bo_c.ap[:, f:f + 1], None, ALU.add),
                        reads=[pq, bo_c], writes=[sg])
                    cx.dma("sync", t["s1_fm"][f, :, qt * 512:(qt + 1) * 512], sg.ap,
                           reads=[sg])


def emit_l1_p6(stack, nc, cx, t, lay, u_in_f32, u_in_bf, u_out_f32, u_out_bf,
               ident_bf, ident_f, onesr, tabs):
    HID = 2730
    with contextlib.ExitStack() as ph:
        qr = [sbuf(ph, nc, f"p6q{c}", [128, 4096], BF16) for c in range(4)]
        kr = [sbuf(ph, nc, f"p6k{c}", [128, 4608], BF16) for c in range(4)]
        vtm = [sbuf(ph, nc, f"p6v{tc}", [128, 512], BF16) for tc in range(36)]
        ce = cx.const_col(1e-5)
        c24 = cx.const_col(1e-24)
        with contextlib.ExitStack() as pa:
            usba = sbuf(pa, nc, "p6u", [128, 8 * 4097], BF16)
            u_sb = [Buf(usba.ap[:, c * 4097:(c + 1) * 4097]) for c in range(8)]
            wst = [sbuf(pa, nc, f"p6w{i}", [128, 1024], BF16) for i in range(2)]
            qkv_st = sbuf(pa, nc, "p6qkv", [128, 2048], BF16)
            vstg = sbuf(pa, nc, "p6vst", [128, 128], BF16)
            for c in range(8):
                cx.op("vector", lambda v, c=c: v.memset(u_sb[c].ap[:, 0:1], 0.0),
                      writes=[u_sb[c]])
            cx.dma("sync",
                   usba.ap.rearrange("p (c t) -> p c t", t=4097)[:, :, 1:4097],
                   u_in_bf.rearrange("c p t -> p c t"),
                   reads=list(u_sb), writes=u_sb + [usba])
            for f in range(12):
                ws = wst[f % 2]
                cx.dma("gpsimd",
                       ws.ap.rearrange("p (k f2) -> p k f2", f2=128),
                       t["wqkv"][lay][:, f * 128:(f + 1) * 128].rearrange(
                           "(k p) f -> p k f", p=128),
                       writes=[ws])
                for half in range(2):
                    psap, qs = cx.pspool.group(half % 2)
                    for k in range(8):
                        off = 1 if k < 4 else 0
                        for tt in range(4):
                            cx.op("tensor",
                                  lambda te, ws=ws, k=k, tt=tt, qs=qs, half=half, off=off:
                                  te.matmul(qs[tt].ap,
                                            ws.ap[:, k * 128:(k + 1) * 128],
                                            u_sb[k].ap[:, off + half * 2048 + tt * 512:
                                                       off + half * 2048 + (tt + 1) * 512],
                                            start=(k == 0), stop=(k == 7)),
                                  reads=[ws, u_sb[k]], writes=[qs[tt]])
                    cx.op("vector", lambda v, a=psap, qs=qs: v.tensor_copy(
                        qkv_st.ap, a), reads=qs, writes=[qkv_st])
                    if f < 4:
                        cx.op("vector", lambda v, f=f, half=half: v.tensor_copy(
                            qr[f].ap[:, half * 2048:(half + 1) * 2048], qkv_st.ap),
                            reads=[qkv_st], writes=[qr[f]])
                    elif f < 8:
                        cx.op("vector", lambda v, f=f, half=half: v.tensor_copy(
                            kr[f - 4].ap[:, 512 + half * 2048:512 + (half + 1) * 2048],
                            qkv_st.ap), reads=[qkv_st], writes=[kr[f - 4]])
                    else:
                        for j in range(16):
                            tc = 4 + half * 16 + j
                            pv = cx.pspool.take(1 - half % 2, j % 4, BF16, 512)
                            cx.op("tensor", lambda te, pv=pv, j=j: te.transpose(
                                pv.ap[:, 0:128], qkv_st.ap[:, j * 128:(j + 1) * 128],
                                ident_bf.ap),
                                reads=[qkv_st, ident_bf], writes=[pv])
                            cx.op("scalar", lambda s, pv=pv: s.copy(
                                vstg.ap, pv.ap[:, 0:128]), reads=[pv], writes=[vstg])
                            cx.op("vector", lambda v, tc=tc, f=f: v.tensor_copy(
                                vtm[tc].ap[:, (f - 8) * 128:(f - 7) * 128], vstg.ap),
                                reads=[vstg], writes=[vtm[tc]])
            for tc in range(4):
                cx.op("vector", lambda v, tc=tc: v.memset(vtm[tc].ap, 0.0),
                      writes=[vtm[tc]])
            for c in range(4):
                cx.op("vector", lambda v, c=c: v.memset(kr[c].ap[:, 0:512], 0.0),
                      reads=[kr[c]], writes=[kr[c]])
        cx.barrier()
        with contextlib.ExitStack() as pa:
            csq = load_table(pa, nc, cx, "csq")
            snq = load_table(pa, nc, cx, "snq")
            csk = load_table(pa, nc, cx, "csk")
            snk = load_table(pa, nc, cx, "snk")
            nrm2 = [sbuf(pa, nc, f"p6n{i}", [8, 512]) for i in range(2)]
            sqb2_ = [sbuf(pa, nc, f"p6sq{i}", [128, 512], BF16) for i in range(2)]
            qn_sb = sbuf(pa, nc, "p6qn", [128, 512])
            qsw = sbuf(pa, nc, "p6qs", [128, 512])
            t1 = sbuf(pa, nc, "p6t1", [128, 512])
            t2 = sbuf(pa, nc, "p6t2", [128, 512])
            scol = sbuf(pa, nc, "p6sc", [128, 2])
            load_cols(cx, scol, 0, t["q_scale"][lay], 0, n=64)
            load_cols(cx, scol, 1, t["k_scale"][lay], 0, n=64)
            cx.op("vector", lambda v: v.tensor_copy(scol.ap[64:128, 0:2],
                                                    scol.ap[0:64, 0:2]),
                  reads=[scol], writes=[scol])
            for (dst, prefix, scix, cst, snt) in ((qr, 0, 0, csq, snq),
                                                  (kr, 512, 1, csk, snk)):
                for st in range(8):
                    pn = cx.pspool.take(0, st % 2)
                    nrm = nrm2[st % 2]
                    for c in range(4):
                        sqb = sqb2_[c % 2]
                        cx.op("scalar", lambda s, c=c, st=st, dst=dst, prefix=prefix,
                              sqb=sqb:
                              s.activation(out=sqb.ap,
                                           in_=dst[c].ap[:, prefix + st * 512:
                                                         prefix + (st + 1) * 512],
                                           func=AF.Square),
                              reads=[dst[c]], writes=[sqb])
                        cx.op("tensor", lambda te, c=c, pn=pn, sqb=sqb: te.matmul(
                            pn.ap[0:8, :], tabs[f"ind{c}"].ap, sqb.ap,
                            start=(c == 0), stop=(c == 3)),
                            reads=[sqb, tabs[f"ind{c}"]], writes=[pn])
                    cx.op("vector", lambda v, pn=pn, nrm=nrm: v.tensor_scalar(
                        nrm.ap, pn.ap[0:8, :], c24.ap[0:8, 0:1], None, ALU.max),
                        reads=[pn, c24], writes=[nrm])
                    cx.op("scalar", lambda s, nrm=nrm: s.activation(
                        out=nrm.ap, in_=nrm.ap, func=AF.Ln),
                        reads=[nrm], writes=[nrm])
                    cx.op("scalar", lambda s, nrm=nrm: s.activation(
                        out=nrm.ap, in_=nrm.ap, func=AF.Exp, scale=-0.5),
                        reads=[nrm], writes=[nrm])
                    for c in range(4):
                        pb = cx.pspool.take(0, 2 + c % 2)
                        cx.op("tensor", lambda te, c=c, pb=pb, nrm=nrm: te.matmul(
                            pb.ap, tabs[f"b8_{c}"].ap, nrm.ap, start=True, stop=True),
                            reads=[nrm, tabs[f"b8_{c}"]], writes=[pb])
                        cx.op("vector", lambda v, c=c, st=st, dst=dst, prefix=prefix,
                              pb=pb, scix=scix: v.scalar_tensor_tensor(
                                  qn_sb.ap, dst[c].ap[:, prefix + st * 512:
                                                      prefix + (st + 1) * 512],
                                  scol.ap[:, scix:scix + 1], pb.ap, ALU.mult, ALU.mult),
                              reads=[dst[c], scol, pb], writes=[qn_sb])
                        for b4 in range(4):
                            src = [32, 0, 96, 64][b4]
                            cx.op("vector", lambda v, b4=b4, src=src: v.tensor_copy(
                                qsw.ap[b4 * 32:(b4 + 1) * 32, :],
                                qn_sb.ap[src:src + 32, :]),
                                reads=[qn_sb], writes=[qsw])
                        cx.op("vector", lambda v, st=st, cst=cst: v.tensor_tensor(
                            t1.ap, qn_sb.ap, cst.ap[:, st * 512:(st + 1) * 512],
                            ALU.mult), reads=[qn_sb, cst], writes=[t1])
                        cx.op("vector", lambda v, st=st, snt=snt: v.tensor_tensor(
                            t2.ap, qsw.ap, snt.ap[:, st * 512:(st + 1) * 512],
                            ALU.mult), reads=[qsw, snt], writes=[t2])
                        cx.op("vector", lambda v, c=c, st=st, dst=dst, prefix=prefix:
                              v.tensor_tensor(
                                  dst[c].ap[:, prefix + st * 512:prefix + (st + 1) * 512],
                                  t1.ap, t2.ap, ALU.add),
                              reads=[t1, t2], writes=[dst[c]])
        cx.barrier()
        with contextlib.ExitStack() as pb_:
            msk = sbuf(pb_, nc, "p6m", [128, 5 * 640])
            mh = nc.inline_tensor(np.ascontiguousarray(_TABLES["lmask"]),
                                  f"lmask{lay}").ap()
            for i in range(5):
                cx.dma("sync", msk.ap[:, i * 640:(i + 1) * 640], mh[i, :, :],
                       writes=[msk])
            masked = [sbuf(pb_, nc, f"p6ms{i}", [128, 640]) for i in range(4)]
            mcol = [sbuf(pb_, nc, f"p6mc{i}", [128, 2]) for i in range(4)]
            pbf = [sbuf(pb_, nc, f"p6pb{i}", [128, 640], BF16) for i in range(4)]
            pt_sb = [sbuf(pb_, nc, f"p6pt{i}", [128, 640], BF16) for i in range(2)]
            o_fm = [sbuf(pb_, nc, f"p6o{c}", [128, 512], BF16) for c in range(4)]
            wo_st = [sbuf(pb_, nc, f"p6wo{i}", [128, 512], BF16) for i in range(2)]
            u_ld = [sbuf(pb_, nc, f"p6ul{i}", [128, 512]) for i in range(2)]
            u1st = [sbuf(pb_, nc, f"p6u1{i}", [128, 512]) for i in range(2)]
            u1bf = sbuf(pb_, nc, "p6u1b", [128, 512], BF16)
            for w in range(8):

                def stage2(pb, bi, ch, of, qs, w=w):
                    pt = pt_sb[bi % 2]
                    for kc in range(5):
                        pv = cx.pspool.take(0, 2 + (kc % 2), BF16, 512)
                        cx.op("tensor", lambda te, pv=pv, pb=pb, kc=kc: te.transpose(
                            pv.ap[:, 0:128], pb.ap[:, kc * 128:(kc + 1) * 128],
                            ident_bf.ap),
                            reads=[pb, ident_bf], writes=[pv])
                        cx.op("scalar", lambda s, pv=pv, pt=pt, kc=kc: s.copy(
                            pt.ap[:, kc * 128:(kc + 1) * 128], pv.ap[:, 0:128]),
                            reads=[pv], writes=[pt])
                    pav = cx.pspool.take(0, bi % 2)
                    h2_ = 2 * ch + (1 if of else 0)
                    for kc in range(5):
                        tc = (w - 1) * 4 + qs + kc + 4
                        cx.op("tensor", lambda te, pav=pav, tc=tc, h2_=h2_, pt=pt,
                              kc=kc:
                              te.matmul(pav.ap[0:64, 0:128],
                                        vtm[tc].ap[:, h2_ * 64:(h2_ + 1) * 64],
                                        pt.ap[:, kc * 128:(kc + 1) * 128],
                                        start=(kc == 0), stop=(kc == 4)),
                              reads=[vtm[tc], pt], writes=[pav])
                    cx.op("scalar", lambda s, pav=pav, ch=ch, of=of, qs=qs: s.copy(
                        o_fm[ch].ap[of:of + 64, qs * 128:(qs + 1) * 128],
                        pav.ap[0:64, 0:128]),
                        reads=[pav], writes=[o_fm[ch]])
                pend = []
                bi = 0
                for h in range(8):
                    ch, of = h // 2, 64 * (h % 2)
                    for qs in range(4):
                        pq = cx.pspool.span(1, 2 * (bi % 2), 2)
                        ms, mc, pb = masked[bi % 4], mcol[bi % 4], pbf[bi % 4]
                        cx.op("tensor", lambda te, pq=pq, ch=ch, of=of, w=w, qs=qs:
                              te.matmul(pq.ap[:, 0:512],
                                        qr[ch].ap[of:of + 64,
                                                  w * 512 + qs * 128:
                                                  w * 512 + (qs + 1) * 128],
                                        kr[ch].ap[of:of + 64,
                                                  w * 512 + qs * 128:
                                                  w * 512 + qs * 128 + 512],
                                        start=True, stop=True),
                              reads=[qr[ch], kr[ch]], writes=[pq])
                        cx.op("tensor", lambda te, pq=pq, ch=ch, of=of, w=w, qs=qs:
                              te.matmul(pq.ap[:, 512:640],
                                        qr[ch].ap[of:of + 64,
                                                  w * 512 + qs * 128:
                                                  w * 512 + (qs + 1) * 128],
                                        kr[ch].ap[of:of + 64,
                                                  w * 512 + qs * 128 + 512:
                                                  w * 512 + qs * 128 + 640],
                                        start=True, stop=True),
                              reads=[qr[ch], kr[ch]], writes=[pq])
                        mi = 0 if w > 0 else 1 + qs
                        cx.op("vector", lambda v, pq=pq, ms=ms, mi=mi:
                              v.scalar_tensor_tensor(
                                  ms.ap, pq.ap[:, 0:640], 1.0,
                                  msk.ap[:, mi * 640:(mi + 1) * 640],
                                  ALU.mult, ALU.add),
                              reads=[pq, msk], writes=[ms])
                        cx.op("vector", lambda v, ms=ms, mc=mc: v.reduce_max(
                            mc.ap[:, 0:1], ms.ap, axis=AX.X),
                            reads=[ms], writes=[mc])
                        cx.op("vector", lambda v, mc=mc: v.tensor_scalar(
                            mc.ap[:, 1:2], mc.ap[:, 0:1], -8.0, None, ALU.mult),
                            reads=[mc], writes=[mc])
                        cx.op("scalar", lambda s, ms=ms, mc=mc, pb=pb: s.activation(
                            out=pb.ap, in_=ms.ap, func=AF.Exp, scale=8.0,
                            bias=mc.ap[:, 1:2], accum_out=mc.ap[:, 0:1]),
                            reads=[ms, mc], writes=[pb, mc])
                        cx.op("vector", lambda v, mc=mc: v.reciprocal(
                            mc.ap[:, 0:1], mc.ap[:, 0:1]),
                            reads=[mc], writes=[mc])
                        cx.op("vector", lambda v, pb=pb, mc=mc: v.tensor_scalar(
                            pb.ap, pb.ap, mc.ap[:, 0:1], None, ALU.mult),
                            reads=[pb, mc], writes=[pb])
                        pend.append((pb, bi, ch, of, qs))
                        if len(pend) > 3:
                            stage2(*pend.pop(0))
                        bi += 1
                for pe_ in pend:
                    stage2(*pe_)
                for f in range(8):
                    wo = wo_st[f % 2]
                    cx.dma("gpsimd",
                           wo.ap.rearrange("p (k f2) -> p k f2", f2=128),
                           t["wo"][lay][:, f * 128:(f + 1) * 128].rearrange(
                               "(k p) f -> p k f", p=128),
                           writes=[wo])
                    pq = cx.pspool.take(1, f % 2)
                    for k in range(4):
                        cx.op("tensor", lambda te, wo=wo, k=k, pq=pq: te.matmul(
                            pq.ap, wo.ap[:, k * 128:(k + 1) * 128], o_fm[k].ap,
                            start=(k == 0), stop=(k == 3)),
                            reads=[wo, o_fm[k]], writes=[pq])
                    ul, u1 = u_ld[f % 2], u1st[f % 2]
                    cx.dma("sync", ul.ap, u_in_f32[f, :, w * 512:(w + 1) * 512],
                           writes=[ul])
                    cx.op("vector", lambda v, pq=pq, ul=ul, u1=u1: v.tensor_tensor(
                        u1.ap, pq.ap, ul.ap, ALU.add),
                        reads=[pq, ul], writes=[u1])
                    cx.op("vector", lambda v, u1=u1: v.tensor_copy(u1bf.ap, u1.ap),
                          reads=[u1], writes=[u1bf])
                    cx.dma("sync", t["u1_f32"][f, :, w * 512:(w + 1) * 512], u1.ap,
                           reads=[u1])
                    cx.dma("sync", t["u1_bf"][f, :, w * 512:(w + 1) * 512], u1bf.ap,
                           reads=[u1bf])
    cx.barrier()
    with contextlib.ExitStack() as pc:
        u1a = sbuf(pc, nc, "p6c", [128, 8 * 4097], BF16)
        u1_sb = [Buf(u1a.ap[:, c * 4097:(c + 1) * 4097]) for c in range(8)]
        hn = [sbuf(pc, nc, f"p6h{c}", [128, 4096], BF16) for c in range(8)]
        sqb2l = [sbuf(pc, nc, f"p6sq2{i}", [128, 512], BF16)
                 for i in range(2)]
        rows2l = [sbuf(pc, nc, "p6r", [1, 2048])] * 2
        lgc = sbuf(pc, nc, "p6lg", [128, 16])
        onescol = load_table(pc, nc, cx, "ones_col_bf", dq="gpsimd")
        tA = sbuf(pc, nc, "p6tA", [128, 512])
        for ci in range(8):
            load_cols(cx, lgc, ci, t["ln_g"][lay], ci * 128)
            load_cols(cx, lgc, 8 + ci, t["ln_b"][lay], ci * 128)
        for c in range(8):
            cx.op("vector", lambda v, c=c: v.memset(u1_sb[c].ap[:, 0:1], 0.0),
                  writes=[u1_sb[c]])
        cx.dma("sync",
               u1a.ap.rearrange("p (c t) -> p c t", t=4097)[:, :, 1:4097],
               t["u1_bf"].rearrange("c p t -> p c t"),
               reads=list(u1_sb), writes=u1_sb + [u1a])

        def ush(c, st):
            off = 1 if c < 4 else 0
            return u1_sb[c].ap[:, off + st * 512:off + (st + 1) * 512]
        for st in range(8):
            rows = rows2l[st % 2]
            psap, qs = cx.pspool.group(0)
            for c in range(8):
                cx.op("tensor", lambda te, c=c, st=st, qs=qs: te.matmul(
                    qs[0].ap[0:1, :], onescol.ap, ush(c, st),
                    start=(c == 0), stop=(c == 7)),
                    reads=[u1_sb[c], onescol], writes=[qs[0]])
            for c in range(8):
                sqb2 = sqb2l[c % 2]
                cx.op("scalar", lambda s, c=c, st=st, sqb2=sqb2: s.activation(
                    out=sqb2.ap, in_=ush(c, st), func=AF.Square),
                    reads=[u1_sb[c]], writes=[sqb2])
                cx.op("tensor", lambda te, c=c, qs=qs, sqb2=sqb2: te.matmul(
                    qs[1].ap[0:1, :], onescol.ap, sqb2.ap,
                    start=(c == 0), stop=(c == 7)),
                    reads=[sqb2, onescol], writes=[qs[1]])
            cx.op("vector", lambda v, qs=qs: v.tensor_scalar(
                rows.ap[0:1, 0:512], qs[0].ap[0:1, :], 1.0 / D, None, ALU.mult),
                reads=[qs[0]], writes=[rows])
            cx.op("vector", lambda v, qs=qs: v.tensor_scalar(
                rows.ap[0:1, 512:1024], qs[1].ap[0:1, :], 1.0 / D, None, ALU.mult),
                reads=[qs[1]], writes=[rows])
            cx.op("vector", lambda v: v.tensor_tensor(
                rows.ap[0:1, 1024:1536], rows.ap[0:1, 0:512], rows.ap[0:1, 0:512],
                ALU.mult), reads=[rows], writes=[rows])
            cx.op("vector", lambda v: v.tensor_tensor(
                rows.ap[0:1, 512:1024], rows.ap[0:1, 512:1024],
                rows.ap[0:1, 1024:1536], ALU.subtract), reads=[rows], writes=[rows])
            cx.op("scalar", lambda s: s.activation(
                out=rows.ap[0:1, 1024:1536], in_=rows.ap[0:1, 512:1024], func=AF.Ln,
                bias=ce.ap[0:1, 0:1]), reads=[rows, ce], writes=[rows])
            cx.op("scalar", lambda s: s.activation(
                out=rows.ap[0:1, 1024:1536], in_=rows.ap[0:1, 1024:1536],
                func=AF.Exp, scale=-0.5), reads=[rows], writes=[rows])
            cx.op("vector", lambda v: v.tensor_tensor(
                rows.ap[0:1, 1536:2048], rows.ap[0:1, 0:512],
                rows.ap[0:1, 1024:1536], ALU.mult), reads=[rows], writes=[rows])
            pr = cx.pspool.take(1, (st % 2) * 2)
            pm = cx.pspool.take(1, (st % 2) * 2 + 1)
            cx.op("tensor", lambda te, pr=pr: te.matmul(
                pr.ap, onesr.ap, rows.ap[0:1, 1024:1536], start=True, stop=True),
                reads=[rows, onesr], writes=[pr])
            cx.op("tensor", lambda te, pm=pm: te.matmul(
                pm.ap, onesr.ap, rows.ap[0:1, 1536:2048], start=True, stop=True),
                reads=[rows, onesr], writes=[pm])
            for c in range(8):
                cx.op("vector", lambda v, c=c, st=st, pr=pr: v.tensor_tensor(
                    tA.ap, ush(c, st), pr.ap, ALU.mult),
                    reads=[u1_sb[c], pr], writes=[tA])
                cx.op("vector", lambda v, pm=pm: v.tensor_tensor(
                    tA.ap, tA.ap, pm.ap, ALU.subtract),
                    reads=[tA, pm], writes=[tA])
                cx.op("vector", lambda v, c=c, st=st: v.tensor_scalar(
                    hn[c].ap[:, st * 512:(st + 1) * 512], tA.ap,
                    lgc.ap[:, c:c + 1], lgc.ap[:, 8 + c:9 + c], ALU.mult, ALU.add),
                    reads=[tA, lgc], writes=[hn[c]])
        wa_st = [sbuf(pc, nc, f"p6wa{i}", [128, 1024], BF16) for i in range(2)]
        wg_st = [sbuf(pc, nc, f"p6wg{i}", [128, 1024], BF16) for i in range(2)]
        gel = sbuf(pc, nc, "p6gel", [128, 2048])
        h2b = sbuf(pc, nc, "p6h2", [128, 2048], BF16)
        b1c = sbuf(pc, nc, "p6b1", [128, 44])
        for j in range(21):
            load_cols(cx, b1c, j, t["b1"][lay], j * 128)
            load_cols(cx, b1c, 22 + j, t["b1"][lay], HID + j * 128)
        cx.dma("sync", b1c.ap[0:42, 21], t["b1"][lay][21 * 128:HID], writes=[b1c])
        cx.dma("sync", b1c.ap[0:42, 43], t["b1"][lay][HID + 21 * 128:2 * HID],
               writes=[b1c])
        for half in range(2):
            h0 = half * 2048
            for j in range(22):
                wj = 128 if j < 21 else 42
                wa, wg = wa_st[j % 2], wg_st[j % 2]
                cx.dma("gpsimd",
                       wa.ap[:, 0:8 * wj].rearrange("p (k f2) -> p k f2", f2=wj),
                       t["w1"][lay][:, j * 128:j * 128 + wj].rearrange(
                           "(k p) f -> p k f", p=128),
                       writes=[wa])
                cx.dma("gpsimd",
                       wg.ap[:, 0:8 * wj].rearrange("p (k f2) -> p k f2", f2=wj),
                       t["w1"][lay][:, HID + j * 128:HID + j * 128 + wj].rearrange(
                           "(k p) f -> p k f", p=128),
                       writes=[wg])
                grp = {}
                for (pi, wb) in ((0, wa), (1, wg)):
                    psap, qs = cx.pspool.group(pi)
                    grp[pi] = (psap, qs)
                    for k in range(8):
                        for tt in range(4):
                            cx.op("tensor",
                                  lambda te, wb=wb, k=k, tt=tt, qs=qs, wj=wj, h0=h0:
                                  te.matmul(qs[tt].ap[0:wj, :],
                                            wb.ap[:, k * wj:(k + 1) * wj],
                                            hn[k].ap[:, h0 + tt * 512:
                                                     h0 + (tt + 1) * 512],
                                            start=(k == 0), stop=(k == 7)),
                                  reads=[wb, hn[k]], writes=[qs[tt]])
                ps0ap, ps0q = grp[0]
                ps1ap, ps1q = grp[1]
                cx.op("scalar", lambda s, a=ps1ap, j=j, wj=wj: s.activation(
                    out=gel.ap[0:wj, :], in_=cx.pspool.t[1][0:wj, :], func=AF.Gelu,
                    bias=b1c.ap[0:wj, 22 + j:23 + j]),
                    reads=ps1q + [b1c], writes=[gel])
                cx.op("vector", lambda v, j=j, wj=wj: v.tensor_scalar(
                    h2b.ap[0:wj, :], cx.pspool.t[0][0:wj, :],
                    b1c.ap[0:wj, j:j + 1], None, ALU.add),
                    reads=ps0q + [b1c], writes=[h2b])
                cx.op("vector", lambda v, wj=wj: v.tensor_tensor(
                    h2b.ap[0:wj, :].bitcast(F32)[:, 0:2048] if False else
                    h2b.ap[0:wj, :], h2b.ap[0:wj, :], gel.ap[0:wj, :], ALU.mult),
                    reads=[h2b, gel], writes=[h2b])
                cx.dma("sync", t["h2_s"][j, 0:wj, h0:h0 + 2048], h2b.ap[0:wj, :],
                       reads=[h2b])
        cx.barrier()
        w2a = [sbuf(pc, nc, f"p6w2{i}", [128, 22 * 128], BF16) for i in range(2)]
        h2l = [sbuf(pc, nc, f"p6h2l{i}", [128, 2048], BF16) for i in range(2)]
        b2c = sbuf(pc, nc, "p6b2", [128, 8])
        uol = [sbuf(pc, nc, "p6uo", [128, 2048])] * 2
        u2st = [sbuf(pc, nc, "p6u2", [128, 2048])] * 2
        u2bf = sbuf(pc, nc, "p6u2b", [128, 2048], BF16)
        for ci in range(8):
            load_cols(cx, b2c, ci, t["b2"][lay], ci * 128)
        for half in range(2):
            h0 = half * 2048
            for fop in range(4):
                grp = {}
                wds = []
                for sub in range(2):
                    fo = fop * 2 + sub
                    psap, qs = cx.pspool.group(sub)
                    grp[sub] = (psap, qs)
                    wd = w2a[sub]
                    cx.dma("gpsimd",
                           wd.ap[:, 0:21 * 128].rearrange(
                               "p (k f2) -> p k f2", f2=128),
                           t["w2"][lay][0:21 * 128, fo * 128:(fo + 1) * 128]
                           .rearrange("(k p) f -> p k f", p=128),
                           writes=[wd])
                    cx.dma("gpsimd",
                           wd.ap[0:42, 21 * 128:22 * 128],
                           t["w2"][lay][21 * 128:HID, fo * 128:(fo + 1) * 128],
                           writes=[wd])
                    wds.append(wd)
                for j in range(22):
                    wj = 128 if j < 21 else 42
                    hl = h2l[j % 2]
                    cx.dma("sync", hl.ap[0:wj, :], t["h2_s"][j, 0:wj, h0:h0 + 2048],
                           writes=[hl])
                    for sub in range(2):
                        psap, qs = grp[sub]
                        for tt in range(4):
                            cx.op("tensor",
                                  lambda te, wd=wds[sub], j=j, tt=tt, qs=qs, wj=wj,
                                  hl=hl:
                                  te.matmul(qs[tt].ap,
                                            wd.ap[0:wj, j * 128:j * 128 + 128],
                                            hl.ap[0:wj, tt * 512:(tt + 1) * 512],
                                            start=(j == 0), stop=(j == 21)),
                                  reads=[wds[sub], hl], writes=[qs[tt]])
                for sub in range(2):
                    fo = fop * 2 + sub
                    psap, qs = grp[sub]
                    ul, u2 = uol[sub], u2st[sub]
                    cx.dma("sync", ul.ap, t["u1_f32"][fo, :, h0:h0 + 2048],
                           writes=[ul])
                    cx.op("vector", lambda v, u2=u2, a=psap, fo=fo, qs=qs:
                          v.tensor_scalar(u2.ap, a, b2c.ap[:, fo:fo + 1], None,
                                          ALU.add),
                          reads=qs + [b2c], writes=[u2])
                    cx.op("vector", lambda v, u2=u2, ul=ul: v.tensor_tensor(
                        u2.ap, u2.ap, ul.ap, ALU.add),
                        reads=[u2, ul], writes=[u2])
                    cx.op("vector", lambda v, u2=u2: v.tensor_copy(u2bf.ap, u2.ap),
                          reads=[u2], writes=[u2bf])
                    cx.dma("sync", u_out_f32[fo, :, h0:h0 + 2048], u2.ap,
                           reads=[u2])
                    cx.dma("sync", u_out_bf[fo, :, h0:h0 + 2048], u2bf.ap,
                           reads=[u2bf])


def emit_l1_p8(stack, nc, cx, t, ident_f):
    with contextlib.ExitStack() as ph:
        u4 = [sbuf(ph, nc, f"p8u{i}", [128, 4096]) for i in range(2)]
        s1 = [sbuf(ph, nc, f"p8s{i}", [128, 4096]) for i in range(2)]
        x1 = [sbuf(ph, nc, f"p8x{i}", [128, 4096]) for i in range(2)]
        sil = sbuf(ph, nc, "p8si", [128, 4096])
        x2 = [sbuf(ph, nc, f"p8x2{i}", [128, 4096]) for i in range(2)]
        ost = [sbuf(ph, nc, f"p8o{i}", [128, 1024]) for i in range(2)]
        for i in range(8):
            bi = i % 2
            t0 = i * 512
            uu, ss, xx, xo = u4[bi], s1[bi], x1[bi], x2[bi]
            cx.dma("sync", uu.ap.rearrange("p (c t) -> p c t", t=512),
                   t["u4_f32"][:, :, t0:t0 + 512].rearrange("c p t -> p c t"),
                   writes=[uu])
            cx.dma("sync", ss.ap.rearrange("p (c t) -> p c t", t=512),
                   t["s1_fm"][:, :, t0:t0 + 512].rearrange("c p t -> p c t"),
                   writes=[ss])
            cx.dma("sync", xx.ap.rearrange("p (c t) -> p c t", t=512),
                   t["x1_fm"][:, :, t0:t0 + 512].rearrange("c p t -> p c t"),
                   writes=[xx])
            cx.op("scalar", lambda s, uu=uu: s.activation(
                out=sil.ap, in_=uu.ap, func=AF.Silu), reads=[uu], writes=[sil])
            cx.op("vector", lambda v, ss=ss, xo=xo: v.tensor_tensor(
                xo.ap, sil.ap, ss.ap, ALU.mult), reads=[sil, ss], writes=[xo])
            cx.op("vector", lambda v, xx=xx, xo=xo: v.tensor_tensor(
                xo.ap, xo.ap, xx.ap, ALU.add), reads=[xo, xx], writes=[xo])
            for j in range(4):
                ob = ost[j % 2]
                for cc in range(2):
                    pq = cx.pspool.take(0, 2 * (j % 2) + cc)
                    for c4 in range(4):
                        c = cc * 4 + c4
                        cx.op("tensor", lambda te, pq=pq, c=c, c4=c4, j=j, xo=xo:
                              te.transpose(pq.ap[:, c4 * 128:(c4 + 1) * 128],
                                           xo.ap[:, c * 512 + j * 128:
                                                 c * 512 + (j + 1) * 128],
                                           ident_f.ap),
                              reads=[xo, ident_f], writes=[pq])
                    cx.op("scalar", lambda s, pq=pq, cc=cc, ob=ob: s.copy(
                        ob.ap[:, cc * 512:(cc + 1) * 512], pq.ap),
                        reads=[pq], writes=[ob])
                cx.dma("sync", t["out"][t0 + j * 128:t0 + (j + 1) * 128, :], ob.ap,
                       reads=[ob])


def build_l1(debug=False):
    nc = bass.Bass("TRN2", target_bir_lowering=False)
    t = {}
    t["x1_fm"] = nc.dram_tensor("x1_fm", [DC, 128, T], F32, kind="ExternalInput").ap()
    for nm, shp in [("Wq", [D, D]), ("bq", [D]), ("Wk", [D, D]), ("bk", [D]),
                    ("Wv", [D, D]), ("bv", [D]), ("Wo", [D, D]), ("bo", [D])]:
        t[nm] = nc.dram_tensor(nm, shp, F32, kind="ExternalInput").ap()
    HID = 2730
    for nm, shp in [("wqkv", [2, D, 1536]), ("q_scale", [2, 64]), ("k_scale", [2, 64]),
                    ("wo", [2, 512, D]), ("ln_g", [2, D]), ("ln_b", [2, D]),
                    ("w1", [2, D, 2 * HID]), ("b1", [2, 2 * HID]),
                    ("w2", [2, HID, D]), ("b2", [2, D])]:
        ap = nc.dram_tensor(nm, shp, F32, kind="ExternalInput").ap()
        t[nm] = [ap[0], ap[1]]
    t["out"] = nc.dram_tensor("out", [T, D], F32, kind="ExternalOutput").ap()
    sk = "ExternalOutput" if debug else "Internal"
    t["xn1_f32"] = nc.dram_tensor("xn1_f32", [DC, 128, T], F32, kind=sk).ap()
    t["xn1_bf"] = nc.dram_tensor("xn1_bf", [DC, 128, T], BF16, kind="Internal").ap()
    t["q_fm"] = nc.dram_tensor("q_fm", [DC, 128, T], BF16, kind="Internal").ap()
    t["k_fm"] = nc.dram_tensor("k_fm", [DC, 128, T], BF16, kind="Internal").ap()
    t["s1_fm"] = nc.dram_tensor("s1_fm", [DC, 128, T], F32, kind=sk).ap()
    t["u1_f32"] = nc.dram_tensor("u1_f32", [DC, 128, T], F32, kind=sk).ap()
    t["u1_bf"] = nc.dram_tensor("u1_bf", [DC, 128, T], BF16, kind="Internal").ap()
    t["u2_f32"] = nc.dram_tensor("u2_f32", [DC, 128, T], F32, kind=sk).ap()
    t["u2_bf"] = nc.dram_tensor("u2_bf", [DC, 128, T], BF16, kind="Internal").ap()
    t["u4_f32"] = nc.dram_tensor("u4_f32", [DC, 128, T], F32, kind=sk).ap()
    t["u4_bf"] = nc.dram_tensor("u4_bf", [DC, 128, T], BF16, kind="Internal").ap()
    t["h2_s"] = nc.dram_tensor("h2_s", [22, 128, T], BF16, kind="Internal").ap()

    with contextlib.ExitStack() as stack:
        kb = KB(nc, stack)
        cx = Ctx(kb)
        cx.pspool = PsumPool(stack, nc)
        cx.cbuf = sbuf(stack, nc, "consts", [128, 16])
        tabs = {}
        for nm in ("ident_bf", "ident_f", "ones_row_f",
                   "ind0", "ind1", "ind2", "ind3", "b8_0", "b8_1", "b8_2", "b8_3"):
            tabs[nm] = load_table(stack, nc, cx, nm)
        ident_bf, ident_f, onesr = tabs["ident_bf"], tabs["ident_f"], tabs["ones_row_f"]
        emit_l1_p4(stack, nc, cx, t, onesr)
        cx.barrier()
        emit_l1_p5(stack, nc, cx, t, ident_bf, onesr)
        cx.barrier()
        emit_l1_p6(stack, nc, cx, t, 0, t["xn1_f32"], t["xn1_bf"],
                   t["u2_f32"], t["u2_bf"], ident_bf, ident_f, onesr, tabs)
        cx.barrier()
        emit_l1_p6(stack, nc, cx, t, 1, t["u2_f32"], t["u2_bf"],
                   t["u4_f32"], t["u4_bf"], ident_bf, ident_f, onesr, tabs)
        cx.barrier()
        emit_l1_p8(stack, nc, cx, t, ident_f)
        with nc.Block() as block:
            kb.finalize(block)
    return nc


def build_full():
    nc = bass.Bass("TRN2", target_bir_lowering=False)
    t = {}
    t["x_in"] = nc.dram_tensor("x", [T, D], F32, kind="ExternalInput").ap()
    t["w_in"] = nc.dram_tensor("W_in", [D, 2 * D], F32, kind="ExternalInput").ap()
    t["b_in"] = nc.dram_tensor("b_in", [2 * D], F32, kind="ExternalInput").ap()
    t["a_in"] = nc.dram_tensor("A", [D], F32, kind="ExternalInput").ap()
    t["c_in"] = nc.dram_tensor("C", [D], F32, kind="ExternalInput").ap()
    t["w_out"] = nc.dram_tensor("W_out", [D, D], F32, kind="ExternalInput").ap()
    t["b_out"] = nc.dram_tensor("b_out", [D], F32, kind="ExternalInput").ap()
    t["w_gate"] = nc.dram_tensor("w_gate", [FFN, D], F32, kind="ExternalInput").ap()
    t["w_up"] = nc.dram_tensor("w_up", [FFN, D], F32, kind="ExternalInput").ap()
    t["w_down"] = nc.dram_tensor("w_down", [D, FFN], F32, kind="ExternalInput").ap()
    for nm, shp in [("Wq", [D, D]), ("bq", [D]), ("Wk", [D, D]), ("bk", [D]),
                    ("Wv", [D, D]), ("bv", [D]), ("Wo", [D, D]), ("bo", [D])]:
        t[nm] = nc.dram_tensor(nm, shp, F32, kind="ExternalInput").ap()
    HID = 2730
    for nm, shp in [("wqkv", [2, D, 1536]), ("q_scale", [2, 64]), ("k_scale", [2, 64]),
                    ("wo", [2, 512, D]), ("ln_g", [2, D]), ("ln_b", [2, D]),
                    ("w1", [2, D, 2 * HID]), ("b1", [2, 2 * HID]),
                    ("w2", [2, HID, D]), ("b2", [2, D])]:
        ap = nc.dram_tensor(nm, shp, F32, kind="ExternalInput").ap()
        t[nm] = [ap[0], ap[1]]
    t["out"] = nc.dram_tensor("out", [T, D], F32, kind="ExternalOutput").ap()
    dbg5 = os.environ.get("K_DBG5", "") == "1"
    for nm, shp, dt in [("x1_fm", [DC, 128, T], F32), ("xn_fm", [DC, 128, T], BF16),
                        ("xq_fm", [DC, 128, T], BF16), ("x_fm", [DC, 128, T], F32),
                        ("s_fm", [DC, 128, T], F32), ("hq_s", [FC, 128, T], BF16),
                        ("wqgt", [DC, 128, FFN], BF16), ("wqut", [DC, 128, FFN], BF16),
                        ("wqdt", [FC, 128, D], BF16),
                        ("xn1_f32", [DC, 128, T], F32), ("xn1_bf", [DC, 128, T], BF16),
                        ("q_fm", [DC, 128, T], BF16), ("k_fm", [DC, 128, T], BF16),
                        ("s1_fm", [DC, 128, T], F32),
                        ("u1_f32", [DC, 128, T], F32), ("u1_bf", [DC, 128, T], BF16),
                        ("u2_f32", [DC, 128, T], F32), ("u2_bf", [DC, 128, T], BF16),
                        ("u4_f32", [DC, 128, T], F32), ("u4_bf", [DC, 128, T], BF16),
                        ("h2_s", [22, 128, T], BF16)]:
        kind = "Internal"
        if dbg5 and nm in ("s1_fm", "u4_f32", "x1_fm", "q_fm", "k_fm",
                           "xn_fm", "xq_fm", "x_fm", "s_fm", "hq_s",
                           "wqgt", "wqut", "wqdt"):
            kind = "ExternalOutput"
        t[nm] = nc.dram_tensor(nm + "_s" if nm in ("x_fm", "s_fm") else nm,
                               shp, dt, kind=kind).ap()
    with contextlib.ExitStack() as stack:
        kb = KB(nc, stack)
        cx = Ctx(kb)
        cx.pspool = PsumPool(stack, nc)
        cx.cbuf = sbuf(stack, nc, "consts", [128, 16])
        tabs = {}
        for nm in ("ident_bf", "ident_f", "ones_row_f",
                   "ind0", "ind1", "ind2", "ind3", "b8_0", "b8_1", "b8_2", "b8_3"):
            tabs[nm] = load_table(stack, nc, cx, nm)
        ident_bf, ident_f, onesr = tabs["ident_bf"], tabs["ident_f"], tabs["ones_row_f"]
        emit_l0_weightquant(stack, nc, cx, t, ident_bf)
        cx.barrier()
        emit_l0_p1(stack, nc, cx, t, ident_bf, ident_f)
        cx.barrier()
        emit_l0_p2(stack, nc, cx, t)
        cx.barrier()
        emit_l0_p3(stack, nc, cx, t, ident_bf, ident_f)
        cx.barrier()
        emit_l1_p4(stack, nc, cx, t, onesr)
        cx.barrier()
        emit_l1_p5(stack, nc, cx, t, ident_bf, onesr)
        cx.barrier()
        emit_l1_p6(stack, nc, cx, t, 0, t["xn1_f32"], t["xn1_bf"],
                   t["u2_f32"], t["u2_bf"], ident_bf, ident_f, onesr, tabs)
        cx.barrier()
        emit_l1_p6(stack, nc, cx, t, 1, t["u2_f32"], t["u2_bf"],
                   t["u4_f32"], t["u4_bf"], ident_bf, ident_f, onesr, tabs)
        cx.barrier()
        emit_l1_p8(stack, nc, cx, t, ident_f)
        with nc.Block() as block:
            kb.finalize(block)
    return nc


_NC_CACHE = {}


def kernel(**inputs):
    if "full" not in _NC_CACHE:
        _NC_CACHE["full"] = build_full()
    ncf = _NC_CACHE["full"]
    x = np.ascontiguousarray(inputs["L0_attn_x"])
    ins = []
    for c in range(NCORES):
        ins.append(dict(
            x=np.ascontiguousarray(x[c]),
            W_in=inputs["L0_attn_W_in"], b_in=inputs["L0_attn_b_in"],
            A=inputs["L0_attn_A"], C=inputs["L0_attn_C"],
            W_out=inputs["L0_attn_W_out"], b_out=inputs["L0_attn_b_out"],
            w_gate=inputs["L0_ffn_w_gate"], w_up=inputs["L0_ffn_w_up"],
            w_down=inputs["L0_ffn_w_down"],
            Wq=inputs["L1_attn_Wq"], bq=inputs["L1_attn_bq"],
            Wk=inputs["L1_attn_Wk"], bk=inputs["L1_attn_bk"],
            Wv=inputs["L1_attn_Wv"], bv=inputs["L1_attn_bv"],
            Wo=inputs["L1_attn_Wo"], bo=inputs["L1_attn_bo"],
            wqkv=inputs["L1_ffn_wqkv"], q_scale=inputs["L1_ffn_q_scale"],
            k_scale=inputs["L1_ffn_k_scale"], wo=inputs["L1_ffn_wo"],
            ln_g=inputs["L1_ffn_ln_g"], ln_b=inputs["L1_ffn_ln_b"],
            w1=inputs["L1_ffn_w1"], b1=inputs["L1_ffn_b1"],
            w2=inputs["L1_ffn_w2"], b2=inputs["L1_ffn_b2"]))
    r = run_bass_kernel_spmd(ncf, ins, core_ids=list(range(NCORES)))
    return np.stack([r.results[c]["out"] for c in range(NCORES)], axis=0)
